# version 06
# baseline (speedup 1.0000x reference)
import math
from contextlib import ExitStack

import numpy as np
import concourse.bass as bass
import concourse.mybir as mybir
from concourse.bass_utils import run_bass_kernel_spmd

F32 = mybir.dt.float32
BF16 = mybir.dt.bfloat16
AF = mybir.ActivationFunctionType
ALU = mybir.AluOpType

MAGIC = 12582912.0
ISQ96 = float(np.float32(1.0) / np.sqrt(np.float32(96.0)))
EXP_HI = float(np.exp(80.0 / math.sqrt(96.0)))
EXP_LO = float(np.exp(-80.0 / math.sqrt(96.0)))

D, S, NB, NH, DH, DHR, DL, II = 1024, 2048, 2, 16, 64, 32, 256, 2816
C8, NT, QT, IC, LC = D // 128, S // 128, S // 512, II // 128, DL // 128
NCORE = 8

DUMP = ()


class Op:
    __slots__ = ("eng", "sem", "fn", "waits", "marked", "inc_n", "cum")

    def __init__(self, eng, sem, fn, waits, inc_n):
        self.eng, self.sem, self.fn = eng, sem, fn
        self.waits = waits
        self.marked = False
        self.inc_n = inc_n
        self.cum = None


class Em:
    ENGS = ("tensor", "vector", "scalar", "gpsimd", "sync")

    def __init__(self):
        self.ops = {e: [] for e in self.ENGS}

    def op(self, eng, fn, waits=(), dma=None, inc=None):
        if dma:
            sem = "q_" + dma
            inc_n = 16
        else:
            sem = {"tensor": "pe", "vector": "dve", "scalar": "act",
                   "gpsimd": "pool", "sync": "sse"}[eng]
            inc_n = 1
        if inc is not None:
            sem, inc_n = inc
        o = Op(eng, sem, fn, [w for w in waits if w is not None], inc_n)
        if dma:
            o.marked = True
        self.ops[eng].append(o)
        for w in o.waits:
            w.marked = True
        return o

    def finalize(self, nc, block):
        cnt = {}
        for e in self.ENGS:
            for o in self.ops[e]:
                if o.marked:
                    cnt[o.sem] = cnt.get(o.sem, 0) + o.inc_n
                    o.cum = (o.sem, cnt[o.sem])
        stack = ExitStack()
        sems = {name: stack.enter_context(nc.semaphore("sem_" + name))
                for name in cnt}
        for e in self.ENGS:
            ops = self.ops[e]
            if not ops:
                continue

            def body(eng, ops=ops, e=e):
                waited = {}
                for o in ops:
                    for w in o.waits:
                        if w.cum is None:
                            continue
                        if (w.eng == e and e == "tensor"
                                and o.eng == "tensor" and w.inc_n == 1
                                and o.inc_n == 1):
                            continue
                        sname, v = w.cum
                        if waited.get(sname, 0) >= v:
                            continue
                        waited[sname] = v
                        eng.wait_ge(sems[sname], v)
                    instr = o.fn(eng)
                    if o.marked:
                        instr.then_inc(sems[o.cum[0]], o.inc_n)
            getattr(block, e)(body)
        return stack


class Ring:
    def __init__(self, n):
        self.n = n
        self.slots = [[] for _ in range(n)]
        self.k = 0

    def next(self):
        i = self.k % self.n
        self.k += 1
        w = self.slots[i]
        self.slots[i] = []
        return i, w

    def used(self, i, *ops):
        self.slots[i].extend(o for o in ops if o is not None)


def host_consts():
    ident = np.eye(128, dtype=np.float32)
    tri = (np.arange(128)[None, :] >= np.arange(128)[:, None]).astype(np.float32)
    p = np.zeros((128, 128), dtype=np.float32)
    for i in range(128):
        p[i, (i // 32) * 32 + ((i % 32) ^ 1)] = 1.0
    d_idx = np.arange(32)
    inv = 10000.0 ** (-2.0 * (d_idx // 2).astype(np.float64) / 32.0)
    ang = np.arange(S, dtype=np.float64)[None, :] * inv[:, None]
    sgn = np.where(d_idx % 2 == 0, -1.0, 1.0)
    ropeC = np.tile(np.cos(ang).astype(np.float32), (4, 1))
    ropeS = np.tile((sgn[:, None] * np.sin(ang)).astype(np.float32), (4, 1))
    return dict(ident=ident, tri=tri, p128=p, ropec=ropeC, ropes=ropeS)


def build_program(dump=(), n_cores=NCORE):
    nc = bass.Bass("TRN2", target_bir_lowering=False)
    em = Em()

    def din(name, shape):
        return nc.dram_tensor(name, list(shape), F32, kind="ExternalInput").ap()

    x_in = din("x", (NB, S, D))
    w_dq, w_dkv = din("w_dq", (D, DL)), din("w_dkv", (D, DL))
    w_uq, w_qr = din("w_uq", (DL, D)), din("w_qr", (DL, NH * DHR))
    w_uk, w_uv = din("w_uk", (DL, D)), din("w_uv", (DL, D))
    w_kr, w_o = din("w_kr", (D, DHR)), din("w_o", (D, D))
    wg_d, wu_d, wd_d = din("wg", (II, D)), din("wu", (II, D)), din("wd", (D, II))
    wmod_d = din("wmod", (D, 6 * D))
    c_d = din("cvec", (NB, D))
    clipx_d, clipd_d = din("clipx", (1, 1)), din("clipd", (1, 1))
    ident_d, tri_d = din("ident", (128, 128)), din("tri", (128, 128))
    p128_d = din("p128", (128, 128))
    ropec_d, ropes_d = din("ropec", (128, S)), din("ropes", (128, S))

    y_out = nc.dram_tensor("y", [NB, S, D], F32, kind="ExternalOutput").ap()

    def dint(name, shape):
        kind = "ExternalOutput" if name in dump else "Internal"
        return nc.dram_tensor(name, list(shape), F32, kind=kind).ap()

    x1scr = dint("x1scr", (NB, S, D))
    h2scr = dint("h2scr", (NB, S, D))
    ascr = dint("ascr", (IC, NB * QT, 128, 512))
    cc1i, cc1o = dint("cc1i", (1, 1)), dint("cc1o", (1, 1))
    cc2i, cc2o = dint("cc2i", (1, 1)), dint("cc2o", (1, 1))

    dump_ops = []

    alloc = ExitStack()

    def sb(name, shape, dt=F32, st=None):
        return (st or alloc).enter_context(nc.sbuf_tensor(name, list(shape), dt))

    def psum(name, shape, dt=F32, st=None):
        return (st or alloc).enter_context(nc.psum_tensor(name, list(shape), dt))

    def dump_sbuf(name, ap, shape, dtype=F32, waits=()):
        if name not in dump:
            return
        d = nc.dram_tensor(name, list(shape), dtype, kind="ExternalOutput").ap()
        dump_ops.append(em.op("gpsimd", lambda g: g.dma_start(out=d, in_=ap),
                              waits=list(waits), dma="dbg"))

    ident_f = sb("ident_f", (128, 128))
    ident_b = sb("ident_b", (128, 128), BF16)
    ones_t = sb("ones_t", (128, 128))
    sfm = sb("sfm", (128, C8 * NB))
    sfm_b = sb("sfm_b", (128, C8 * NB), BF16)
    modfm = sb("modfm", (128, 96))
    sc = sb("sc", (128, 8))
    scal = sb("scal", (1, 16))
    xr = [sb(f"xr{i}", (128, D)) for i in range(2)]
    stg = [sb(f"stg{i}", (128, D)) for i in range(2)]
    sqs = sb("sqs", (128, D))
    stat = sb("stat", (128, 12))
    amax1 = sb("amax1", (128, 4))
    garow = sb("garow", (128, D))
    rsc1 = sb("rsc1", (128, 1024))
    rsc2 = sb("rsc2", (128, 1024))
    ctile = rsc1[0:NB, :]
    silu_c = sqs[0:NB, :]

    stA = ExitStack()
    tri_b = sb("tri_b", (128, 128), BF16, st=stA)
    p128_b = sb("p128_b", (128, 128), BF16, st=stA)
    ropeC = sb("ropeC", (128, S), BF16, st=stA)
    ropeS = sb("ropeS", (128, S), BF16, st=stA)
    WMODW = 1024
    wmb = [sb(f"wmb{i}", (128, WMODW), st=stA) for i in range(2)]
    wdq_s = sb("wdq_s", (128, C8 * DL), BF16, st=stA)
    wdkv_s = sb("wdkv_s", (128, C8 * DL), BF16, st=stA)
    wkr_s = sb("wkr_s", (128, C8 * DHR), BF16, st=stA)
    wuq_s = sb("wuq_s", (128, LC * D), BF16, st=stA)
    wqr_s = sb("wqr_s", (128, LC * NH * DHR), BF16, st=stA)
    wuk_s = sb("wuk_s", (128, LC * D), BF16, st=stA)
    wuv_s = sb("wuv_s", (128, LC * D), BF16, st=stA)
    wo_s = sb("wo_s", (128, C8 * D), BF16, st=stA)
    wdq_p = sb("wdq_p", (128, C8 * DL), BF16, st=stA)
    wdkv_p = sb("wdkv_p", (128, C8 * DL), BF16, st=stA)
    wkr_p = sb("wkr_p", (128, C8 * DHR), BF16, st=stA)
    bias_cq = sb("bias_cq", (128, LC), st=stA)
    bias_ckv = sb("bias_ckv", (128, LC), st=stA)
    bias_kr = sb("bias_kr", (128, 1), st=stA)
    hbuf = sb("hbuf", (128, C8 * S), BF16, st=stA)
    nrmb = sb("nrmb", (128, D), BF16, st=stA)
    cq_s = sb("cq_s", (128, LC * S), BF16, st=stA)
    ckv_s = sb("ckv_s", (128, LC * S), BF16, st=stA)
    krt = sb("krt", (128, S), BF16, st=stA)
    kg = sb("kg", (128, 4 * S), BF16, st=stA)
    vg = sb("vg", (128, NT * 4 * 65), BF16, st=stA)
    qgt = sb("qgt", (128, 4 * S), BF16, st=stA)
    qrq = sb("qrq", (128, S), BF16, st=stA)
    ebuf = [sb(f"ebuf{i}", (128, 1024), BF16, st=stA) for i in range(4)]
    anorm = sb("anorm", (128, 512), st=stA)
    zrow = anorm[64:65, 0:512]

    psS = [psum("psS_A", (128, 1024), st=stA), psum("psS_B", (128, 1024), st=stA)]
    psACC = psum("psACC", (128, 512), st=stA)
    psZ = psum("psZ", (128, 512), st=stA)
    psU = psum("psU", (128, 512), st=stA)
    psT = psum("psT", (128, 1024), BF16, st=stA)

    pss = Ring(2)
    psu = Ring(1)
    xr_ring = Ring(2)
    stg_ring = Ring(2)
    eb_ring = Ring(4)

    ld_w = {}
    for key, w, dst, width in (("dq", w_dq, wdq_s, DL), ("dkv", w_dkv, wdkv_s, DL),
                               ("kr", w_kr, wkr_s, DHR), ("o", w_o, wo_s, D)):
        ld_w[key] = em.op("gpsimd", lambda g, w=w, dst=dst, width=width:
                          g.dma_start(
            out=dst[:].rearrange("p (c j) -> p c j", j=width),
            in_=w.rearrange("(c p) j -> p c j", p=128)), dma="gw")
    for key, w, dst, width in (("uq", w_uq, wuq_s, D),
                               ("qr", w_qr, wqr_s, NH * DHR),
                               ("uk", w_uk, wuk_s, D), ("uv", w_uv, wuv_s, D)):
        ld_w[key] = em.op("gpsimd", lambda g, w=w, dst=dst, width=width:
                          g.dma_start(
            out=dst[:].rearrange("p (l j) -> p l j", j=width),
            in_=w.rearrange("(l p) j -> p l j", p=128)), dma="gw")
    ld_cst = {}
    for key, t, d_ in (("identb", ident_b, ident_d), ("tri", tri_b, tri_d),
                       ("p128", p128_b, p128_d), ("ropec", ropeC, ropec_d),
                       ("ropes", ropeS, ropes_d)):
        ld_cst[key] = em.op("gpsimd", lambda g, t=t, d_=d_:
                            g.dma_start(out=t[:], in_=d_), dma="gw")
    ld_ones = em.op("gpsimd", lambda g: g.memset(ones_t[:], 1.0))
    for key, t, d_ in (("identf", ident_f, ident_d), ("c", ctile, c_d)):
        ld_cst[key] = em.op("sync", lambda s, t=t, d_=d_:
                            s.dma_start(out=t[:], in_=d_), dma="sc0")
    ld_cst["clipx"] = em.op("sync", lambda s: s.dma_start(
        out=scal[0:1, 0:1], in_=clipx_d), dma="sc0")
    ld_cst["clipd"] = em.op("sync", lambda s: s.dma_start(
        out=scal[0:1, 1:2], in_=clipd_d), dma="sc0")
    NHALF = (6 * D) // WMODW
    gw_last = ld_cst["ropes"]
    for k in list(ld_w):
        ld_w[k] = gw_last
    for k in ("identb", "tri", "p128", "ropec", "ropes"):
        ld_cst[k] = gw_last
    sc_last = ld_cst["clipd"]
    for k in ("identf", "c", "clipx", "clipd"):
        ld_cst[k] = sc_last
    ld_wmb = [em.op("sync", lambda s: s.dma_start(
        out=wmb[0][:], in_=wmod_d[0:128, 0:WMODW]), dma="wmb0")]

    o_silu = em.op("scalar", lambda s: s.activation(
        silu_c[0:NB, :], ctile[0:NB, :], AF.Silu), waits=[ld_cst["c"]])
    o_sfm = None
    for c in range(C8):
        sl, w = pss.next()
        o_t = em.op("tensor", lambda t, c=c, sl=sl: t.transpose(
            psS[sl][:, 0:NB], silu_c[0:NB, c * 128:(c + 1) * 128],
            ident_f[0:NB, 0:NB]), waits=[o_silu, ld_cst["identf"]] + w)
        o_sfm = em.op("vector", lambda v, c=c, sl=sl: v.tensor_copy(
            sfm[:, c * NB:(c + 1) * NB], psS[sl][:, 0:NB]), waits=[o_t])
        pss.used(sl, o_sfm)
    o_sfmb = None

    JH = WMODW // 128
    o_mm_last = None
    mod_mms = []
    sl_u, w_u = psu.next()
    for li in range(C8 * NHALF):
        dc, half = li // NHALF, li % NHALF
        if li + 1 < C8 * NHALF:
            dc2, half2 = (li + 1) // NHALF, (li + 1) % NHALF
            ld_wmb.append(em.op(
                "sync", lambda s, dc2=dc2, half2=half2, li=li: s.dma_start(
                    out=wmb[(li + 1) % 2][:],
                    in_=wmod_d[dc2 * 128:(dc2 + 1) * 128,
                               half2 * WMODW:(half2 + 1) * WMODW]),
                waits=[mod_mms[-1]] if li >= 1 else [],
                dma=f"wmb{(li + 1) % 2}"))
        for j in range(JH):
            jj = half * JH + j
            col = (jj // 8) * 16 + (jj % 8) * NB
            o_mm_last = em.op("tensor", lambda t, li=li, j=j, col=col: t.matmul(
                psU[:, col:col + NB], wmb[li % 2][:, j * 128:(j + 1) * 128],
                sfm[:, (li // NHALF) * NB:(li // NHALF + 1) * NB],
                start=(li == 0 and j == 0),
                stop=(li == C8 * NHALF - 1 and j == JH - 1)),
                waits=[o_sfm, ld_wmb[li]] + w_u)
            w_u = []
        mod_mms.append(o_mm_last)
    acc_last = em.op("vector", lambda v: v.tensor_copy(
        modfm[:, 0:96], psU[:, 0:96]), waits=[o_mm_last])
    psu.used(0, acc_last)

    em.op("vector", lambda v: v.tensor_scalar(
        modfm[:, 16:32], modfm[:, 16:32], 1.0, None, ALU.add), waits=[acc_last])
    em.op("vector", lambda v: v.tensor_scalar(
        modfm[:, 64:80], modfm[:, 64:80], 1.0, None, ALU.add), waits=[acc_last])
    o_mod = None
    for base in (32, 80):
        o1 = em.op("vector", lambda v, base=base: v.tensor_scalar(
            rsc1[:, 0:16], modfm[:, base:base + 16], 0.0, None, ALU.is_ge),
            waits=[acc_last] + ([o_mod] if o_mod else []))
        o2 = em.op("vector", lambda v: v.tensor_scalar(
            rsc1[:, 0:16], rsc1[:, 0:16], 2.0, -1.0, ALU.mult, ALU.add),
            waits=[o1])
        o3 = em.op("vector", lambda v, base=base: v.tensor_tensor(
            rsc1[:, 16:32], modfm[:, base:base + 16], rsc1[:, 0:16], ALU.mult),
            waits=[o2])
        o4 = em.op("vector", lambda v: v.tensor_scalar(
            rsc1[:, 16:32], rsc1[:, 16:32], 0.25, None, ALU.max), waits=[o3])
        o_mod = em.op("vector", lambda v, base=base: v.tensor_tensor(
            modfm[:, base:base + 16], rsc1[:, 16:32], rsc1[:, 0:16], ALU.mult),
            waits=[o4])
    o_sfmb = em.op("vector", lambda v: v.tensor_copy(sfm_b[:], modfm[:, 0:16]),
                   waits=[o_mod])
    dump_sbuf("dbg_modfm", modfm[:], (128, 96), waits=[o_mod])

    bcast_chain = [None]

    def build_bcast(target, v_i, b, pstile, ring, scale_col=None, extra=()):
        outs = []
        o2 = bcast_chain[0]
        for c in range(C8):
            col = v_i * 16 + c * 2 + b
            o1 = em.op("vector", lambda vv, col=col: vv.tensor_scalar(
                rsc2[:, 0:128], ones_t[:, 0:128], modfm[:, col:col + 1], None,
                ALU.mult), waits=[o_mod, ld_ones, o2] + list(extra))
            sl, w = ring.next()
            o2 = em.op("tensor", lambda t, sl=sl: t.transpose(
                pstile[sl][:, 0:128] if isinstance(pstile, list)
                else pstile[:, 0:128], rsc2[:, 0:128], ident_f[:]),
                waits=[o1, ld_cst["identf"]] + w)
            src = (pstile[sl] if isinstance(pstile, list) else pstile)
            if scale_col is None:
                o3 = em.op("vector", lambda vv, c=c, src=src: vv.tensor_copy(
                    target[:, c * 128:(c + 1) * 128], src[:, 0:128]),
                    waits=[o2])
            else:
                o3 = em.op("vector", lambda vv, c=c, src=src: vv.tensor_scalar(
                    target[:, c * 128:(c + 1) * 128], src[:, 0:128],
                    sc[:, scale_col:scale_col + 1], None, ALU.mult),
                    waits=[o2] + list(extra))
            ring.used(sl, o3)
            outs.append(o3)
        bcast_chain[0] = o2
        return outs

    def ln_stats(src_tile, t_par, waits):
        sb0 = (t_par % 2) * 6
        o_sq = em.op("scalar", lambda s: s.activation(
            sqs[:], src_tile[:], AF.Square, accum_out=stat[:, sb0 + 1:sb0 + 2]),
            waits=waits)
        o_sum = em.op("vector", lambda v: v.tensor_reduce(
            stat[:, sb0:sb0 + 1], src_tile[:], axis=mybir.AxisListType.X,
            op=ALU.add), waits=waits)
        o_mean = em.op("vector", lambda v: v.tensor_scalar(
            stat[:, sb0 + 2:sb0 + 3], stat[:, sb0:sb0 + 1], 1.0 / D, None,
            ALU.mult), waits=[o_sum])
        o_msq = em.op("vector", lambda v: v.tensor_scalar(
            stat[:, sb0 + 3:sb0 + 4], stat[:, sb0 + 1:sb0 + 2], 1.0 / D, None,
            ALU.mult), waits=[o_sq])
        o_mm2 = em.op("vector", lambda v: v.tensor_tensor(
            stat[:, sb0 + 4:sb0 + 5], stat[:, sb0 + 2:sb0 + 3],
            stat[:, sb0 + 2:sb0 + 3], ALU.mult), waits=[o_mean])
        o_var = em.op("vector", lambda v: v.scalar_tensor_tensor(
            stat[:, sb0 + 3:sb0 + 4], stat[:, sb0 + 3:sb0 + 4], 1e-5,
            stat[:, sb0 + 4:sb0 + 5], ALU.add, ALU.subtract),
            waits=[o_msq, o_mm2])
        o_std = em.op("scalar", lambda s: s.activation(
            stat[:, sb0 + 4:sb0 + 5], stat[:, sb0 + 3:sb0 + 4], AF.Sqrt),
            waits=[o_var])
        o_rstd = em.op("vector", lambda v: v.reciprocal(
            stat[:, sb0 + 3:sb0 + 4], stat[:, sb0 + 4:sb0 + 5]), waits=[o_std])
        o_nb1 = em.op("vector", lambda v: v.tensor_tensor(
            stat[:, sb0 + 5:sb0 + 6], stat[:, sb0 + 2:sb0 + 3],
            stat[:, sb0 + 3:sb0 + 4], ALU.mult), waits=[o_rstd])
        o_nb = em.op("vector", lambda v: v.tensor_scalar(
            stat[:, sb0 + 5:sb0 + 6], stat[:, sb0 + 5:sb0 + 6], -1.0, None,
            ALU.mult), waits=[o_nb1])
        return sb0, o_rstd, o_nb, o_sq, o_sum

    o_x1_wr = []
    kg_gate = []
    vg_ones_tok = None
    prev_psT_evac = None
    o_attn_last = None
    acc_readers = []
    prev_bc = None

    for b in range(NB):
        o_h1_done = None
        for t in range(NT):
            xi, xw = xr_ring.next()
            o_ld = em.op("sync", lambda s, b=b, t=t, xi=xi: s.dma_start(
                out=xr[xi][:], in_=x_in[b, t * 128:(t + 1) * 128, :]),
                waits=xw, dma=f"xr{xi}")
            sb0, o_rstd, o_nb, o_sq, o_sum = ln_stats(
                xr[xi], t, [o_ld] + ([o_sfm] if (b == 0 and t == 0) else []))
            o_nrm = em.op("vector", lambda v, xi=xi, sb0=sb0: v.tensor_scalar(
                nrmb[:], xr[xi][:], stat[:, sb0 + 3:sb0 + 4],
                stat[:, sb0 + 5:sb0 + 6], ALU.mult, ALU.add),
                waits=[o_nb, prev_psT_evac] + kg_gate)
            tps = []
            for c in range(C8):
                tps.append(em.op("tensor", lambda t_, c=c: t_.transpose(
                    psT[:, c * 128:(c + 1) * 128],
                    nrmb[:, c * 128:(c + 1) * 128], ident_b[:]),
                    waits=[o_nrm, ld_cst["identb"], prev_psT_evac]))
            o_h1_done = em.op("vector", lambda v, t=t: v.tensor_copy(
                hbuf[:].rearrange("p (c s) -> p c s", s=S)[:, :, t * 128:(t + 1) * 128],
                psT[:].rearrange("p (c j) -> p c j", j=128)), waits=tps)
            prev_psT_evac = o_h1_done
            xr_ring.used(xi, o_sq, o_sum, o_nrm)
        if b == 0:
            dump_sbuf("dbg_h1fm", hbuf[:], (128, C8 * S), BF16,
                      waits=[o_h1_done])

        o_wp = None
        for (w_src, w_dst, width) in ((wdq_s, wdq_p, DL), (wdkv_s, wdkv_p, DL),
                                      (wkr_s, wkr_p, DHR)):
            for c in range(C8):
                col = 16 + c * 2 + b
                o_wp = em.op(
                    "vector",
                    lambda v, w_src=w_src, w_dst=w_dst, c=c, col=col, width=width:
                    v.tensor_scalar(
                        w_dst[:, c * width:(c + 1) * width],
                        w_src[:, c * width:(c + 1) * width],
                        modfm[:, col:col + 1], None, ALU.mult),
                    waits=[o_mod, ld_w["dq"], ld_w["dkv"], ld_w["kr"]] + kg_gate)
        for (w_src, dst, nl, width) in ((wdq_s, bias_cq, LC, DL),
                                        (wdkv_s, bias_ckv, LC, DL),
                                        (wkr_s, bias_kr, 1, DHR)):
            for l in range(nl):
                sl_u, w_u = psu.next()
                mpart = 128 if width == DL else DHR
                mms = []
                for c in range(C8):
                    sacol = c * 2 + b
                    mms.append(em.op(
                        "tensor",
                        lambda t, w_src=w_src, c=c, l=l, width=width,
                        mpart=mpart, sacol=sacol: t.matmul(
                            psU[0:mpart, 0:1],
                            w_src[:, c * width + l * 128:
                                  c * width + l * 128 + mpart],
                            sfm_b[:, sacol:sacol + 1],
                            start=(c == 0), stop=(c == C8 - 1)),
                        waits=[o_sfmb] + w_u))
                o_wp = em.op("vector",
                             lambda v, dst=dst, l=l, mpart=mpart: v.tensor_copy(
                                 dst[0:mpart, l:l + 1], psU[0:mpart, 0:1]),
                             waits=[mms[-1]])
                psu.used(0, o_wp)

        def proj_lat(w_p, bias_t, dst):
            last = None
            for l in range(LC):
                for q4 in range(4):
                    sl, w = pss.next()
                    mms = []
                    for c in range(C8):
                        mms.append(em.op(
                            "tensor",
                            lambda t, w_p=w_p, l=l, q4=q4, c=c, sl=sl: t.matmul(
                                psS[sl][:, 0:512],
                                w_p[:, c * DL + l * 128:c * DL + (l + 1) * 128],
                                hbuf[:, c * S + q4 * 512:c * S + (q4 + 1) * 512],
                                start=(c == 0), stop=(c == C8 - 1)),
                            waits=[o_h1_done, o_wp] + w))
                    last = em.op("vector", lambda v, dst=dst, l=l, q4=q4, sl=sl,
                                 bias_t=bias_t: v.tensor_scalar(
                        dst[:, l * S + q4 * 512:l * S + (q4 + 1) * 512],
                        psS[sl][:, 0:512], bias_t[:, l:l + 1], None, ALU.add),
                        waits=[mms[-1]])
                    pss.used(sl, last)
            return last

        o_cq = proj_lat(wdq_p, bias_cq, cq_s)
        o_ckv = proj_lat(wdkv_p, bias_ckv, ckv_s)
        if b == 0:
            dump_sbuf("dbg_cq", cq_s[:], (128, LC * S), BF16, waits=[o_cq])
            dump_sbuf("dbg_ckv", ckv_s[:], (128, LC * S), BF16, waits=[o_ckv])
            dump_sbuf("dbg_wdqp", wdq_p[:], (128, C8 * DL), BF16, waits=[o_wp])
            dump_sbuf("dbg_biascq", bias_cq[:], (128, LC), F32, waits=[o_wp])

        o_kr = None
        for q4 in range(4):
            sl, w = pss.next()
            mms = []
            for c in range(C8):
                mms.append(em.op("tensor", lambda t, c=c, q4=q4, sl=sl: t.matmul(
                    psS[sl][0:32, 0:512], wkr_p[:, c * DHR:(c + 1) * DHR],
                    hbuf[:, c * S + q4 * 512:c * S + (q4 + 1) * 512],
                    start=(c == 0), stop=(c == C8 - 1)),
                    waits=[o_h1_done, o_wp] + w))
            o_e = em.op("vector", lambda v, q4=q4, sl=sl: v.tensor_scalar(
                krt[64:96, q4 * 512:(q4 + 1) * 512], psS[sl][0:32, 0:512],
                bias_kr[0:32, 0:1], None, ALU.add),
                waits=[mms[-1]] + kg_gate)
            pss.used(sl, o_e)
            sl_u, w_u = psu.next()
            o_sw = em.op("tensor", lambda t, q4=q4: t.matmul(
                psU[64:96, 0:512], p128_b[64:96, 64:96],
                krt[64:96, q4 * 512:(q4 + 1) * 512], start=True, stop=True),
                waits=[o_e, ld_cst["p128"]] + w_u)
            o_r1 = em.op("vector", lambda v, q4=q4: v.tensor_tensor(
                rsc1[64:96, 0:512], krt[64:96, q4 * 512:(q4 + 1) * 512],
                ropeC[64:96, q4 * 512:(q4 + 1) * 512], ALU.mult),
                waits=[o_e, ld_cst["ropec"]])
            o_r2 = em.op("vector", lambda v, q4=q4: v.tensor_tensor(
                rsc2[64:96, 0:512], psU[64:96, 0:512],
                ropeS[64:96, q4 * 512:(q4 + 1) * 512], ALU.mult),
                waits=[o_sw, ld_cst["ropes"]])
            o_kr = em.op("vector", lambda v, q4=q4: v.tensor_tensor(
                krt[64:96, q4 * 512:(q4 + 1) * 512], rsc1[64:96, 0:512],
                rsc2[64:96, 0:512], ALU.add), waits=[o_r1, o_r2])
            psu.used(0, o_r2)

        ga_bc = build_bcast(garow, 2, b, psS, pss)

        vg_ones_tok = em.op("vector", lambda v: v.memset(
            vg[:].rearrange("p (t h j) -> p t h j", h=4, j=65)[:, :, :, 64:65],
            1.0), waits=kg_gate)
        av_ops = []
        for g in range(4):
            kv_gate = kg_gate if g == 0 else list(av_ops)
            for pr in range(2):
                h0 = (g * 4 + 2 * pr)
                for q4 in range(4):
                    sl, w = pss.next()
                    mms = []
                    for l in range(LC):
                        mms.append(em.op(
                            "tensor", lambda t, h0=h0, l=l, q4=q4, sl=sl:
                            t.matmul(
                                psS[sl][:, 0:512],
                                wuk_s[:, l * D + h0 * DH:l * D + (h0 + 2) * DH],
                                ckv_s[:, l * S + q4 * 512:l * S + (q4 + 1) * 512],
                                start=(l == 0), stop=(l == LC - 1)),
                            waits=[o_ckv, ld_w["uk"]] + w))
                    hh_e, hh_o = 2 * pr, 2 * pr + 1
                    o_e1 = em.op("scalar", lambda s, hh_e=hh_e, q4=q4, sl=sl:
                                 s.activation(
                        kg[0:64, hh_e * S + q4 * 512:hh_e * S + (q4 + 1) * 512],
                        psS[sl][0:64, 0:512], AF.Copy),
                        waits=[mms[-1]] + kv_gate)
                    o_e2 = em.op("vector", lambda v, hh_o=hh_o, q4=q4, sl=sl:
                                 v.tensor_copy(
                        kg[0:64, hh_o * S + q4 * 512:hh_o * S + (q4 + 1) * 512],
                        psS[sl][64:128, 0:512]), waits=[mms[-1]] + kv_gate)
                    pss.used(sl, o_e1, o_e2)
            kr_cp = []
            for hh in range(4):
                kr_cp.append(em.op("vector", lambda v, hh=hh: v.tensor_copy(
                    kg[64:96, hh * S:(hh + 1) * S], krt[64:96, :]),
                    waits=[o_kr] + kv_gate))
            vg_evs = []
            for tt in range(NT):
                sl, w = pss.next()
                mms = []
                for l in range(LC):
                    mms.append(em.op("tensor",
                                     lambda t, g=g, tt=tt, l=l, sl=sl: t.matmul(
                        psS[sl][:, 0:256],
                        ckv_s[:, l * S + tt * 128:l * S + (tt + 1) * 128],
                        wuv_s[:, l * D + g * 256:l * D + (g + 1) * 256],
                        start=(l == 0), stop=(l == LC - 1)),
                        waits=[o_ckv, ld_w["uv"]] + w))
                o_ve = em.op("vector", lambda v, tt=tt, sl=sl: v.tensor_copy(
                    vg[:, tt * 260:(tt + 1) * 260].rearrange(
                        "p (h j) -> p h j", j=65)[:, :, 0:64],
                    psS[sl][:, 0:256].rearrange("p (h j) -> p h j", j=64)),
                    waits=[mms[-1]] + kv_gate)
                pss.used(sl, o_ve)
                vg_evs.append(o_ve)
            q_evs = []
            for pr in range(2):
                h0 = (g * 4 + 2 * pr)
                for q4 in range(4):
                    sl, w = pss.next()
                    mms = []
                    for l in range(LC):
                        mms.append(em.op(
                            "tensor", lambda t, h0=h0, l=l, q4=q4, sl=sl:
                            t.matmul(
                                psS[sl][:, 0:512],
                                wuq_s[:, l * D + h0 * DH:l * D + (h0 + 2) * DH],
                                cq_s[:, l * S + q4 * 512:l * S + (q4 + 1) * 512],
                                start=(l == 0), stop=(l == LC - 1)),
                            waits=[o_cq, ld_w["uq"]] + w))
                    hh_e, hh_o = 2 * pr, 2 * pr + 1
                    o_e1 = em.op("scalar", lambda s, hh_e=hh_e, q4=q4, sl=sl:
                                 s.activation(
                        qgt[0:64, hh_e * S + q4 * 512:hh_e * S + (q4 + 1) * 512],
                        psS[sl][0:64, 0:512], AF.Copy),
                        waits=[mms[-1]] + kv_gate)
                    o_e2 = em.op("vector", lambda v, hh_o=hh_o, q4=q4, sl=sl:
                                 v.tensor_copy(
                        qgt[0:64, hh_o * S + q4 * 512:hh_o * S + (q4 + 1) * 512],
                        psS[sl][64:128, 0:512]), waits=[mms[-1]] + kv_gate)
                    pss.used(sl, o_e1, o_e2)
                    q_evs += [o_e1, o_e2]
            qr_last = None
            for q4 in range(4):
                sl, w = pss.next()
                mms = []
                for l in range(LC):
                    mms.append(em.op("tensor",
                                     lambda t, g=g, l=l, q4=q4, sl=sl: t.matmul(
                        psS[sl][:, 0:512],
                        wqr_s[:, l * NH * DHR + g * 128:
                              l * NH * DHR + (g + 1) * 128],
                        cq_s[:, l * S + q4 * 512:l * S + (q4 + 1) * 512],
                        start=(l == 0), stop=(l == LC - 1)),
                        waits=[o_cq, ld_w["qr"]] + w))
                o_qe = em.op("vector", lambda v, q4=q4, sl=sl: v.tensor_copy(
                    qrq[:, q4 * 512:(q4 + 1) * 512], psS[sl][:, 0:512]),
                    waits=[mms[-1]] + kv_gate)
                pss.used(sl, o_qe)
                sl_u, w_u = psu.next()
                o_sw = em.op("tensor", lambda t, q4=q4: t.matmul(
                    psU[:, 0:512], p128_b[:], qrq[:, q4 * 512:(q4 + 1) * 512],
                    start=True, stop=True), waits=[o_qe] + w_u)
                o_r1 = em.op("vector", lambda v, q4=q4: v.tensor_tensor(
                    rsc1[:, 0:512], qrq[:, q4 * 512:(q4 + 1) * 512],
                    ropeC[:, q4 * 512:(q4 + 1) * 512], ALU.mult), waits=[o_qe])
                o_r2 = em.op("vector", lambda v, q4=q4: v.tensor_tensor(
                    rsc2[:, 0:512], psU[:, 0:512],
                    ropeS[:, q4 * 512:(q4 + 1) * 512], ALU.mult), waits=[o_sw])
                qr_last = em.op("vector", lambda v, q4=q4: v.tensor_tensor(
                    qrq[:, q4 * 512:(q4 + 1) * 512], rsc1[:, 0:512],
                    rsc2[:, 0:512], ALU.add), waits=[o_r1, o_r2])
                psu.used(0, o_r2)
            for hh in range(4):
                q_evs.append(em.op("vector", lambda v, hh=hh: v.tensor_copy(
                    qgt[64:96, hh * S:(hh + 1) * S],
                    qrq[hh * 32:(hh + 1) * 32, :]), waits=[qr_last] + kv_gate))
            o_build = em.op("scalar", lambda s: s.copy(
                scal[0:1, 15:16], scal[0:1, 15:16]),
                waits=[o_e1, o_e2, kr_cp[-1], vg_evs[-1]] + q_evs[-4:])
            if b == 0 and g == 0:
                dump_sbuf("dbg_kg", kg[:], (128, 4 * S), BF16, waits=[o_build])
                dump_sbuf("dbg_vg", vg[:], (128, NT * 4 * 65), BF16,
                          waits=[o_build])
                dump_sbuf("dbg_qg", qgt[:], (128, 4 * S), BF16, waits=[o_build])

            av_ops = []
            for hh in range(4):
                h = g * 4 + hh
                for qt in range(QT):
                    q0 = qt * 512
                    groups = []
                    nfull = 4 * qt
                    for kp in range(nfull // 2):
                        groups.append([(2 * kp, 0, 512, 0),
                                       (2 * kp + 1, 512, 512, 0)])
                    kd = nfull
                    groups.append([(kd, 0, 512, 0), (kd + 1, 512, 384, 128),
                                   (kd + 3, 896, 128, 384)])
                    groups.append([(kd + 2, 0, 256, 256)])
                    n_g = len(groups)
                    ginfo = {}

                    def emit_scores(gi):
                        regs = groups[gi]
                        sl, w = pss.next()
                        first_waits = [o_build] + w
                        mms = []
                        for (k, col, n, qs) in regs:
                            mms.append(em.op(
                                "tensor",
                                lambda t, k=k, col=col, n=n, qs=qs, sl=sl,
                                hh=hh, q0=q0: t.matmul(
                                    psS[sl][:, col:col + n],
                                    kg[0:96, hh * S + k * 128:
                                       hh * S + (k + 1) * 128],
                                    qgt[0:96, hh * S + q0 + qs:hh * S + q0 + 512],
                                    start=True, stop=True),
                                waits=first_waits))
                            first_waits = []
                        wid0 = regs[0][1]
                        tot = (regs[-1][1] + regs[-1][2]) - wid0
                        ei, ew = eb_ring.next()
                        o_exp = em.op("scalar", lambda s, sl=sl, ei=ei,
                                      wid0=wid0, tot=tot: s.activation(
                            ebuf[ei][:, wid0:wid0 + tot],
                            psS[sl][:, wid0:wid0 + tot], AF.Exp, scale=ISQ96),
                            waits=mms + ew)
                        pss.used(sl, o_exp)
                        o_clip = em.op("vector", lambda v, ei=ei, wid0=wid0,
                                       tot=tot: v.tensor_scalar(
                            ebuf[ei][:, wid0:wid0 + tot],
                            ebuf[ei][:, wid0:wid0 + tot],
                            EXP_HI, EXP_LO, ALU.min, ALU.max), waits=[o_exp])
                        gates = [o_clip]
                        diag = regs[-1][3] > 0 or regs[0][0] >= nfull
                        if diag:
                            for (k, col, n, qs) in regs:
                                gates.append(em.op(
                                    "gpsimd",
                                    lambda g_, ei=ei, col=col: g_.tensor_tensor(
                                        ebuf[ei][:, col:col + 128],
                                        ebuf[ei][:, col:col + 128],
                                        tri_b[:], ALU.mult),
                                    waits=[o_clip, ld_cst["tri"]]))
                        ginfo[gi] = (ei, regs, gates)

                    def emit_attnv(gi, first, last):
                        nonlocal av_ops
                        ei, regs, gates = ginfo[gi]
                        outs = []
                        for ri, (k, col, n, qs) in enumerate(regs):
                            ww = list(gates) + [vg_ones_tok]
                            if first and ri == 0:
                                ww += acc_readers
                            outs.append(em.op(
                                "tensor",
                                lambda t, k=k, col=col, n=n, qs=qs, ei=ei,
                                hh=hh, st=(first and ri == 0),
                                sp=(last and ri == len(regs) - 1): t.matmul(
                                    psACC[0:65, qs:512],
                                    vg[:, k * 260 + hh * 65:
                                       k * 260 + (hh + 1) * 65],
                                    ebuf[ei][:, col:col + n],
                                    start=st, stop=sp),
                                waits=ww))
                        eb_ring.used(ei, *outs)
                        av_ops += outs
                        return outs

                    avq = []
                    for gi in range(n_g + 1):
                        if gi < n_g:
                            emit_scores(gi)
                        if gi >= 1:
                            avq += emit_attnv(gi - 1, gi == 1, gi == n_g)
                    o_rec = em.op("vector", lambda v: v.reciprocal(
                        zrow, psACC[64:65, 0:512]),
                        waits=[avq[-1], prev_bc])
                    o_bc = em.op("tensor", lambda t: t.matmul(
                        psZ[0:64, 0:512], ones_t[64:65, 0:64],
                        zrow, start=True, stop=True),
                        waits=[o_rec, ld_ones, o_attn_last])
                    o_an = em.op("scalar", lambda s: s.activation(
                        anorm[0:64, 0:512], psACC[0:64, 0:512], AF.Copy),
                        waits=[avq[-1], o_attn_last])
                    o_nm = em.op("vector", lambda v, h=h, q0=q0: v.tensor_tensor(
                        hbuf[(h % 2) * 64:(h % 2) * 64 + 64,
                             (h // 2) * S + q0:(h // 2) * S + q0 + 512],
                        anorm[0:64, 0:512], psZ[0:64, 0:512], ALU.mult),
                        waits=[o_an, o_bc])
                    acc_readers = [o_rec, o_an]
                    prev_bc = o_bc
                    o_attn_last = o_nm
        if b == 0:
            dump_sbuf("dbg_attnout", hbuf[:], (128, C8 * S), BF16,
                      waits=[o_attn_last])

        for tt in range(NT):
            xi, xw = xr_ring.next()
            o_ldx = em.op("sync", lambda s, b=b, tt=tt, xi=xi: s.dma_start(
                out=xr[xi][:], in_=x_in[b, tt * 128:(tt + 1) * 128, :]),
                waits=xw, dma=f"xr{xi}")
            sl, slw = pss.next()
            mms = []
            for half in range(2):
                for c in range(C8):
                    mms.append(em.op(
                        "tensor", lambda t, tt=tt, half=half, c=c, sl=sl:
                        t.matmul(
                            psS[sl][:, half * 512:(half + 1) * 512],
                            hbuf[:, c * S + tt * 128:c * S + (tt + 1) * 128],
                            wo_s[:, c * D + half * 512:c * D + (half + 1) * 512],
                            start=(c == 0), stop=(c == C8 - 1)),
                        waits=[o_attn_last, ld_w["o"]] + slw))
            o_m = em.op("vector", lambda v, sl=sl: v.tensor_tensor(
                rsc1[:, 0:1024], psS[sl][:, 0:1024], garow[:], ALU.mult),
                waits=[mms[-1]] + ga_bc)
            pss.used(sl, o_m)
            si, sw = stg_ring.next()
            o_x1 = em.op("vector", lambda v, xi=xi, si=si: v.tensor_tensor(
                stg[si][:], rsc1[:, 0:1024], xr[xi][:], ALU.add),
                waits=[o_m, o_ldx] + sw)
            xr_ring.used(xi, o_x1)
            o_wr = em.op("scalar", lambda s, b=b, tt=tt, si=si: s.dma_start(
                out=x1scr[b, tt * 128:(tt + 1) * 128, :], in_=stg[si][:]),
                waits=[o_x1], dma=f"stg{si}")
            stg_ring.used(si, o_wr)
            o_x1_wr.append(o_wr)
        kg_gate = [mms[-1], o_attn_last]

    barrier = em.op("vector", lambda v: v.memset(amax1[:, 0:4], 0.0),
                    waits=[o_x1_wr[-1], o_attn_last, av_ops[-1]])

    stA.close()
    stF = ExitStack()
    stFa = ExitStack()
    wdT = sb("wdT", (128, IC * D), BF16, st=stF)
    wquant = sb("wquant", (128, 12), st=stF)
    cbrow = sb("cbrow", (128, D), st=stF)
    sbrow = sb("sbrow", (128, D), st=stF)
    qxfm = sb("qxfm", (128, C8 * NB * S), BF16, st=stFa)
    wgr = [sb(f"wgr{i}", (128, D), st=stFa) for i in range(2)]
    wur = [sb(f"wur{i}", (128, D), st=stFa) for i in range(2)]
    wgq = sb("wgq", (128, D), BF16, st=stFa)
    wuq2 = sb("wuq2", (128, D), BF16, st=stFa)
    wgT = [sb(f"wgT{i}", (128, D), BF16, st=stFa) for i in range(2)]
    wuT = [sb(f"wuT{i}", (128, D), BF16, st=stFa) for i in range(2)]
    wdr = sb("wdr", (128, II), st=stFa)
    wdq2 = sb("wdq2", (128, II), BF16, st=stFa)
    silu_t = [sb(f"silu{i}", (128, 512), st=stFa) for i in range(2)]
    atile = [sb(f"atile{i}", (128, 512), st=stFa) for i in range(3)]

    psBig = psum("psBig", (128, 2048), st=stF)
    psT2 = psum("psT2", (128, 1024), BF16, st=stF)
    psW = psum("psW", (128, 512), st=stF)
    psw = Ring(1)
    a_ring = Ring(3)

    o_h2_wr = []
    o_am_h2 = None
    o_pool_prev = None
    for b in range(NB):
        cb_bc = build_bcast(cbrow, 4, b, psW, psw,
                            extra=[barrier, o_pool_prev])
        sb_bc = build_bcast(sbrow, 3, b, psW, psw,
                            extra=[barrier, o_pool_prev])
        for t in range(NT):
            xi, xw = xr_ring.next()
            o_ld = em.op("sync", lambda s, b=b, t=t, xi=xi: s.dma_start(
                out=xr[xi][:], in_=x1scr[b, t * 128:(t + 1) * 128, :]),
                waits=xw + [o_x1_wr[b * NT + t], barrier], dma=f"xr{xi}")
            sb0, o_rstd, o_nb, o_sq, o_sum = ln_stats(xr[xi], t, [o_ld])
            o_nrm = em.op("vector", lambda v, xi=xi, sb0=sb0: v.tensor_scalar(
                rsc1[:, 0:1024], xr[xi][:], stat[:, sb0 + 3:sb0 + 4],
                stat[:, sb0 + 5:sb0 + 6], ALU.mult, ALU.add),
                waits=[o_nb, o_pool_prev])
            o_h2a = em.op("vector", lambda v: v.tensor_tensor(
                rsc2[:, 0:1024], rsc1[:, 0:1024], cbrow[:], ALU.mult),
                waits=[o_nrm] + cb_bc)
            si, sw = stg_ring.next()
            o_h2 = em.op("vector", lambda v, si=si: v.tensor_tensor(
                stg[si][:], rsc2[:, 0:1024], sbrow[:], ALU.add),
                waits=[o_h2a] + sb_bc + sw)
            o_pool_prev = o_h2
            o_wr = em.op("scalar", lambda s, b=b, t=t, si=si: s.dma_start(
                out=h2scr[b, t * 128:(t + 1) * 128, :], in_=stg[si][:]),
                waits=[o_h2], dma=f"stg{si}")
            o_am = em.op("vector", lambda v, si=si: v.tensor_reduce(
                amax1[:, 2:3], stg[si][:], axis=mybir.AxisListType.X,
                op=ALU.max, apply_absolute_value=True),
                waits=[o_h2, barrier])
            o_am_h2 = em.op("vector", lambda v: v.tensor_tensor(
                amax1[:, 0:1], amax1[:, 0:1], amax1[:, 2:3], ALU.max),
                waits=[o_am, o_am_h2])
            stg_ring.used(si, o_wr, o_am)
            xr_ring.used(xi, o_sq, o_sum, o_nrm)
            o_h2_wr.append(o_wr)

    def collective_scale(amax_col, cci, cco, clip_key, inv_col, sc_col, waits):
        sl_w, ww = psw.next()
        o_tp = em.op("tensor", lambda t: t.transpose(
            psW[0:1, 0:128], amax1[:, amax_col:amax_col + 1], ident_f[:]),
            waits=list(waits) + [ld_cst["identf"]] + ww)
        o_red = em.op("vector", lambda v: v.tensor_reduce(
            scal[0:1, 4:5], psW[0:1, 0:128], axis=mybir.AxisListType.X,
            op=ALU.max), waits=[o_tp])
        psw.used(0, o_red)
        o_out = em.op("gpsimd", lambda g: g.dma_start(out=cci,
                                                      in_=scal[0:1, 4:5]),
                      waits=[o_red], dma="cio" + clip_key)
        o_cc = em.op("gpsimd", lambda g: g.collective_compute(
            "AllReduce", ALU.max, replica_groups=[list(range(n_cores))],
            ins=[cci], outs=[cco]), waits=[o_out],
            inc=("ccs" + clip_key, 1))
        o_in = em.op("gpsimd", lambda g: g.dma_start(out=scal[0:1, 5:6],
                                                     in_=cco),
                     waits=[o_cc], dma="cio" + clip_key)
        o_sg = em.op("scalar", lambda s: s.activation(
            scal[0:1, 6:7], scal[0:1, 0:1] if clip_key == "x"
            else scal[0:1, 1:2], AF.Sigmoid),
            waits=[ld_cst["clipx"], ld_cst["clipd"], o_in])
        o_m1 = em.op("vector", lambda v: v.tensor_tensor(
            scal[0:1, 7:8], scal[0:1, 5:6], scal[0:1, 6:7], ALU.mult),
            waits=[o_sg, o_in])
        o_m2 = em.op("vector", lambda v: v.tensor_scalar(
            scal[0:1, 8:9], scal[0:1, 7:8], 1.0 / 127.0, None, ALU.mult),
            waits=[o_m1])
        o_rc = em.op("vector", lambda v: v.reciprocal(
            scal[0:1, 9:10], scal[0:1, 8:9]), waits=[o_m2])
        outs = []
        for (src, dstc) in ((9, inv_col), (8, sc_col)):
            sl_w, ww = psw.next()
            o_b1 = em.op("tensor", lambda t, src=src: t.matmul(
                psW[:, 0:1], ones_t[0:1, 0:128], scal[0:1, src:src + 1],
                start=True, stop=True), waits=[o_rc, ld_ones] + ww)
            o_cp = em.op("vector", lambda v, dstc=dstc: v.tensor_copy(
                sc[:, dstc:dstc + 1], psW[:, 0:1]), waits=[o_b1])
            psw.used(0, o_cp)
            outs.append(o_cp)
        return outs

    sx_ops = collective_scale(0, cc1i, cc1o, "x", 0, 1, [o_am_h2])

    prev_psT2 = None
    o_qx_last = None
    o_q3 = None
    for b in range(NB):
        for t in range(NT):
            xi, xw = xr_ring.next()
            o_ld = em.op("sync", lambda s, b=b, t=t, xi=xi: s.dma_start(
                out=xr[xi][:], in_=h2scr[b, t * 128:(t + 1) * 128, :]),
                waits=xw + [o_h2_wr[b * NT + t]], dma=f"xr{xi}")
            o_q1 = em.op("scalar", lambda sx_, xi=xi: sx_.activation(
                rsc1[:, 0:1024], xr[xi][:], AF.Copy,
                scale=sc[:, 0:1], bias=MAGIC),
                waits=[o_ld, o_q3] + sx_ops)
            o_q2 = em.op("vector", lambda v: v.tensor_scalar(
                rsc1[:, 0:1024], rsc1[:, 0:1024], MAGIC, 127.0, ALU.subtract,
                ALU.min), waits=[o_q1])
            o_q3 = em.op("vector", lambda v: v.tensor_scalar(
                rsc2[:, 0:1024], rsc1[:, 0:1024], -128.0, None, ALU.max),
                waits=[o_q2])
            o_cb = em.op("vector", lambda v: v.tensor_copy(
                wgq[:, 0:1024], rsc2[:, 0:1024]), waits=[o_q3, barrier])
            tps = []
            for c in range(C8):
                tps.append(em.op("tensor", lambda t_, c=c: t_.transpose(
                    psT2[:, c * 128:(c + 1) * 128],
                    wgq[:, c * 128:(c + 1) * 128], ident_b[:]),
                    waits=[o_cb, prev_psT2]))
            gcol = b * S + t * 128
            o_qx_last = em.op("vector", lambda v, gcol=gcol: v.tensor_copy(
                qxfm[:].rearrange("p (c s) -> p c s",
                                  s=NB * S)[:, :, gcol:gcol + 128],
                psT2[:].rearrange("p (c j) -> p c j", j=128)), waits=tps)
            prev_psT2 = o_qx_last
            xr_ring.used(xi, o_q1)
    dump_sbuf("dbg_qxfm", qxfm[:], (128, C8 * NB * S), BF16,
              waits=[o_qx_last])

    def quant_weight(dst_bf, src_f32, width, q0_, waits):
        o_r = em.op("vector", lambda v: v.tensor_reduce(
            wquant[:, q0_:q0_ + 1], src_f32[:, 0:width],
            axis=mybir.AxisListType.X, op=ALU.max, apply_absolute_value=True),
            waits=waits)
        o_s = em.op("vector", lambda v: v.tensor_scalar(
            wquant[:, q0_ + 1:q0_ + 2], wquant[:, q0_:q0_ + 1],
            1.0 / 127.0, None, ALU.mult), waits=[o_r])
        o_i = em.op("vector", lambda v: v.reciprocal(
            wquant[:, q0_ + 2:q0_ + 3], wquant[:, q0_ + 1:q0_ + 2]),
            waits=[o_s])
        o_1 = em.op("vector", lambda v: v.tensor_scalar(
            src_f32[:, 0:width], src_f32[:, 0:width],
            wquant[:, q0_ + 2:q0_ + 3], MAGIC, ALU.mult, ALU.add),
            waits=[o_i])
        o_2 = em.op("vector", lambda v: v.tensor_scalar(
            src_f32[:, 0:width], src_f32[:, 0:width], MAGIC, 127.0,
            ALU.subtract, ALU.min), waits=[o_1])
        o_3 = em.op("vector", lambda v: v.tensor_scalar(
            dst_bf[:, 0:width], src_f32[:, 0:width], -128.0,
            wquant[:, q0_ + 1:q0_ + 2], ALU.max, ALU.mult), waits=[o_2])
        return o_3

    a_wr = {}
    o_am_a = None
    gu_read = {0: [barrier], 1: [barrier]}
    slot_g = {}
    slot_u = {}
    wd_state = {"gate": [barrier], "last": None}

    def emit_wd_chunk(hc):
        nonlocal prev_psT2
        o_ldw = em.op("sync", lambda s, hc=hc: s.dma_start(
            out=wdr[:], in_=wd_d[hc * 128:(hc + 1) * 128, :]),
            waits=wd_state["gate"], dma="wdr")
        o_q = quant_weight(wdq2, wdr, II, 6, [o_ldw])
        wd_state["gate"] = [o_q]
        for icc in range(IC):
            o_tp = em.op("tensor", lambda t, icc=icc: t.transpose(
                psT2[:, (icc % 8) * 128:(icc % 8) * 128 + 128],
                wdq2[:, icc * 128:(icc + 1) * 128], ident_b[:]),
                waits=[o_q, prev_psT2])
            o_e = em.op("vector", lambda v, hc=hc, icc=icc: v.tensor_copy(
                wdT[:, icc * D + hc * 128:icc * D + (hc + 1) * 128],
                psT2[:, (icc % 8) * 128:(icc % 8) * 128 + 128]),
                waits=[o_tp])
            prev_psT2 = o_e
            wd_state["gate"].append(o_e)
            wd_state["last"] = o_e

    for ic in range(IC):
        buf = ic % 2
        if False and ic % 3 == 2 and ic // 3 < C8:
            emit_wd_chunk(ic // 3)
        o_lg = em.op("sync", lambda s, ic=ic, buf=buf: s.dma_start(
            out=wgr[buf][:], in_=wg_d[ic * 128:(ic + 1) * 128, :]),
            waits=gu_read[buf], dma=f"wg{buf}")
        o_lu = em.op("sync", lambda s, ic=ic, buf=buf: s.dma_start(
            out=wur[buf][:], in_=wu_d[ic * 128:(ic + 1) * 128, :]),
            waits=gu_read[buf], dma=f"wu{buf}")
        gu_read[buf] = []
        o_qg_ = quant_weight(wgq, wgr[buf], D, 0, [o_lg, o_qx_last, barrier])
        tps_g = []
        for c in range(C8):
            tps_g.append(em.op("tensor", lambda t, c=c: t.transpose(
                psT2[:, c * 128:(c + 1) * 128], wgq[:, c * 128:(c + 1) * 128],
                ident_b[:]), waits=[o_qg_, prev_psT2]))
        o_gT = em.op("vector", lambda v, buf=buf: v.tensor_copy(
            wgT[buf][:], psT2[:]), waits=tps_g)
        prev_psT2 = o_gT
        o_qu_ = quant_weight(wuq2, wur[buf], D, 3, [o_lu])
        tps_u = []
        for c in range(C8):
            tps_u.append(em.op("tensor", lambda t, c=c: t.transpose(
                psT2[:, c * 128:(c + 1) * 128], wuq2[:, c * 128:(c + 1) * 128],
                ident_b[:]), waits=[o_qu_, prev_psT2]))
        o_uT = em.op("vector", lambda v, buf=buf: v.tensor_copy(
            wuT[buf][:], psT2[:]), waits=tps_u)
        prev_psT2 = o_uT
        for t8 in range(NB * QT):
            slg = t8 % 2
            cg = slg * 512
            cu = 1024 + slg * 512
            mm_g, mm_u = [], []
            for c in range(C8):
                mm_g.append(em.op("tensor",
                                  lambda t, c=c, t8=t8, cg=cg, buf=buf:
                                  t.matmul(
                    psBig[:, cg:cg + 512], wgT[buf][:, c * 128:(c + 1) * 128],
                    qxfm[:, c * NB * S + t8 * 512:c * NB * S + (t8 + 1) * 512],
                    start=(c == 0), stop=(c == C8 - 1)),
                    waits=[o_gT, o_qx_last] + slot_g.get(slg, [])))
            for c in range(C8):
                mm_u.append(em.op("tensor",
                                  lambda t, c=c, t8=t8, cu=cu, buf=buf:
                                  t.matmul(
                    psBig[:, cu:cu + 512], wuT[buf][:, c * 128:(c + 1) * 128],
                    qxfm[:, c * NB * S + t8 * 512:c * NB * S + (t8 + 1) * 512],
                    start=(c == 0), stop=(c == C8 - 1)),
                    waits=[o_uT] + slot_u.get(slg, [])))
            o_si = em.op("scalar", lambda s, slg=slg, cg=cg: s.activation(
                silu_t[slg][:], psBig[:, cg:cg + 512], AF.Silu,
                scale=sc[:, 1:2]),
                waits=[mm_g[-1]] + sx_ops + slot_u.get(slg, []))
            ai, aw = a_ring.next()
            o_a = em.op("vector", lambda v, slg=slg, cu=cu, ai=ai:
                        v.scalar_tensor_tensor(
                atile[ai][:], psBig[:, cu:cu + 512], sc[:, 1:2],
                silu_t[slg][:], ALU.mult, ALU.mult),
                waits=[mm_u[-1], o_si] + aw + sx_ops)
            slot_g[slg] = [o_si]
            slot_u[slg] = [o_a]
            o_am = em.op("vector", lambda v, ai=ai: v.tensor_reduce(
                amax1[:, 3:4], atile[ai][:], axis=mybir.AxisListType.X,
                op=ALU.max, apply_absolute_value=True), waits=[o_a, barrier])
            o_am_a = em.op("vector", lambda v: v.tensor_tensor(
                amax1[:, 1:2], amax1[:, 1:2], amax1[:, 3:4], ALU.max),
                waits=[o_am, o_am_a])
            o_awr = em.op("scalar", lambda s, ic=ic, t8=t8, ai=ai: s.dma_start(
                out=ascr[ic, t8], in_=atile[ai][:]), waits=[o_a],
                dma=f"at{ai}")
            a_ring.used(ai, o_awr, o_am)
            a_wr[(ic, t8)] = o_awr
            gu_read[buf] += [mm_g[-1], mm_u[-1]]

    for hc in range(0, C8):
        emit_wd_chunk(hc)
    o_wdT = wd_state["last"]

    sd_ops = collective_scale(1, cc2i, cc2o, "d", 2, 3, [o_am_a])
    dump_sbuf("dbg_sc", sc[:], (128, 8), F32, waits=sx_ops + sd_ops)
    dump_sbuf("dbg_amax", amax1[:], (128, 4), F32, waits=sx_ops + sd_ops)
    dump_sbuf("dbg_scal", scal[:], (1, 16), F32, waits=sx_ops + sd_ops)

    stFa.close()
    stF2 = ExitStack()
    qh_sb = [sb(f"qh_sb{i}", (128, IC * 512), BF16, st=stF2) for i in range(2)]
    arin = [sb(f"arin{i}", (128, 512), st=stF2) for i in range(4)]
    ar_ring = Ring(4)
    y_wr = []
    qh_gate = {0: [], 1: []}
    psd_gate = {0: [], 1: []}
    gb_built = {}
    for t8 in range(NB * QT):
        b = t8 // QT
        if b not in gb_built:
            gb_built[b] = build_bcast(garow, 5, b, psW, psw, scale_col=3,
                                      extra=sd_ops + [barrier])
        qb = (t8 // 1) % 2
        qh_done = None
        for icc in range(IC):
            ai, aw = ar_ring.next()
            o_la = em.op("sync", lambda s, icc=icc, t8=t8, ai=ai: s.dma_start(
                out=arin[ai][:], in_=ascr[icc, t8]),
                waits=aw + [a_wr[(icc, t8)]], dma=f"ar{ai}")
            o_1 = em.op("vector", lambda v, ai=ai: v.tensor_scalar(
                arin[ai][:], arin[ai][:], sc[:, 2:3], MAGIC, ALU.mult,
                ALU.add), waits=[o_la] + sd_ops)
            o_2 = em.op("vector", lambda v, ai=ai: v.tensor_scalar(
                arin[ai][:], arin[ai][:], MAGIC, 127.0, ALU.subtract,
                ALU.min), waits=[o_1])
            qh_done = em.op("vector", lambda v, icc=icc, qb=qb, ai=ai:
                            v.tensor_scalar(
                qh_sb[qb][:, icc * 512:(icc + 1) * 512], arin[ai][:],
                -128.0, None, ALU.max), waits=[o_2] + qh_gate[qb])
            ar_ring.used(ai, qh_done)
        qh_gate[qb] = []
        for tk in range(4):
            sl = tk % 2
            c0 = sl * 1024
            gu_war = []
            if t8 == 0 and tk == 0:
                gu_war = slot_g.get(0, []) + slot_g.get(1, [])
            elif t8 == 0 and tk == 1:
                gu_war = slot_u.get(0, []) + slot_u.get(1, [])
            mms = []
            for half in range(2):
                for icc in range(IC):
                    mms.append(em.op(
                        "tensor",
                        lambda t, icc=icc, tk=tk, half=half, c0=c0, qb=qb:
                        t.matmul(
                            psBig[:, c0 + half * 512:c0 + (half + 1) * 512],
                            qh_sb[qb][:, icc * 512 + tk * 128:
                                      icc * 512 + (tk + 1) * 128],
                            wdT[:, icc * D + half * 512:
                                icc * D + (half + 1) * 512],
                            start=(icc == 0), stop=(icc == IC - 1)),
                        waits=[o_wdT, qh_done] + psd_gate[sl] + gu_war))
            qh_gate[qb].append(mms[-1])
            tt = (t8 % QT) * 4 + tk
            xi, xw = xr_ring.next()
            o_ldx = em.op("sync", lambda s, b=b, tt=tt, xi=xi: s.dma_start(
                out=xr[xi][:], in_=x1scr[b, tt * 128:(tt + 1) * 128, :]),
                waits=xw + [o_x1_wr[b * NT + tt]], dma=f"xr{xi}")
            o_m = em.op("vector", lambda v, c0=c0: v.tensor_tensor(
                rsc1[:, 0:1024], psBig[:, c0:c0 + 1024], garow[:], ALU.mult),
                waits=[mms[-1]] + gb_built[b])
            psd_gate[sl] = [o_m]
            si, sw = stg_ring.next()
            o_y = em.op("vector", lambda v, xi=xi, si=si: v.tensor_tensor(
                stg[si][:], rsc1[:, 0:1024], xr[xi][:], ALU.add),
                waits=[o_m, o_ldx] + sw)
            xr_ring.used(xi, o_y)
            o_wry = em.op("scalar", lambda s, b=b, tt=tt, si=si: s.dma_start(
                out=y_out[b, tt * 128:(tt + 1) * 128, :], in_=stg[si][:]),
                waits=[o_y], dma=f"stg{si}")
            stg_ring.used(si, o_wry)
            y_wr.append(o_wry)

    em.op("sync", lambda s: s.nop(), waits=y_wr + dump_ops)

    with nc.Block() as block:
        semstack = em.finalize(nc, block)
    semstack.close()
    stF2.close()
    stF.close()
    alloc.close()
    return nc


NAME_MAP = {
    "w_dq": "L0_attn_W_DQ", "w_uq": "L0_attn_W_UQ", "w_qr": "L0_attn_W_QR",
    "w_dkv": "L0_attn_W_DKV", "w_uk": "L0_attn_W_UK", "w_uv": "L0_attn_W_UV",
    "w_kr": "L0_attn_W_KR", "w_o": "L0_attn_W_O", "wg": "L0_ffn_w_gate",
    "wu": "L0_ffn_w_up", "wd": "L0_ffn_w_down", "wmod": "L0_wmod",
}


def kernel(**inputs):
    consts = host_consts()
    nc = build_program(dump=DUMP)
    x = np.ascontiguousarray(np.asarray(inputs["L0_attn_x"], dtype=np.float32))
    c = np.ascontiguousarray(np.asarray(inputs["L0_c"], dtype=np.float32))
    base = {k: np.ascontiguousarray(np.asarray(inputs[v], dtype=np.float32))
            for k, v in NAME_MAP.items()}
    base["clipx"] = np.asarray(inputs["L0_ffn_clip_x"],
                               dtype=np.float32).reshape(1, 1)
    base["clipd"] = np.asarray(inputs["L0_ffn_clip_down"],
                               dtype=np.float32).reshape(1, 1)
    base.update(consts)
    in_maps = []
    for k in range(NCORE):
        m = dict(base)
        m["x"] = np.ascontiguousarray(x[2 * k:2 * k + 2])
        m["cvec"] = np.ascontiguousarray(c[2 * k:2 * k + 2])
        in_maps.append(m)
    res = run_bass_kernel_spmd(nc, in_maps, core_ids=list(range(NCORE)))
    out = np.concatenate([np.asarray(res.results[k]["y"])
                          for k in range(NCORE)], axis=0)
    if DUMP:
        kernel.last_results = res
    return out


# version 07
# speedup vs baseline: 1.0584x; 1.0584x over previous
import math
from contextlib import ExitStack

import numpy as np
import concourse.bass as bass
import concourse.mybir as mybir
from concourse.bass_utils import run_bass_kernel_spmd

F32 = mybir.dt.float32
BF16 = mybir.dt.bfloat16
AF = mybir.ActivationFunctionType
ALU = mybir.AluOpType

MAGIC = 12582912.0
ISQ96 = float(np.float32(1.0) / np.sqrt(np.float32(96.0)))
EXP_HI = float(np.exp(80.0 / math.sqrt(96.0)))
EXP_LO = float(np.exp(-80.0 / math.sqrt(96.0)))

D, S, NB, NH, DH, DHR, DL, II = 1024, 2048, 2, 16, 64, 32, 256, 2816
C8, NT, QT, IC, LC = D // 128, S // 128, S // 512, II // 128, DL // 128
NCORE = 8

DUMP = ()


class Op:
    __slots__ = ("eng", "sem", "fn", "waits", "marked", "inc_n", "cum")

    def __init__(self, eng, sem, fn, waits, inc_n):
        self.eng, self.sem, self.fn = eng, sem, fn
        self.waits = waits
        self.marked = False
        self.inc_n = inc_n
        self.cum = None


class Em:
    ENGS = ("tensor", "vector", "scalar", "gpsimd", "sync")

    def __init__(self):
        self.ops = {e: [] for e in self.ENGS}

    def op(self, eng, fn, waits=(), dma=None, inc=None):
        if dma:
            sem = "q_" + dma
            inc_n = 16
        else:
            sem = {"tensor": "pe", "vector": "dve", "scalar": "act",
                   "gpsimd": "pool", "sync": "sse"}[eng]
            inc_n = 1
        if inc is not None:
            sem, inc_n = inc
        o = Op(eng, sem, fn, [w for w in waits if w is not None], inc_n)
        if dma:
            o.marked = True
        self.ops[eng].append(o)
        for w in o.waits:
            w.marked = True
        return o

    def finalize(self, nc, block):
        cnt = {}
        for e in self.ENGS:
            for o in self.ops[e]:
                if o.marked:
                    cnt[o.sem] = cnt.get(o.sem, 0) + o.inc_n
                    o.cum = (o.sem, cnt[o.sem])
        stack = ExitStack()
        sems = {name: stack.enter_context(nc.semaphore("sem_" + name))
                for name in cnt}
        for e in self.ENGS:
            ops = self.ops[e]
            if not ops:
                continue

            def body(eng, ops=ops, e=e):
                waited = {}
                for o in ops:
                    for w in o.waits:
                        if w.cum is None:
                            continue
                        if (w.eng == e and e == "tensor"
                                and o.eng == "tensor" and w.inc_n == 1
                                and o.inc_n == 1):
                            continue
                        sname, v = w.cum
                        if waited.get(sname, 0) >= v:
                            continue
                        waited[sname] = v
                        eng.wait_ge(sems[sname], v)
                    instr = o.fn(eng)
                    if o.marked:
                        instr.then_inc(sems[o.cum[0]], o.inc_n)
            getattr(block, e)(body)
        return stack


class Ring:
    def __init__(self, n):
        self.n = n
        self.slots = [[] for _ in range(n)]
        self.k = 0

    def next(self):
        i = self.k % self.n
        self.k += 1
        w = self.slots[i]
        self.slots[i] = []
        return i, w

    def used(self, i, *ops):
        self.slots[i].extend(o for o in ops if o is not None)


def host_consts():
    ident = np.eye(128, dtype=np.float32)
    tri = (np.arange(128)[None, :] >= np.arange(128)[:, None]).astype(np.float32)
    p = np.zeros((128, 128), dtype=np.float32)
    for i in range(128):
        p[i, (i // 32) * 32 + ((i % 32) ^ 1)] = 1.0
    d_idx = np.arange(32)
    inv = 10000.0 ** (-2.0 * (d_idx // 2).astype(np.float64) / 32.0)
    ang = np.arange(S, dtype=np.float64)[None, :] * inv[:, None]
    sgn = np.where(d_idx % 2 == 0, -1.0, 1.0)
    ropeC = np.tile(np.cos(ang).astype(np.float32), (4, 1))
    ropeS = np.tile((sgn[:, None] * np.sin(ang)).astype(np.float32), (4, 1))
    return dict(ident=ident, tri=tri, p128=p, ropec=ropeC, ropes=ropeS)


def build_program(dump=(), n_cores=NCORE):
    nc = bass.Bass("TRN2", target_bir_lowering=False)
    em = Em()

    def din(name, shape):
        return nc.dram_tensor(name, list(shape), F32, kind="ExternalInput").ap()

    x_in = din("x", (NB, S, D))
    w_dq, w_dkv = din("w_dq", (D, DL)), din("w_dkv", (D, DL))
    w_uq, w_qr = din("w_uq", (DL, D)), din("w_qr", (DL, NH * DHR))
    w_uk, w_uv = din("w_uk", (DL, D)), din("w_uv", (DL, D))
    w_kr, w_o = din("w_kr", (D, DHR)), din("w_o", (D, D))
    wg_d, wu_d, wd_d = din("wg", (II, D)), din("wu", (II, D)), din("wd", (D, II))
    wmod_d = din("wmod", (D, 6 * D))
    c_d = din("cvec", (NB, D))
    clipx_d, clipd_d = din("clipx", (1, 1)), din("clipd", (1, 1))
    ident_d, tri_d = din("ident", (128, 128)), din("tri", (128, 128))
    p128_d = din("p128", (128, 128))
    ropec_d, ropes_d = din("ropec", (128, S)), din("ropes", (128, S))

    y_out = nc.dram_tensor("y", [NB, S, D], F32, kind="ExternalOutput").ap()

    def dint(name, shape):
        kind = "ExternalOutput" if name in dump else "Internal"
        return nc.dram_tensor(name, list(shape), F32, kind=kind).ap()

    x1scr = dint("x1scr", (NB, S, D))
    h2scr = dint("h2scr", (NB, S, D))
    ascr = dint("ascr", (IC, NB * QT, 128, 512))
    cc1i, cc1o = dint("cc1i", (1, 1)), dint("cc1o", (1, 1))
    cc2i, cc2o = dint("cc2i", (1, 1)), dint("cc2o", (1, 1))

    dump_ops = []

    alloc = ExitStack()

    def sb(name, shape, dt=F32, st=None):
        return (st or alloc).enter_context(nc.sbuf_tensor(name, list(shape), dt))

    def psum(name, shape, dt=F32, st=None):
        return (st or alloc).enter_context(nc.psum_tensor(name, list(shape), dt))

    def dump_sbuf(name, ap, shape, dtype=F32, waits=()):
        if name not in dump:
            return
        d = nc.dram_tensor(name, list(shape), dtype, kind="ExternalOutput").ap()
        dump_ops.append(em.op("gpsimd", lambda g: g.dma_start(out=d, in_=ap),
                              waits=list(waits), dma="dbg"))

    ident_f = sb("ident_f", (128, 128))
    ident_b = sb("ident_b", (128, 128), BF16)
    ones_t = sb("ones_t", (128, 128))
    sfm = sb("sfm", (128, C8 * NB))
    sfm_b = sb("sfm_b", (128, C8 * NB), BF16)
    modfm = sb("modfm", (128, 96))
    sc = sb("sc", (128, 8))
    scal = sb("scal", (1, 16))
    xr = [sb(f"xr{i}", (128, D)) for i in range(2)]
    stg = [sb(f"stg{i}", (128, D)) for i in range(2)]
    sqs = sb("sqs", (128, D))
    stat = sb("stat", (128, 12))
    amax1 = sb("amax1", (128, 4))
    garow = sb("garow", (128, D))
    rsc1 = sb("rsc1", (128, 1024))
    rsc2 = sb("rsc2", (128, 1024))
    ctile = rsc1[0:NB, :]
    silu_c = sqs[0:NB, :]

    stA = ExitStack()
    tri_b = sb("tri_b", (128, 128), BF16, st=stA)
    p128_b = sb("p128_b", (128, 128), BF16, st=stA)
    ropeC = sb("ropeC", (128, S), BF16, st=stA)
    ropeS = sb("ropeS", (128, S), BF16, st=stA)
    WMODW = 1024
    wmb = [sb(f"wmb{i}", (128, WMODW), st=stA) for i in range(2)]
    wdq_s = sb("wdq_s", (128, C8 * DL), BF16, st=stA)
    wdkv_s = sb("wdkv_s", (128, C8 * DL), BF16, st=stA)
    wkr_s = sb("wkr_s", (128, C8 * DHR), BF16, st=stA)
    wuq_s = sb("wuq_s", (128, LC * D), BF16, st=stA)
    wqr_s = sb("wqr_s", (128, LC * NH * DHR), BF16, st=stA)
    wuk_s = sb("wuk_s", (128, LC * D), BF16, st=stA)
    wuv_s = sb("wuv_s", (128, LC * D), BF16, st=stA)
    wo_s = sb("wo_s", (128, C8 * D), BF16, st=stA)
    wdq_p = sb("wdq_p", (128, C8 * DL), BF16, st=stA)
    wdkv_p = sb("wdkv_p", (128, C8 * DL), BF16, st=stA)
    wkr_p = sb("wkr_p", (128, C8 * DHR), BF16, st=stA)
    bias_cq = sb("bias_cq", (128, LC), st=stA)
    bias_ckv = sb("bias_ckv", (128, LC), st=stA)
    bias_kr = sb("bias_kr", (128, 1), st=stA)
    hbuf = sb("hbuf", (128, C8 * S), BF16, st=stA)
    nrmb = sb("nrmb", (128, D), BF16, st=stA)
    cq_s = sb("cq_s", (128, LC * S), BF16, st=stA)
    ckv_s = sb("ckv_s", (128, LC * S), BF16, st=stA)
    krt = sb("krt", (128, S), BF16, st=stA)
    kg = sb("kg", (128, 4 * S), BF16, st=stA)
    vg = sb("vg", (128, NT * 4 * 65), BF16, st=stA)
    qgt = sb("qgt", (128, 4 * S), BF16, st=stA)
    qrq = sb("qrq", (128, S), BF16, st=stA)
    ebuf = [sb(f"ebuf{i}", (128, 1024), BF16, st=stA) for i in range(4)]
    anorm = sb("anorm", (128, 512), st=stA)
    zrow = anorm[64:65, 0:512]

    psS = [psum("psS_A", (128, 1024), st=stA), psum("psS_B", (128, 1024), st=stA)]
    psACC = psum("psACC", (128, 512), st=stA)
    psZ = psum("psZ", (128, 512), st=stA)
    psU = psum("psU", (128, 512), st=stA)
    psT = psum("psT", (128, 1024), BF16, st=stA)

    pss = Ring(2)
    psu = Ring(1)
    xr_ring = Ring(2)
    stg_ring = Ring(2)
    eb_ring = Ring(4)

    ld_w = {}
    for key, w, dst, width in (("dq", w_dq, wdq_s, DL), ("dkv", w_dkv, wdkv_s, DL),
                               ("kr", w_kr, wkr_s, DHR), ("o", w_o, wo_s, D)):
        ld_w[key] = em.op("gpsimd", lambda g, w=w, dst=dst, width=width:
                          g.dma_start(
            out=dst[:].rearrange("p (c j) -> p c j", j=width),
            in_=w.rearrange("(c p) j -> p c j", p=128)), dma="gw")
    for key, w, dst, width in (("uq", w_uq, wuq_s, D),
                               ("qr", w_qr, wqr_s, NH * DHR),
                               ("uk", w_uk, wuk_s, D), ("uv", w_uv, wuv_s, D)):
        ld_w[key] = em.op("gpsimd", lambda g, w=w, dst=dst, width=width:
                          g.dma_start(
            out=dst[:].rearrange("p (l j) -> p l j", j=width),
            in_=w.rearrange("(l p) j -> p l j", p=128)), dma="gw")
    ld_cst = {}
    for key, t, d_ in (("identb", ident_b, ident_d), ("tri", tri_b, tri_d),
                       ("p128", p128_b, p128_d), ("ropec", ropeC, ropec_d),
                       ("ropes", ropeS, ropes_d)):
        ld_cst[key] = em.op("gpsimd", lambda g, t=t, d_=d_:
                            g.dma_start(out=t[:], in_=d_), dma="gw")
    ld_ones = em.op("gpsimd", lambda g: g.memset(ones_t[:], 1.0))
    for key, t, d_ in (("identf", ident_f, ident_d), ("c", ctile, c_d)):
        ld_cst[key] = em.op("sync", lambda s, t=t, d_=d_:
                            s.dma_start(out=t[:], in_=d_), dma="sc0")
    ld_cst["clipx"] = em.op("sync", lambda s: s.dma_start(
        out=scal[0:1, 0:1], in_=clipx_d), dma="sc0")
    ld_cst["clipd"] = em.op("sync", lambda s: s.dma_start(
        out=scal[0:1, 1:2], in_=clipd_d), dma="sc0")
    NHALF = (6 * D) // WMODW
    gw_last = ld_cst["ropes"]
    for k in list(ld_w):
        ld_w[k] = gw_last
    for k in ("identb", "tri", "p128", "ropec", "ropes"):
        ld_cst[k] = gw_last
    sc_last = ld_cst["clipd"]
    for k in ("identf", "c", "clipx", "clipd"):
        ld_cst[k] = sc_last
    ld_wmb = [em.op("sync", lambda s: s.dma_start(
        out=wmb[0][:], in_=wmod_d[0:128, 0:WMODW]), dma="wmb0")]

    o_silu = em.op("scalar", lambda s: s.activation(
        silu_c[0:NB, :], ctile[0:NB, :], AF.Silu), waits=[ld_cst["c"]])
    o_sfm = None
    for c in range(C8):
        sl, w = pss.next()
        o_t = em.op("tensor", lambda t, c=c, sl=sl: t.transpose(
            psS[sl][:, 0:NB], silu_c[0:NB, c * 128:(c + 1) * 128],
            ident_f[0:NB, 0:NB]), waits=[o_silu, ld_cst["identf"]] + w)
        o_sfm = em.op("vector", lambda v, c=c, sl=sl: v.tensor_copy(
            sfm[:, c * NB:(c + 1) * NB], psS[sl][:, 0:NB]), waits=[o_t])
        pss.used(sl, o_sfm)
    o_sfmb = None

    JH = WMODW // 128
    o_mm_last = None
    mod_mms = []
    sl_u, w_u = psu.next()
    for li in range(C8 * NHALF):
        dc, half = li // NHALF, li % NHALF
        if li + 1 < C8 * NHALF:
            dc2, half2 = (li + 1) // NHALF, (li + 1) % NHALF
            ld_wmb.append(em.op(
                "sync", lambda s, dc2=dc2, half2=half2, li=li: s.dma_start(
                    out=wmb[(li + 1) % 2][:],
                    in_=wmod_d[dc2 * 128:(dc2 + 1) * 128,
                               half2 * WMODW:(half2 + 1) * WMODW]),
                waits=[mod_mms[-1]] if li >= 1 else [],
                dma=f"wmb{(li + 1) % 2}"))
        for j in range(JH):
            jj = half * JH + j
            col = (jj // 8) * 16 + (jj % 8) * NB
            o_mm_last = em.op("tensor", lambda t, li=li, j=j, col=col: t.matmul(
                psU[:, col:col + NB], wmb[li % 2][:, j * 128:(j + 1) * 128],
                sfm[:, (li // NHALF) * NB:(li // NHALF + 1) * NB],
                start=(li == 0 and j == 0),
                stop=(li == C8 * NHALF - 1 and j == JH - 1)),
                waits=[o_sfm, ld_wmb[li]] + w_u)
            w_u = []
        mod_mms.append(o_mm_last)
    acc_last = em.op("vector", lambda v: v.tensor_copy(
        modfm[:, 0:96], psU[:, 0:96]), waits=[o_mm_last])
    psu.used(0, acc_last)

    em.op("vector", lambda v: v.tensor_scalar(
        modfm[:, 16:32], modfm[:, 16:32], 1.0, None, ALU.add), waits=[acc_last])
    em.op("vector", lambda v: v.tensor_scalar(
        modfm[:, 64:80], modfm[:, 64:80], 1.0, None, ALU.add), waits=[acc_last])
    o_mod = None
    for base in (32, 80):
        o1 = em.op("vector", lambda v, base=base: v.tensor_scalar(
            rsc1[:, 0:16], modfm[:, base:base + 16], 0.0, None, ALU.is_ge),
            waits=[acc_last] + ([o_mod] if o_mod else []))
        o2 = em.op("vector", lambda v: v.tensor_scalar(
            rsc1[:, 0:16], rsc1[:, 0:16], 2.0, -1.0, ALU.mult, ALU.add),
            waits=[o1])
        o3 = em.op("vector", lambda v, base=base: v.tensor_tensor(
            rsc1[:, 16:32], modfm[:, base:base + 16], rsc1[:, 0:16], ALU.mult),
            waits=[o2])
        o4 = em.op("vector", lambda v: v.tensor_scalar(
            rsc1[:, 16:32], rsc1[:, 16:32], 0.25, None, ALU.max), waits=[o3])
        o_mod = em.op("vector", lambda v, base=base: v.tensor_tensor(
            modfm[:, base:base + 16], rsc1[:, 16:32], rsc1[:, 0:16], ALU.mult),
            waits=[o4])
    o_sfmb = em.op("vector", lambda v: v.tensor_copy(sfm_b[:], modfm[:, 0:16]),
                   waits=[o_mod])
    dump_sbuf("dbg_modfm", modfm[:], (128, 96), waits=[o_mod])

    bcast_chain = [None]

    def build_bcast(target, v_i, b, pstile, ring, scale_col=None, extra=()):
        outs = []
        o2 = bcast_chain[0]
        for c in range(C8):
            col = v_i * 16 + c * 2 + b
            o1 = em.op("vector", lambda vv, col=col: vv.tensor_scalar(
                rsc2[:, 0:128], ones_t[:, 0:128], modfm[:, col:col + 1], None,
                ALU.mult), waits=[o_mod, ld_ones, o2] + list(extra))
            sl, w = ring.next()
            o2 = em.op("tensor", lambda t, sl=sl: t.transpose(
                pstile[sl][:, 0:128] if isinstance(pstile, list)
                else pstile[:, 0:128], rsc2[:, 0:128], ident_f[:]),
                waits=[o1, ld_cst["identf"]] + w)
            src = (pstile[sl] if isinstance(pstile, list) else pstile)
            if scale_col is None:
                o3 = em.op("vector", lambda vv, c=c, src=src: vv.tensor_copy(
                    target[:, c * 128:(c + 1) * 128], src[:, 0:128]),
                    waits=[o2])
            else:
                o3 = em.op("vector", lambda vv, c=c, src=src: vv.tensor_scalar(
                    target[:, c * 128:(c + 1) * 128], src[:, 0:128],
                    sc[:, scale_col:scale_col + 1], None, ALU.mult),
                    waits=[o2] + list(extra))
            ring.used(sl, o3)
            outs.append(o3)
        bcast_chain[0] = o2
        return outs

    def ln_stats(src_tile, t_par, waits):
        sb0 = (t_par % 2) * 6
        o_sq = em.op("scalar", lambda s: s.activation(
            sqs[:], src_tile[:], AF.Square, accum_out=stat[:, sb0 + 1:sb0 + 2]),
            waits=waits)
        o_sum = em.op("vector", lambda v: v.tensor_reduce(
            stat[:, sb0:sb0 + 1], src_tile[:], axis=mybir.AxisListType.X,
            op=ALU.add), waits=waits)
        o_mean = em.op("vector", lambda v: v.tensor_scalar(
            stat[:, sb0 + 2:sb0 + 3], stat[:, sb0:sb0 + 1], 1.0 / D, None,
            ALU.mult), waits=[o_sum])
        o_msq = em.op("vector", lambda v: v.tensor_scalar(
            stat[:, sb0 + 3:sb0 + 4], stat[:, sb0 + 1:sb0 + 2], 1.0 / D, None,
            ALU.mult), waits=[o_sq])
        o_mm2 = em.op("vector", lambda v: v.tensor_tensor(
            stat[:, sb0 + 4:sb0 + 5], stat[:, sb0 + 2:sb0 + 3],
            stat[:, sb0 + 2:sb0 + 3], ALU.mult), waits=[o_mean])
        o_var = em.op("vector", lambda v: v.scalar_tensor_tensor(
            stat[:, sb0 + 3:sb0 + 4], stat[:, sb0 + 3:sb0 + 4], 1e-5,
            stat[:, sb0 + 4:sb0 + 5], ALU.add, ALU.subtract),
            waits=[o_msq, o_mm2])
        o_std = em.op("scalar", lambda s: s.activation(
            stat[:, sb0 + 4:sb0 + 5], stat[:, sb0 + 3:sb0 + 4], AF.Sqrt),
            waits=[o_var])
        o_rstd = em.op("vector", lambda v: v.reciprocal(
            stat[:, sb0 + 3:sb0 + 4], stat[:, sb0 + 4:sb0 + 5]), waits=[o_std])
        o_nb1 = em.op("vector", lambda v: v.tensor_tensor(
            stat[:, sb0 + 5:sb0 + 6], stat[:, sb0 + 2:sb0 + 3],
            stat[:, sb0 + 3:sb0 + 4], ALU.mult), waits=[o_rstd])
        o_nb = em.op("vector", lambda v: v.tensor_scalar(
            stat[:, sb0 + 5:sb0 + 6], stat[:, sb0 + 5:sb0 + 6], -1.0, None,
            ALU.mult), waits=[o_nb1])
        return sb0, o_rstd, o_nb, o_sq, o_sum

    o_x1_wr = []
    kg_gate = []
    vg_ones_tok = None
    prev_psT_evac = None
    o_attn_last = None
    acc_readers = []
    prev_bc = None

    for b in range(NB):
        o_h1_done = None
        for t in range(NT):
            xi, xw = xr_ring.next()
            o_ld = em.op("sync", lambda s, b=b, t=t, xi=xi: s.dma_start(
                out=xr[xi][:], in_=x_in[b, t * 128:(t + 1) * 128, :]),
                waits=xw, dma=f"xr{xi}")
            sb0, o_rstd, o_nb, o_sq, o_sum = ln_stats(
                xr[xi], t, [o_ld] + ([o_sfm] if (b == 0 and t == 0) else []))
            o_nrm = em.op("vector", lambda v, xi=xi, sb0=sb0: v.tensor_scalar(
                nrmb[:], xr[xi][:], stat[:, sb0 + 3:sb0 + 4],
                stat[:, sb0 + 5:sb0 + 6], ALU.mult, ALU.add),
                waits=[o_nb, prev_psT_evac] + kg_gate)
            tps = []
            for c in range(C8):
                tps.append(em.op("tensor", lambda t_, c=c: t_.transpose(
                    psT[:, c * 128:(c + 1) * 128],
                    nrmb[:, c * 128:(c + 1) * 128], ident_b[:]),
                    waits=[o_nrm, ld_cst["identb"], prev_psT_evac]))
            o_h1_done = em.op("vector", lambda v, t=t: v.tensor_copy(
                hbuf[:].rearrange("p (c s) -> p c s", s=S)[:, :, t * 128:(t + 1) * 128],
                psT[:].rearrange("p (c j) -> p c j", j=128)), waits=tps)
            prev_psT_evac = o_h1_done
            xr_ring.used(xi, o_sq, o_sum, o_nrm)
        if b == 0:
            dump_sbuf("dbg_h1fm", hbuf[:], (128, C8 * S), BF16,
                      waits=[o_h1_done])

        o_wp = None
        for (w_src, w_dst, width) in ((wdq_s, wdq_p, DL), (wdkv_s, wdkv_p, DL),
                                      (wkr_s, wkr_p, DHR)):
            for c in range(C8):
                col = 16 + c * 2 + b
                o_wp = em.op(
                    "vector",
                    lambda v, w_src=w_src, w_dst=w_dst, c=c, col=col, width=width:
                    v.tensor_scalar(
                        w_dst[:, c * width:(c + 1) * width],
                        w_src[:, c * width:(c + 1) * width],
                        modfm[:, col:col + 1], None, ALU.mult),
                    waits=[o_mod, ld_w["dq"], ld_w["dkv"], ld_w["kr"]] + kg_gate)
        for (w_src, dst, nl, width) in ((wdq_s, bias_cq, LC, DL),
                                        (wdkv_s, bias_ckv, LC, DL),
                                        (wkr_s, bias_kr, 1, DHR)):
            for l in range(nl):
                sl_u, w_u = psu.next()
                mpart = 128 if width == DL else DHR
                mms = []
                for c in range(C8):
                    sacol = c * 2 + b
                    mms.append(em.op(
                        "tensor",
                        lambda t, w_src=w_src, c=c, l=l, width=width,
                        mpart=mpart, sacol=sacol: t.matmul(
                            psU[0:mpart, 0:1],
                            w_src[:, c * width + l * 128:
                                  c * width + l * 128 + mpart],
                            sfm_b[:, sacol:sacol + 1],
                            start=(c == 0), stop=(c == C8 - 1)),
                        waits=[o_sfmb] + w_u))
                o_wp = em.op("vector",
                             lambda v, dst=dst, l=l, mpart=mpart: v.tensor_copy(
                                 dst[0:mpart, l:l + 1], psU[0:mpart, 0:1]),
                             waits=[mms[-1]])
                psu.used(0, o_wp)

        def proj_lat(w_p, bias_t, dst):
            last = None
            for l in range(LC):
                for q4 in range(4):
                    sl, w = pss.next()
                    mms = []
                    for c in range(C8):
                        mms.append(em.op(
                            "tensor",
                            lambda t, w_p=w_p, l=l, q4=q4, c=c, sl=sl: t.matmul(
                                psS[sl][:, 0:512],
                                w_p[:, c * DL + l * 128:c * DL + (l + 1) * 128],
                                hbuf[:, c * S + q4 * 512:c * S + (q4 + 1) * 512],
                                start=(c == 0), stop=(c == C8 - 1)),
                            waits=[o_h1_done, o_wp] + w))
                    last = em.op("vector", lambda v, dst=dst, l=l, q4=q4, sl=sl,
                                 bias_t=bias_t: v.tensor_scalar(
                        dst[:, l * S + q4 * 512:l * S + (q4 + 1) * 512],
                        psS[sl][:, 0:512], bias_t[:, l:l + 1], None, ALU.add),
                        waits=[mms[-1]])
                    pss.used(sl, last)
            return last

        o_cq = proj_lat(wdq_p, bias_cq, cq_s)
        o_ckv = proj_lat(wdkv_p, bias_ckv, ckv_s)
        if b == 0:
            dump_sbuf("dbg_cq", cq_s[:], (128, LC * S), BF16, waits=[o_cq])
            dump_sbuf("dbg_ckv", ckv_s[:], (128, LC * S), BF16, waits=[o_ckv])
            dump_sbuf("dbg_wdqp", wdq_p[:], (128, C8 * DL), BF16, waits=[o_wp])
            dump_sbuf("dbg_biascq", bias_cq[:], (128, LC), F32, waits=[o_wp])

        o_kr = None
        for q4 in range(4):
            sl, w = pss.next()
            mms = []
            for c in range(C8):
                mms.append(em.op("tensor", lambda t, c=c, q4=q4, sl=sl: t.matmul(
                    psS[sl][0:32, 0:512], wkr_p[:, c * DHR:(c + 1) * DHR],
                    hbuf[:, c * S + q4 * 512:c * S + (q4 + 1) * 512],
                    start=(c == 0), stop=(c == C8 - 1)),
                    waits=[o_h1_done, o_wp] + w))
            o_e = em.op("vector", lambda v, q4=q4, sl=sl: v.tensor_scalar(
                krt[64:96, q4 * 512:(q4 + 1) * 512], psS[sl][0:32, 0:512],
                bias_kr[0:32, 0:1], None, ALU.add),
                waits=[mms[-1]] + kg_gate)
            pss.used(sl, o_e)
            sl_u, w_u = psu.next()
            o_sw = em.op("tensor", lambda t, q4=q4: t.matmul(
                psU[64:96, 0:512], p128_b[64:96, 64:96],
                krt[64:96, q4 * 512:(q4 + 1) * 512], start=True, stop=True),
                waits=[o_e, ld_cst["p128"]] + w_u)
            o_r1 = em.op("vector", lambda v, q4=q4: v.tensor_tensor(
                rsc1[64:96, 0:512], krt[64:96, q4 * 512:(q4 + 1) * 512],
                ropeC[64:96, q4 * 512:(q4 + 1) * 512], ALU.mult),
                waits=[o_e, ld_cst["ropec"]])
            o_r2 = em.op("vector", lambda v, q4=q4: v.tensor_tensor(
                rsc2[64:96, 0:512], psU[64:96, 0:512],
                ropeS[64:96, q4 * 512:(q4 + 1) * 512], ALU.mult),
                waits=[o_sw, ld_cst["ropes"]])
            o_kr = em.op("vector", lambda v, q4=q4: v.tensor_tensor(
                krt[64:96, q4 * 512:(q4 + 1) * 512], rsc1[64:96, 0:512],
                rsc2[64:96, 0:512], ALU.add), waits=[o_r1, o_r2])
            psu.used(0, o_r2)

        ga_bc = build_bcast(garow, 2, b, psS, pss)

        attn_pend = []
        norm_box = [None]
        vg_ones_tok = em.op("vector", lambda v: v.memset(
            vg[:].rearrange("p (t h j) -> p t h j", h=4, j=65)[:, :, :, 64:65],
            1.0), waits=kg_gate)
        av_ops = []
        for g in range(4):
            kv_gate = kg_gate if g == 0 else list(av_ops)
            for pr in range(2):
                h0 = (g * 4 + 2 * pr)
                for q4 in range(4):
                    sl, w = pss.next()
                    mms = []
                    for l in range(LC):
                        mms.append(em.op(
                            "tensor", lambda t, h0=h0, l=l, q4=q4, sl=sl:
                            t.matmul(
                                psS[sl][:, 0:512],
                                wuk_s[:, l * D + h0 * DH:l * D + (h0 + 2) * DH],
                                ckv_s[:, l * S + q4 * 512:l * S + (q4 + 1) * 512],
                                start=(l == 0), stop=(l == LC - 1)),
                            waits=[o_ckv, ld_w["uk"]] + w))
                    hh_e, hh_o = 2 * pr, 2 * pr + 1
                    o_e1 = em.op("scalar", lambda s, hh_e=hh_e, q4=q4, sl=sl:
                                 s.activation(
                        kg[0:64, hh_e * S + q4 * 512:hh_e * S + (q4 + 1) * 512],
                        psS[sl][0:64, 0:512], AF.Copy),
                        waits=[mms[-1]] + kv_gate)
                    o_e2 = em.op("vector", lambda v, hh_o=hh_o, q4=q4, sl=sl:
                                 v.tensor_copy(
                        kg[0:64, hh_o * S + q4 * 512:hh_o * S + (q4 + 1) * 512],
                        psS[sl][64:128, 0:512]), waits=[mms[-1]] + kv_gate)
                    pss.used(sl, o_e1, o_e2)
            kr_cp = []
            for hh in range(4):
                kr_cp.append(em.op("vector", lambda v, hh=hh: v.tensor_copy(
                    kg[64:96, hh * S:(hh + 1) * S], krt[64:96, :]),
                    waits=[o_kr] + kv_gate))
            vg_evs = []
            for tt in range(NT):
                sl, w = pss.next()
                mms = []
                for l in range(LC):
                    mms.append(em.op("tensor",
                                     lambda t, g=g, tt=tt, l=l, sl=sl: t.matmul(
                        psS[sl][:, 0:256],
                        ckv_s[:, l * S + tt * 128:l * S + (tt + 1) * 128],
                        wuv_s[:, l * D + g * 256:l * D + (g + 1) * 256],
                        start=(l == 0), stop=(l == LC - 1)),
                        waits=[o_ckv, ld_w["uv"]] + w))
                o_ve = em.op("vector", lambda v, tt=tt, sl=sl: v.tensor_copy(
                    vg[:, tt * 260:(tt + 1) * 260].rearrange(
                        "p (h j) -> p h j", j=65)[:, :, 0:64],
                    psS[sl][:, 0:256].rearrange("p (h j) -> p h j", j=64)),
                    waits=[mms[-1]] + kv_gate)
                pss.used(sl, o_ve)
                vg_evs.append(o_ve)
            q_evs = []
            for pr in range(2):
                h0 = (g * 4 + 2 * pr)
                for q4 in range(4):
                    sl, w = pss.next()
                    mms = []
                    for l in range(LC):
                        mms.append(em.op(
                            "tensor", lambda t, h0=h0, l=l, q4=q4, sl=sl:
                            t.matmul(
                                psS[sl][:, 0:512],
                                wuq_s[:, l * D + h0 * DH:l * D + (h0 + 2) * DH],
                                cq_s[:, l * S + q4 * 512:l * S + (q4 + 1) * 512],
                                start=(l == 0), stop=(l == LC - 1)),
                            waits=[o_cq, ld_w["uq"]] + w))
                    hh_e, hh_o = 2 * pr, 2 * pr + 1
                    o_e1 = em.op("scalar", lambda s, hh_e=hh_e, q4=q4, sl=sl:
                                 s.activation(
                        qgt[0:64, hh_e * S + q4 * 512:hh_e * S + (q4 + 1) * 512],
                        psS[sl][0:64, 0:512], AF.Copy),
                        waits=[mms[-1]] + kv_gate)
                    o_e2 = em.op("vector", lambda v, hh_o=hh_o, q4=q4, sl=sl:
                                 v.tensor_copy(
                        qgt[0:64, hh_o * S + q4 * 512:hh_o * S + (q4 + 1) * 512],
                        psS[sl][64:128, 0:512]), waits=[mms[-1]] + kv_gate)
                    pss.used(sl, o_e1, o_e2)
                    q_evs += [o_e1, o_e2]
            qr_last = None
            for q4 in range(4):
                sl, w = pss.next()
                mms = []
                for l in range(LC):
                    mms.append(em.op("tensor",
                                     lambda t, g=g, l=l, q4=q4, sl=sl: t.matmul(
                        psS[sl][:, 0:512],
                        wqr_s[:, l * NH * DHR + g * 128:
                              l * NH * DHR + (g + 1) * 128],
                        cq_s[:, l * S + q4 * 512:l * S + (q4 + 1) * 512],
                        start=(l == 0), stop=(l == LC - 1)),
                        waits=[o_cq, ld_w["qr"]] + w))
                o_qe = em.op("vector", lambda v, q4=q4, sl=sl: v.tensor_copy(
                    qrq[:, q4 * 512:(q4 + 1) * 512], psS[sl][:, 0:512]),
                    waits=[mms[-1]] + kv_gate)
                pss.used(sl, o_qe)
                sl_u, w_u = psu.next()
                o_sw = em.op("tensor", lambda t, q4=q4: t.matmul(
                    psU[:, 0:512], p128_b[:], qrq[:, q4 * 512:(q4 + 1) * 512],
                    start=True, stop=True), waits=[o_qe] + w_u)
                o_r1 = em.op("vector", lambda v, q4=q4: v.tensor_tensor(
                    rsc1[:, 0:512], qrq[:, q4 * 512:(q4 + 1) * 512],
                    ropeC[:, q4 * 512:(q4 + 1) * 512], ALU.mult), waits=[o_qe])
                o_r2 = em.op("vector", lambda v, q4=q4: v.tensor_tensor(
                    rsc2[:, 0:512], psU[:, 0:512],
                    ropeS[:, q4 * 512:(q4 + 1) * 512], ALU.mult), waits=[o_sw])
                qr_last = em.op("vector", lambda v, q4=q4: v.tensor_tensor(
                    qrq[:, q4 * 512:(q4 + 1) * 512], rsc1[:, 0:512],
                    rsc2[:, 0:512], ALU.add), waits=[o_r1, o_r2])
                psu.used(0, o_r2)
            for hh in range(4):
                q_evs.append(em.op("vector", lambda v, hh=hh: v.tensor_copy(
                    qgt[64:96, hh * S:(hh + 1) * S],
                    qrq[hh * 32:(hh + 1) * 32, :]), waits=[qr_last] + kv_gate))
            o_build = em.op("scalar", lambda s: s.copy(
                scal[0:1, 15:16], scal[0:1, 15:16]),
                waits=[o_e1, o_e2, kr_cp[-1], vg_evs[-1]] + q_evs[-4:])
            if b == 0 and g == 0:
                dump_sbuf("dbg_kg", kg[:], (128, 4 * S), BF16, waits=[o_build])
                dump_sbuf("dbg_vg", vg[:], (128, NT * 4 * 65), BF16,
                          waits=[o_build])
                dump_sbuf("dbg_qg", qgt[:], (128, 4 * S), BF16, waits=[o_build])

            AVD = 3
            av_ops = []

            def pop_emit():
                nonlocal acc_readers, prev_bc, o_attn_last, av_ops
                it = attn_pend.pop(0)
                if it["first"] and norm_box[0] is not None:
                    flush_norm()
                ei, regs, gates = it["ei"], it["regs"], it["gates"]
                hh_, first, last = it["hh"], it["first"], it["last"]
                outs = []
                for ri, (k, col, n, qs) in enumerate(regs):
                    ww = list(gates) + [vg_ones_tok]
                    if first and ri == 0:
                        ww += acc_readers
                    outs.append(em.op(
                        "tensor",
                        lambda t, k=k, col=col, n=n, qs=qs, ei=ei, hh_=hh_,
                        st=(first and ri == 0),
                        sp=(last and ri == len(regs) - 1): t.matmul(
                            psACC[0:65, qs:512],
                            vg[:, k * 260 + hh_ * 65:k * 260 + (hh_ + 1) * 65],
                            ebuf[ei][:, col:col + n],
                            start=st, stop=sp),
                        waits=ww))
                eb_ring.used(ei, *outs)
                av_ops += outs
                if last:
                    norm_box[0] = (it["h"], it["q0"], outs[-1])

            def flush_norm():
                nonlocal acc_readers, prev_bc, o_attn_last
                h_, q0_, o_last_av = norm_box[0]
                norm_box[0] = None
                o_rec = em.op("vector", lambda v: v.reciprocal(
                    zrow, psACC[64:65, 0:512]), waits=[o_last_av, prev_bc])
                o_bc = em.op("tensor", lambda t: t.matmul(
                    psZ[0:64, 0:512], ones_t[64:65, 0:64],
                    zrow, start=True, stop=True),
                    waits=[o_rec, ld_ones, o_attn_last])
                o_an = em.op("scalar", lambda s: s.activation(
                    anorm[0:64, 0:512], psACC[0:64, 0:512], AF.Copy),
                    waits=[o_last_av, o_attn_last])
                o_nm = em.op("vector", lambda v, h_=h_, q0_=q0_: v.tensor_tensor(
                    hbuf[(h_ % 2) * 64:(h_ % 2) * 64 + 64,
                         (h_ // 2) * S + q0_:(h_ // 2) * S + q0_ + 512],
                    anorm[0:64, 0:512], psZ[0:64, 0:512], ALU.mult),
                    waits=[o_an, o_bc])
                acc_readers = [o_rec, o_an]
                prev_bc = o_bc
                o_attn_last = o_nm

            for hh in range(4):
                h = g * 4 + hh
                for qt in range(QT):
                    q0 = qt * 512
                    groups = []
                    nfull = 4 * qt
                    for kp in range(nfull // 2):
                        groups.append([(2 * kp, 0, 512, 0),
                                       (2 * kp + 1, 512, 512, 0)])
                    kd = nfull
                    groups.append([(kd, 0, 512, 0), (kd + 1, 512, 384, 128),
                                   (kd + 3, 896, 128, 384)])
                    groups.append([(kd + 2, 0, 256, 256)])
                    n_g = len(groups)
                    for gi in range(n_g):
                        regs = groups[gi]
                        sl, w = pss.next()
                        first_waits = [o_build] + w
                        mms = []
                        for (k, col, n, qs) in regs:
                            mms.append(em.op(
                                "tensor",
                                lambda t, k=k, col=col, n=n, qs=qs, sl=sl,
                                hh=hh, q0=q0: t.matmul(
                                    psS[sl][:, col:col + n],
                                    kg[0:96, hh * S + k * 128:
                                       hh * S + (k + 1) * 128],
                                    qgt[0:96, hh * S + q0 + qs:
                                        hh * S + q0 + 512],
                                    start=True, stop=True),
                                waits=first_waits))
                            first_waits = []
                        wid0 = regs[0][1]
                        tot = (regs[-1][1] + regs[-1][2]) - wid0
                        ei, ew = eb_ring.next()
                        o_exp = em.op("scalar", lambda s, sl=sl, ei=ei,
                                      wid0=wid0, tot=tot: s.activation(
                            ebuf[ei][:, wid0:wid0 + tot],
                            psS[sl][:, wid0:wid0 + tot], AF.Exp, scale=ISQ96),
                            waits=mms + ew)
                        pss.used(sl, o_exp)
                        o_clip = em.op("vector", lambda v, ei=ei, wid0=wid0,
                                       tot=tot: v.tensor_scalar(
                            ebuf[ei][:, wid0:wid0 + tot],
                            ebuf[ei][:, wid0:wid0 + tot],
                            EXP_HI, EXP_LO, ALU.min, ALU.max), waits=[o_exp])
                        gates = [o_clip]
                        diag = regs[-1][3] > 0 or regs[0][0] >= nfull
                        if diag:
                            for (k, col, n, qs) in regs:
                                gates.append(em.op(
                                    "gpsimd",
                                    lambda g_, ei=ei, col=col: g_.tensor_tensor(
                                        ebuf[ei][:, col:col + 128],
                                        ebuf[ei][:, col:col + 128],
                                        tri_b[:], ALU.mult),
                                    waits=[o_clip, ld_cst["tri"]]))
                        attn_pend.append(dict(ei=ei, regs=regs, gates=gates,
                                              hh=hh, h=h, q0=q0,
                                              first=(gi == 0),
                                              last=(gi == n_g - 1)))
                        while len(attn_pend) > AVD:
                            pop_emit()
            while attn_pend:
                pop_emit()
            flush_norm()
        if b == 0:
            dump_sbuf("dbg_attnout", hbuf[:], (128, C8 * S), BF16,
                      waits=[o_attn_last])

        for tt in range(NT):
            xi, xw = xr_ring.next()
            o_ldx = em.op("sync", lambda s, b=b, tt=tt, xi=xi: s.dma_start(
                out=xr[xi][:], in_=x_in[b, tt * 128:(tt + 1) * 128, :]),
                waits=xw, dma=f"xr{xi}")
            sl, slw = pss.next()
            mms = []
            for half in range(2):
                for c in range(C8):
                    mms.append(em.op(
                        "tensor", lambda t, tt=tt, half=half, c=c, sl=sl:
                        t.matmul(
                            psS[sl][:, half * 512:(half + 1) * 512],
                            hbuf[:, c * S + tt * 128:c * S + (tt + 1) * 128],
                            wo_s[:, c * D + half * 512:c * D + (half + 1) * 512],
                            start=(c == 0), stop=(c == C8 - 1)),
                        waits=[o_attn_last, ld_w["o"]] + slw))
            o_m = em.op("vector", lambda v, sl=sl: v.tensor_tensor(
                rsc1[:, 0:1024], psS[sl][:, 0:1024], garow[:], ALU.mult),
                waits=[mms[-1]] + ga_bc)
            pss.used(sl, o_m)
            si, sw = stg_ring.next()
            o_x1 = em.op("vector", lambda v, xi=xi, si=si: v.tensor_tensor(
                stg[si][:], rsc1[:, 0:1024], xr[xi][:], ALU.add),
                waits=[o_m, o_ldx] + sw)
            xr_ring.used(xi, o_x1)
            o_wr = em.op("scalar", lambda s, b=b, tt=tt, si=si: s.dma_start(
                out=x1scr[b, tt * 128:(tt + 1) * 128, :], in_=stg[si][:]),
                waits=[o_x1], dma=f"stg{si}")
            stg_ring.used(si, o_wr)
            o_x1_wr.append(o_wr)
        kg_gate = [mms[-1], o_attn_last]

    barrier = em.op("vector", lambda v: v.memset(amax1[:, 0:4], 0.0),
                    waits=[o_x1_wr[-1], o_attn_last, av_ops[-1]])

    stA.close()
    stF = ExitStack()
    stFa = ExitStack()
    wdT = sb("wdT", (128, IC * D), BF16, st=stF)
    wquant = sb("wquant", (128, 12), st=stF)
    cbrow = sb("cbrow", (128, D), st=stF)
    sbrow = sb("sbrow", (128, D), st=stF)
    qxfm = sb("qxfm", (128, C8 * NB * S), BF16, st=stFa)
    wgr = [sb(f"wgr{i}", (128, D), st=stFa) for i in range(2)]
    wur = [sb(f"wur{i}", (128, D), st=stFa) for i in range(2)]
    wgq = sb("wgq", (128, D), BF16, st=stFa)
    wuq2 = sb("wuq2", (128, D), BF16, st=stFa)
    wgT = [sb(f"wgT{i}", (128, D), BF16, st=stFa) for i in range(2)]
    wuT = [sb(f"wuT{i}", (128, D), BF16, st=stFa) for i in range(2)]
    wdr = sb("wdr", (128, II), st=stFa)
    wdq2 = sb("wdq2", (128, II), BF16, st=stFa)
    silu_t = [sb(f"silu{i}", (128, 512), st=stFa) for i in range(2)]
    atile = [sb(f"atile{i}", (128, 512), st=stFa) for i in range(3)]

    psBig = psum("psBig", (128, 2048), st=stF)
    psT2 = psum("psT2", (128, 1024), BF16, st=stF)
    psW = psum("psW", (128, 512), st=stF)
    psw = Ring(1)
    a_ring = Ring(3)

    o_h2_wr = []
    o_am_h2 = None
    o_pool_prev = None
    for b in range(NB):
        cb_bc = build_bcast(cbrow, 4, b, psW, psw,
                            extra=[barrier, o_pool_prev])
        sb_bc = build_bcast(sbrow, 3, b, psW, psw,
                            extra=[barrier, o_pool_prev])
        for t in range(NT):
            xi, xw = xr_ring.next()
            o_ld = em.op("sync", lambda s, b=b, t=t, xi=xi: s.dma_start(
                out=xr[xi][:], in_=x1scr[b, t * 128:(t + 1) * 128, :]),
                waits=xw + [o_x1_wr[b * NT + t], barrier], dma=f"xr{xi}")
            sb0, o_rstd, o_nb, o_sq, o_sum = ln_stats(xr[xi], t, [o_ld])
            o_nrm = em.op("vector", lambda v, xi=xi, sb0=sb0: v.tensor_scalar(
                rsc1[:, 0:1024], xr[xi][:], stat[:, sb0 + 3:sb0 + 4],
                stat[:, sb0 + 5:sb0 + 6], ALU.mult, ALU.add),
                waits=[o_nb, o_pool_prev])
            o_h2a = em.op("gpsimd", lambda g: g.tensor_tensor(
                rsc2[:, 0:1024], rsc1[:, 0:1024], cbrow[:], ALU.mult),
                waits=[o_nrm] + cb_bc)
            si, sw = stg_ring.next()
            o_h2 = em.op("gpsimd", lambda g, si=si: g.tensor_tensor(
                stg[si][:], rsc2[:, 0:1024], sbrow[:], ALU.add),
                waits=[o_h2a] + sb_bc + sw)
            o_pool_prev = o_h2
            o_wr = em.op("scalar", lambda s, b=b, t=t, si=si: s.dma_start(
                out=h2scr[b, t * 128:(t + 1) * 128, :], in_=stg[si][:]),
                waits=[o_h2], dma=f"stg{si}")
            o_am = em.op("vector", lambda v, si=si: v.tensor_reduce(
                amax1[:, 2:3], stg[si][:], axis=mybir.AxisListType.X,
                op=ALU.max, apply_absolute_value=True),
                waits=[o_h2, barrier])
            o_am_h2 = em.op("vector", lambda v: v.tensor_tensor(
                amax1[:, 0:1], amax1[:, 0:1], amax1[:, 2:3], ALU.max),
                waits=[o_am, o_am_h2])
            stg_ring.used(si, o_wr, o_am)
            xr_ring.used(xi, o_sq, o_sum, o_nrm)
            o_h2_wr.append(o_wr)

    def collective_scale(amax_col, cci, cco, clip_key, inv_col, sc_col, waits):
        sl_w, ww = psw.next()
        o_tp = em.op("tensor", lambda t: t.transpose(
            psW[0:1, 0:128], amax1[:, amax_col:amax_col + 1], ident_f[:]),
            waits=list(waits) + [ld_cst["identf"]] + ww)
        o_red = em.op("vector", lambda v: v.tensor_reduce(
            scal[0:1, 4:5], psW[0:1, 0:128], axis=mybir.AxisListType.X,
            op=ALU.max), waits=[o_tp])
        psw.used(0, o_red)
        o_out = em.op("gpsimd", lambda g: g.dma_start(out=cci,
                                                      in_=scal[0:1, 4:5]),
                      waits=[o_red], dma="cio" + clip_key)
        o_cc = em.op("gpsimd", lambda g: g.collective_compute(
            "AllReduce", ALU.max, replica_groups=[list(range(n_cores))],
            ins=[cci], outs=[cco]), waits=[o_out],
            inc=("ccs" + clip_key, 1))
        o_in = em.op("gpsimd", lambda g: g.dma_start(out=scal[0:1, 5:6],
                                                     in_=cco),
                     waits=[o_cc], dma="cio" + clip_key)
        o_sg = em.op("scalar", lambda s: s.activation(
            scal[0:1, 6:7], scal[0:1, 0:1] if clip_key == "x"
            else scal[0:1, 1:2], AF.Sigmoid),
            waits=[ld_cst["clipx"], ld_cst["clipd"], o_in])
        o_m1 = em.op("vector", lambda v: v.tensor_tensor(
            scal[0:1, 7:8], scal[0:1, 5:6], scal[0:1, 6:7], ALU.mult),
            waits=[o_sg, o_in])
        o_m2 = em.op("vector", lambda v: v.tensor_scalar(
            scal[0:1, 8:9], scal[0:1, 7:8], 1.0 / 127.0, None, ALU.mult),
            waits=[o_m1])
        o_rc = em.op("vector", lambda v: v.reciprocal(
            scal[0:1, 9:10], scal[0:1, 8:9]), waits=[o_m2])
        outs = []
        for (src, dstc) in ((9, inv_col), (8, sc_col)):
            sl_w, ww = psw.next()
            o_b1 = em.op("tensor", lambda t, src=src: t.matmul(
                psW[:, 0:1], ones_t[0:1, 0:128], scal[0:1, src:src + 1],
                start=True, stop=True), waits=[o_rc, ld_ones] + ww)
            o_cp = em.op("vector", lambda v, dstc=dstc: v.tensor_copy(
                sc[:, dstc:dstc + 1], psW[:, 0:1]), waits=[o_b1])
            psw.used(0, o_cp)
            outs.append(o_cp)
        return outs

    sx_ops = collective_scale(0, cc1i, cc1o, "x", 0, 1, [o_am_h2])

    prev_psT2 = None
    o_qx_last = None
    o_q3 = None
    for b in range(NB):
        for t in range(NT):
            xi, xw = xr_ring.next()
            o_ld = em.op("sync", lambda s, b=b, t=t, xi=xi: s.dma_start(
                out=xr[xi][:], in_=h2scr[b, t * 128:(t + 1) * 128, :]),
                waits=xw + [o_h2_wr[b * NT + t]], dma=f"xr{xi}")
            o_q1 = em.op("scalar", lambda sx_, xi=xi: sx_.activation(
                rsc1[:, 0:1024], xr[xi][:], AF.Copy,
                scale=sc[:, 0:1], bias=MAGIC),
                waits=[o_ld, o_q3] + sx_ops)
            o_q2 = em.op("vector", lambda v: v.tensor_scalar(
                rsc1[:, 0:1024], rsc1[:, 0:1024], MAGIC, 127.0, ALU.subtract,
                ALU.min), waits=[o_q1])
            o_q3 = em.op("vector", lambda v: v.tensor_scalar(
                rsc2[:, 0:1024], rsc1[:, 0:1024], -128.0, None, ALU.max),
                waits=[o_q2])
            o_cb = em.op("vector", lambda v: v.tensor_copy(
                wgq[:, 0:1024], rsc2[:, 0:1024]), waits=[o_q3, barrier])
            tps = []
            for c in range(C8):
                tps.append(em.op("tensor", lambda t_, c=c: t_.transpose(
                    psT2[:, c * 128:(c + 1) * 128],
                    wgq[:, c * 128:(c + 1) * 128], ident_b[:]),
                    waits=[o_cb, prev_psT2]))
            gcol = b * S + t * 128
            o_qx_last = em.op("vector", lambda v, gcol=gcol: v.tensor_copy(
                qxfm[:].rearrange("p (c s) -> p c s",
                                  s=NB * S)[:, :, gcol:gcol + 128],
                psT2[:].rearrange("p (c j) -> p c j", j=128)), waits=tps)
            prev_psT2 = o_qx_last
            xr_ring.used(xi, o_q1)
    dump_sbuf("dbg_qxfm", qxfm[:], (128, C8 * NB * S), BF16,
              waits=[o_qx_last])

    def quant_weight(dst_bf, src_f32, width, q0_, waits):
        o_r = em.op("vector", lambda v: v.tensor_reduce(
            wquant[:, q0_:q0_ + 1], src_f32[:, 0:width],
            axis=mybir.AxisListType.X, op=ALU.max, apply_absolute_value=True),
            waits=waits)
        o_s = em.op("vector", lambda v: v.tensor_scalar(
            wquant[:, q0_ + 1:q0_ + 2], wquant[:, q0_:q0_ + 1],
            1.0 / 127.0, None, ALU.mult), waits=[o_r])
        o_i = em.op("vector", lambda v: v.reciprocal(
            wquant[:, q0_ + 2:q0_ + 3], wquant[:, q0_ + 1:q0_ + 2]),
            waits=[o_s])
        o_1 = em.op("vector", lambda v: v.tensor_scalar(
            src_f32[:, 0:width], src_f32[:, 0:width],
            wquant[:, q0_ + 2:q0_ + 3], MAGIC, ALU.mult, ALU.add),
            waits=[o_i])
        o_2 = em.op("vector", lambda v: v.tensor_scalar(
            src_f32[:, 0:width], src_f32[:, 0:width], MAGIC, 127.0,
            ALU.subtract, ALU.min), waits=[o_1])
        o_3 = em.op("vector", lambda v: v.tensor_scalar(
            dst_bf[:, 0:width], src_f32[:, 0:width], -128.0,
            wquant[:, q0_ + 1:q0_ + 2], ALU.max, ALU.mult), waits=[o_2])
        return o_3

    a_wr = {}
    o_am_a = None
    gu_read = {0: [barrier], 1: [barrier]}
    slot_g = {}
    slot_u = {}
    wd_state = {"gate": [barrier], "last": None}

    def emit_wd_chunk(hc):
        nonlocal prev_psT2
        o_ldw = em.op("sync", lambda s, hc=hc: s.dma_start(
            out=wdr[:], in_=wd_d[hc * 128:(hc + 1) * 128, :]),
            waits=wd_state["gate"], dma="wdr")
        o_q = quant_weight(wdq2, wdr, II, 6, [o_ldw])
        wd_state["gate"] = [o_q]
        for icc in range(IC):
            o_tp = em.op("tensor", lambda t, icc=icc: t.transpose(
                psT2[:, (icc % 8) * 128:(icc % 8) * 128 + 128],
                wdq2[:, icc * 128:(icc + 1) * 128], ident_b[:]),
                waits=[o_q, prev_psT2])
            o_e = em.op("vector", lambda v, hc=hc, icc=icc: v.tensor_copy(
                wdT[:, icc * D + hc * 128:icc * D + (hc + 1) * 128],
                psT2[:, (icc % 8) * 128:(icc % 8) * 128 + 128]),
                waits=[o_tp])
            prev_psT2 = o_e
            wd_state["gate"].append(o_e)
            wd_state["last"] = o_e

    for ic in range(IC):
        buf = ic % 2
        if False and ic % 3 == 2 and ic // 3 < C8:
            emit_wd_chunk(ic // 3)
        o_lg = em.op("sync", lambda s, ic=ic, buf=buf: s.dma_start(
            out=wgr[buf][:], in_=wg_d[ic * 128:(ic + 1) * 128, :]),
            waits=gu_read[buf], dma=f"wg{buf}")
        o_lu = em.op("sync", lambda s, ic=ic, buf=buf: s.dma_start(
            out=wur[buf][:], in_=wu_d[ic * 128:(ic + 1) * 128, :]),
            waits=gu_read[buf], dma=f"wu{buf}")
        gu_read[buf] = []
        o_qg_ = quant_weight(wgq, wgr[buf], D, 0, [o_lg, o_qx_last, barrier])
        tps_g = []
        for c in range(C8):
            tps_g.append(em.op("tensor", lambda t, c=c: t.transpose(
                psT2[:, c * 128:(c + 1) * 128], wgq[:, c * 128:(c + 1) * 128],
                ident_b[:]), waits=[o_qg_, prev_psT2]))
        o_gT = em.op("vector", lambda v, buf=buf: v.tensor_copy(
            wgT[buf][:], psT2[:]), waits=tps_g)
        prev_psT2 = o_gT
        o_qu_ = quant_weight(wuq2, wur[buf], D, 3, [o_lu])
        tps_u = []
        for c in range(C8):
            tps_u.append(em.op("tensor", lambda t, c=c: t.transpose(
                psT2[:, c * 128:(c + 1) * 128], wuq2[:, c * 128:(c + 1) * 128],
                ident_b[:]), waits=[o_qu_, prev_psT2]))
        o_uT = em.op("vector", lambda v, buf=buf: v.tensor_copy(
            wuT[buf][:], psT2[:]), waits=tps_u)
        prev_psT2 = o_uT
        for t8 in range(NB * QT):
            slg = t8 % 2
            cg = slg * 512
            cu = 1024 + slg * 512
            mm_g, mm_u = [], []
            for c in range(C8):
                mm_g.append(em.op("tensor",
                                  lambda t, c=c, t8=t8, cg=cg, buf=buf:
                                  t.matmul(
                    psBig[:, cg:cg + 512], wgT[buf][:, c * 128:(c + 1) * 128],
                    qxfm[:, c * NB * S + t8 * 512:c * NB * S + (t8 + 1) * 512],
                    start=(c == 0), stop=(c == C8 - 1)),
                    waits=[o_gT, o_qx_last] + slot_g.get(slg, [])))
            for c in range(C8):
                mm_u.append(em.op("tensor",
                                  lambda t, c=c, t8=t8, cu=cu, buf=buf:
                                  t.matmul(
                    psBig[:, cu:cu + 512], wuT[buf][:, c * 128:(c + 1) * 128],
                    qxfm[:, c * NB * S + t8 * 512:c * NB * S + (t8 + 1) * 512],
                    start=(c == 0), stop=(c == C8 - 1)),
                    waits=[o_uT] + slot_u.get(slg, [])))
            o_si = em.op("scalar", lambda s, slg=slg, cg=cg: s.activation(
                silu_t[slg][:], psBig[:, cg:cg + 512], AF.Silu,
                scale=sc[:, 1:2]),
                waits=[mm_g[-1]] + sx_ops + slot_u.get(slg, []))
            ai, aw = a_ring.next()
            o_a = em.op("vector", lambda v, slg=slg, cu=cu, ai=ai:
                        v.scalar_tensor_tensor(
                atile[ai][:], psBig[:, cu:cu + 512], sc[:, 1:2],
                silu_t[slg][:], ALU.mult, ALU.mult),
                waits=[mm_u[-1], o_si] + aw + sx_ops)
            slot_g[slg] = [o_si]
            slot_u[slg] = [o_a]
            o_am = em.op("vector", lambda v, ai=ai: v.tensor_reduce(
                amax1[:, 3:4], atile[ai][:], axis=mybir.AxisListType.X,
                op=ALU.max, apply_absolute_value=True), waits=[o_a, barrier])
            o_am_a = em.op("vector", lambda v: v.tensor_tensor(
                amax1[:, 1:2], amax1[:, 1:2], amax1[:, 3:4], ALU.max),
                waits=[o_am, o_am_a])
            o_awr = em.op("scalar", lambda s, ic=ic, t8=t8, ai=ai: s.dma_start(
                out=ascr[ic, t8], in_=atile[ai][:]), waits=[o_a],
                dma=f"at{ai}")
            a_ring.used(ai, o_awr, o_am)
            a_wr[(ic, t8)] = o_awr
            gu_read[buf] += [mm_g[-1], mm_u[-1]]

    for hc in range(0, C8):
        emit_wd_chunk(hc)
    o_wdT = wd_state["last"]

    sd_ops = collective_scale(1, cc2i, cc2o, "d", 2, 3, [o_am_a])
    dump_sbuf("dbg_sc", sc[:], (128, 8), F32, waits=sx_ops + sd_ops)
    dump_sbuf("dbg_amax", amax1[:], (128, 4), F32, waits=sx_ops + sd_ops)
    dump_sbuf("dbg_scal", scal[:], (1, 16), F32, waits=sx_ops + sd_ops)

    stFa.close()
    stF2 = ExitStack()
    qh_sb = [sb(f"qh_sb{i}", (128, IC * 512), BF16, st=stF2) for i in range(2)]
    arin = [sb(f"arin{i}", (128, 512), st=stF2) for i in range(4)]
    qsc = [sb(f"qsc{i}", (128, 512), st=stF2) for i in range(2)]
    ar_ring = Ring(4)
    y_wr = []
    qh_gate = {0: [], 1: []}
    psd_gate = {0: [], 1: []}
    qsc_gate = {0: None, 1: None}
    gb_built = {}
    for t8 in range(NB * QT):
        b = t8 // QT
        if b not in gb_built:
            gb_built[b] = build_bcast(garow, 5, b, psW, psw, scale_col=3,
                                      extra=sd_ops + [barrier])
        qb = (t8 // 1) % 2
        qh_done = None
        for icc in range(IC):
            ai, aw = ar_ring.next()
            o_la = em.op("sync", lambda s, icc=icc, t8=t8, ai=ai: s.dma_start(
                out=arin[ai][:], in_=ascr[icc, t8]),
                waits=aw + [a_wr[(icc, t8)]], dma=f"ar{ai}")
            qi = icc % 2
            o_1 = em.op("scalar", lambda sx_, ai=ai, qi=qi: sx_.activation(
                qsc[qi][:], arin[ai][:], AF.Copy,
                scale=sc[:, 2:3], bias=MAGIC),
                waits=[o_la, qsc_gate[qi]] + sd_ops)
            o_2 = em.op("vector", lambda v, qi=qi: v.tensor_scalar(
                qsc[qi][:], qsc[qi][:], MAGIC, 127.0, ALU.subtract,
                ALU.min), waits=[o_1])
            qh_done = em.op("vector", lambda v, icc=icc, qb=qb, qi=qi:
                            v.tensor_scalar(
                qh_sb[qb][:, icc * 512:(icc + 1) * 512], qsc[qi][:],
                -128.0, None, ALU.max), waits=[o_2] + qh_gate[qb])
            qsc_gate[qi] = qh_done
            ar_ring.used(ai, o_1)
        qh_gate[qb] = []
        for tk in range(4):
            sl = tk % 2
            c0 = sl * 1024
            gu_war = []
            if t8 == 0 and tk == 0:
                gu_war = slot_g.get(0, []) + slot_g.get(1, [])
            elif t8 == 0 and tk == 1:
                gu_war = slot_u.get(0, []) + slot_u.get(1, [])
            mms = []
            for half in range(2):
                for icc in range(IC):
                    mms.append(em.op(
                        "tensor",
                        lambda t, icc=icc, tk=tk, half=half, c0=c0, qb=qb:
                        t.matmul(
                            psBig[:, c0 + half * 512:c0 + (half + 1) * 512],
                            qh_sb[qb][:, icc * 512 + tk * 128:
                                      icc * 512 + (tk + 1) * 128],
                            wdT[:, icc * D + half * 512:
                                icc * D + (half + 1) * 512],
                            start=(icc == 0), stop=(icc == IC - 1)),
                        waits=[o_wdT, qh_done] + psd_gate[sl] + gu_war))
            qh_gate[qb].append(mms[-1])
            tt = (t8 % QT) * 4 + tk
            xi, xw = xr_ring.next()
            o_ldx = em.op("sync", lambda s, b=b, tt=tt, xi=xi: s.dma_start(
                out=xr[xi][:], in_=x1scr[b, tt * 128:(tt + 1) * 128, :]),
                waits=xw + [o_x1_wr[b * NT + tt]], dma=f"xr{xi}")
            o_m = em.op("vector", lambda v, c0=c0: v.tensor_tensor(
                rsc1[:, 0:1024], psBig[:, c0:c0 + 1024], garow[:], ALU.mult),
                waits=[mms[-1]] + gb_built[b])
            psd_gate[sl] = [o_m]
            si, sw = stg_ring.next()
            o_y = em.op("vector", lambda v, xi=xi, si=si: v.tensor_tensor(
                stg[si][:], rsc1[:, 0:1024], xr[xi][:], ALU.add),
                waits=[o_m, o_ldx] + sw)
            xr_ring.used(xi, o_y)
            o_wry = em.op("scalar", lambda s, b=b, tt=tt, si=si: s.dma_start(
                out=y_out[b, tt * 128:(tt + 1) * 128, :], in_=stg[si][:]),
                waits=[o_y], dma=f"stg{si}")
            stg_ring.used(si, o_wry)
            y_wr.append(o_wry)

    em.op("sync", lambda s: s.nop(), waits=y_wr + dump_ops)

    with nc.Block() as block:
        semstack = em.finalize(nc, block)
    semstack.close()
    stF2.close()
    stF.close()
    alloc.close()
    return nc


NAME_MAP = {
    "w_dq": "L0_attn_W_DQ", "w_uq": "L0_attn_W_UQ", "w_qr": "L0_attn_W_QR",
    "w_dkv": "L0_attn_W_DKV", "w_uk": "L0_attn_W_UK", "w_uv": "L0_attn_W_UV",
    "w_kr": "L0_attn_W_KR", "w_o": "L0_attn_W_O", "wg": "L0_ffn_w_gate",
    "wu": "L0_ffn_w_up", "wd": "L0_ffn_w_down", "wmod": "L0_wmod",
}


def kernel(**inputs):
    consts = host_consts()
    nc = build_program(dump=DUMP)
    x = np.ascontiguousarray(np.asarray(inputs["L0_attn_x"], dtype=np.float32))
    c = np.ascontiguousarray(np.asarray(inputs["L0_c"], dtype=np.float32))
    base = {k: np.ascontiguousarray(np.asarray(inputs[v], dtype=np.float32))
            for k, v in NAME_MAP.items()}
    base["clipx"] = np.asarray(inputs["L0_ffn_clip_x"],
                               dtype=np.float32).reshape(1, 1)
    base["clipd"] = np.asarray(inputs["L0_ffn_clip_down"],
                               dtype=np.float32).reshape(1, 1)
    base.update(consts)
    in_maps = []
    for k in range(NCORE):
        m = dict(base)
        m["x"] = np.ascontiguousarray(x[2 * k:2 * k + 2])
        m["cvec"] = np.ascontiguousarray(c[2 * k:2 * k + 2])
        in_maps.append(m)
    res = run_bass_kernel_spmd(nc, in_maps, core_ids=list(range(NCORE)))
    out = np.concatenate([np.asarray(res.results[k]["y"])
                          for k in range(NCORE)], axis=0)
    if DUMP:
        kernel.last_results = res
    return out


# version 08
# speedup vs baseline: 1.0810x; 1.0213x over previous
import math
from contextlib import ExitStack

import numpy as np
import concourse.bass as bass
import concourse.mybir as mybir
from concourse.bass_utils import run_bass_kernel_spmd

F32 = mybir.dt.float32
BF16 = mybir.dt.bfloat16
AF = mybir.ActivationFunctionType
ALU = mybir.AluOpType

MAGIC = 12582912.0
ISQ96 = float(np.float32(1.0) / np.sqrt(np.float32(96.0)))
EXP_HI = float(np.exp(80.0 / math.sqrt(96.0)))
EXP_LO = float(np.exp(-80.0 / math.sqrt(96.0)))

D, S, NB, NH, DH, DHR, DL, II = 1024, 2048, 2, 16, 64, 32, 256, 2816
C8, NT, QT, IC, LC = D // 128, S // 128, S // 512, II // 128, DL // 128
NCORE = 8

DUMP = ()


class Op:
    __slots__ = ("eng", "sem", "fn", "waits", "marked", "inc_n", "cum")

    def __init__(self, eng, sem, fn, waits, inc_n):
        self.eng, self.sem, self.fn = eng, sem, fn
        self.waits = waits
        self.marked = False
        self.inc_n = inc_n
        self.cum = None


class Em:
    ENGS = ("tensor", "vector", "scalar", "gpsimd", "sync")

    def __init__(self):
        self.ops = {e: [] for e in self.ENGS}

    def op(self, eng, fn, waits=(), dma=None, inc=None):
        if dma:
            sem = "q_" + dma
            inc_n = 16
        else:
            sem = {"tensor": "pe", "vector": "dve", "scalar": "act",
                   "gpsimd": "pool", "sync": "sse"}[eng]
            inc_n = 1
        if inc is not None:
            sem, inc_n = inc
        o = Op(eng, sem, fn, [w for w in waits if w is not None], inc_n)
        if dma:
            o.marked = True
        self.ops[eng].append(o)
        for w in o.waits:
            w.marked = True
        return o

    def finalize(self, nc, block):
        cnt = {}
        for e in self.ENGS:
            for o in self.ops[e]:
                if o.marked:
                    cnt[o.sem] = cnt.get(o.sem, 0) + o.inc_n
                    o.cum = (o.sem, cnt[o.sem])
        stack = ExitStack()
        sems = {name: stack.enter_context(nc.semaphore("sem_" + name))
                for name in cnt}
        for e in self.ENGS:
            ops = self.ops[e]
            if not ops:
                continue

            def body(eng, ops=ops, e=e):
                waited = {}
                for o in ops:
                    for w in o.waits:
                        if w.cum is None:
                            continue
                        if (w.eng == e and e == "tensor"
                                and o.eng == "tensor" and w.inc_n == 1
                                and o.inc_n == 1):
                            continue
                        sname, v = w.cum
                        if waited.get(sname, 0) >= v:
                            continue
                        waited[sname] = v
                        eng.wait_ge(sems[sname], v)
                    instr = o.fn(eng)
                    if o.marked:
                        instr.then_inc(sems[o.cum[0]], o.inc_n)
            getattr(block, e)(body)
        return stack


class Ring:
    def __init__(self, n):
        self.n = n
        self.slots = [[] for _ in range(n)]
        self.k = 0

    def next(self):
        i = self.k % self.n
        self.k += 1
        w = self.slots[i]
        self.slots[i] = []
        return i, w

    def used(self, i, *ops):
        self.slots[i].extend(o for o in ops if o is not None)


def host_consts():
    ident = np.eye(128, dtype=np.float32)
    tri = (np.arange(128)[None, :] >= np.arange(128)[:, None]).astype(np.float32)
    p = np.zeros((128, 128), dtype=np.float32)
    for i in range(128):
        p[i, (i // 32) * 32 + ((i % 32) ^ 1)] = 1.0
    d_idx = np.arange(32)
    inv = 10000.0 ** (-2.0 * (d_idx // 2).astype(np.float64) / 32.0)
    ang = np.arange(S, dtype=np.float64)[None, :] * inv[:, None]
    sgn = np.where(d_idx % 2 == 0, -1.0, 1.0)
    ropeC = np.tile(np.cos(ang).astype(np.float32), (4, 1))
    ropeS = np.tile((sgn[:, None] * np.sin(ang)).astype(np.float32), (4, 1))
    return dict(ident=ident, tri=tri, p128=p, ropec=ropeC, ropes=ropeS)


def build_program(dump=(), n_cores=NCORE):
    nc = bass.Bass("TRN2", target_bir_lowering=False)
    em = Em()

    def din(name, shape):
        return nc.dram_tensor(name, list(shape), F32, kind="ExternalInput").ap()

    x_in = din("x", (NB, S, D))
    w_dq, w_dkv = din("w_dq", (D, DL)), din("w_dkv", (D, DL))
    w_uq, w_qr = din("w_uq", (DL, D)), din("w_qr", (DL, NH * DHR))
    w_uk, w_uv = din("w_uk", (DL, D)), din("w_uv", (DL, D))
    w_kr, w_o = din("w_kr", (D, DHR)), din("w_o", (D, D))
    wg_d, wu_d, wd_d = din("wg", (II, D)), din("wu", (II, D)), din("wd", (D, II))
    wmod_d = din("wmod", (D, 6 * D))
    c_d = din("cvec", (NB, D))
    clipx_d, clipd_d = din("clipx", (1, 1)), din("clipd", (1, 1))
    ident_d, tri_d = din("ident", (128, 128)), din("tri", (128, 128))
    p128_d = din("p128", (128, 128))
    ropec_d, ropes_d = din("ropec", (128, S)), din("ropes", (128, S))

    y_out = nc.dram_tensor("y", [NB, S, D], F32, kind="ExternalOutput").ap()

    def dint(name, shape):
        kind = "ExternalOutput" if name in dump else "Internal"
        return nc.dram_tensor(name, list(shape), F32, kind=kind).ap()

    x1scr = dint("x1scr", (NB, S, D))
    h2scr = dint("h2scr", (NB, S, D))
    ascr = dint("ascr", (IC, NB * QT, 128, 512))
    cc1i, cc1o = dint("cc1i", (1, 1)), dint("cc1o", (1, 1))
    cc2i, cc2o = dint("cc2i", (1, 1)), dint("cc2o", (1, 1))

    dump_ops = []

    alloc = ExitStack()

    def sb(name, shape, dt=F32, st=None):
        return (st or alloc).enter_context(nc.sbuf_tensor(name, list(shape), dt))

    def psum(name, shape, dt=F32, st=None):
        return (st or alloc).enter_context(nc.psum_tensor(name, list(shape), dt))

    def dump_sbuf(name, ap, shape, dtype=F32, waits=()):
        if name not in dump:
            return
        d = nc.dram_tensor(name, list(shape), dtype, kind="ExternalOutput").ap()
        dump_ops.append(em.op("gpsimd", lambda g: g.dma_start(out=d, in_=ap),
                              waits=list(waits), dma="dbg"))

    ident_f = sb("ident_f", (128, 128))
    ident_b = sb("ident_b", (128, 128), BF16)
    ones_t = sb("ones_t", (128, 128))
    sfm = sb("sfm", (128, C8 * NB))
    sfm_b = sb("sfm_b", (128, C8 * NB), BF16)
    modfm = sb("modfm", (128, 96))
    sc = sb("sc", (128, 8))
    scal = sb("scal", (1, 16))
    xr = [sb(f"xr{i}", (128, D)) for i in range(2)]
    stg = [sb(f"stg{i}", (128, D)) for i in range(2)]
    sqs = sb("sqs", (128, D))
    stat = sb("stat", (128, 12))
    amax1 = sb("amax1", (128, 4))
    garow = sb("garow", (128, D))
    rsc1 = sb("rsc1", (128, 1024))
    rsc2 = sb("rsc2", (128, 1024))
    ctile = rsc1[0:NB, :]
    silu_c = sqs[0:NB, :]

    stA = ExitStack()
    tri_b = sb("tri_b", (128, 128), BF16, st=stA)
    p128_b = sb("p128_b", (128, 128), BF16, st=stA)
    ropeC = sb("ropeC", (128, S), BF16, st=stA)
    ropeS = sb("ropeS", (128, S), BF16, st=stA)
    WMODW = 1024
    wmb = [sb(f"wmb{i}", (128, WMODW), st=stA) for i in range(2)]
    wdq_s = sb("wdq_s", (128, C8 * DL), BF16, st=stA)
    wdkv_s = sb("wdkv_s", (128, C8 * DL), BF16, st=stA)
    wkr_s = sb("wkr_s", (128, C8 * DHR), BF16, st=stA)
    wuq_s = sb("wuq_s", (128, LC * D), BF16, st=stA)
    wqr_s = sb("wqr_s", (128, LC * NH * DHR), BF16, st=stA)
    wuk_s = sb("wuk_s", (128, LC * D), BF16, st=stA)
    wuv_s = sb("wuv_s", (128, LC * D), BF16, st=stA)
    wo_s = sb("wo_s", (128, C8 * D), BF16, st=stA)
    wdq_p = sb("wdq_p", (128, C8 * DL), BF16, st=stA)
    wdkv_p = sb("wdkv_p", (128, C8 * DL), BF16, st=stA)
    wkr_p = sb("wkr_p", (128, C8 * DHR), BF16, st=stA)
    bias_cq = sb("bias_cq", (128, LC), st=stA)
    bias_ckv = sb("bias_ckv", (128, LC), st=stA)
    bias_kr = sb("bias_kr", (128, 1), st=stA)
    hbuf = sb("hbuf", (128, C8 * S), BF16, st=stA)
    nrmb = sb("nrmb", (128, D), BF16, st=stA)
    cq_s = sb("cq_s", (128, LC * S), BF16, st=stA)
    ckv_s = sb("ckv_s", (128, LC * S), BF16, st=stA)
    krt = sb("krt", (128, S), BF16, st=stA)
    kg = sb("kg", (128, 4 * S), BF16, st=stA)
    vg = sb("vg", (128, NT * 4 * 65), BF16, st=stA)
    qgt = sb("qgt", (128, 4 * S), BF16, st=stA)
    qrq = sb("qrq", (128, S), BF16, st=stA)
    ebuf = [sb(f"ebuf{i}", (128, 1024), BF16, st=stA) for i in range(4)]
    anorm = sb("anorm", (128, 512), st=stA)
    zrow = anorm[64:65, 0:512]

    psS = [psum("psS_A", (128, 1024), st=stA), psum("psS_B", (128, 1024), st=stA)]
    psACCs = [psum("psACC", (128, 512), st=stA),
              psum("psACC2", (128, 512), st=stA)]
    psU = psum("psU", (128, 512), st=stA)
    psT = psum("psT", (128, 1024), BF16, st=stA)

    pss = Ring(2)
    psu = Ring(1)
    xr_ring = Ring(2)
    stg_ring = Ring(2)
    eb_ring = Ring(4)

    ld_w = {}
    for key, w, dst, width in (("dq", w_dq, wdq_s, DL), ("dkv", w_dkv, wdkv_s, DL),
                               ("kr", w_kr, wkr_s, DHR), ("o", w_o, wo_s, D)):
        ld_w[key] = em.op("gpsimd", lambda g, w=w, dst=dst, width=width:
                          g.dma_start(
            out=dst[:].rearrange("p (c j) -> p c j", j=width),
            in_=w.rearrange("(c p) j -> p c j", p=128)), dma="gw")
    for key, w, dst, width in (("uq", w_uq, wuq_s, D),
                               ("qr", w_qr, wqr_s, NH * DHR),
                               ("uk", w_uk, wuk_s, D), ("uv", w_uv, wuv_s, D)):
        ld_w[key] = em.op("gpsimd", lambda g, w=w, dst=dst, width=width:
                          g.dma_start(
            out=dst[:].rearrange("p (l j) -> p l j", j=width),
            in_=w.rearrange("(l p) j -> p l j", p=128)), dma="gw")
    ld_cst = {}
    for key, t, d_ in (("identb", ident_b, ident_d), ("tri", tri_b, tri_d),
                       ("p128", p128_b, p128_d), ("ropec", ropeC, ropec_d),
                       ("ropes", ropeS, ropes_d)):
        ld_cst[key] = em.op("gpsimd", lambda g, t=t, d_=d_:
                            g.dma_start(out=t[:], in_=d_), dma="gw")
    ld_ones = em.op("gpsimd", lambda g: g.memset(ones_t[:], 1.0))
    for key, t, d_ in (("identf", ident_f, ident_d), ("c", ctile, c_d)):
        ld_cst[key] = em.op("sync", lambda s, t=t, d_=d_:
                            s.dma_start(out=t[:], in_=d_), dma="sc0")
    ld_cst["clipx"] = em.op("sync", lambda s: s.dma_start(
        out=scal[0:1, 0:1], in_=clipx_d), dma="sc0")
    ld_cst["clipd"] = em.op("sync", lambda s: s.dma_start(
        out=scal[0:1, 1:2], in_=clipd_d), dma="sc0")
    NHALF = (6 * D) // WMODW
    gw_last = ld_cst["ropes"]
    for k in list(ld_w):
        ld_w[k] = gw_last
    for k in ("identb", "tri", "p128", "ropec", "ropes"):
        ld_cst[k] = gw_last
    sc_last = ld_cst["clipd"]
    for k in ("identf", "c", "clipx", "clipd"):
        ld_cst[k] = sc_last
    ld_wmb = [em.op("sync", lambda s: s.dma_start(
        out=wmb[0][:], in_=wmod_d[0:128, 0:WMODW]), dma="wmb0")]

    o_silu = em.op("scalar", lambda s: s.activation(
        silu_c[0:NB, :], ctile[0:NB, :], AF.Silu), waits=[ld_cst["c"]])
    o_sfm = None
    for c in range(C8):
        sl, w = pss.next()
        o_t = em.op("tensor", lambda t, c=c, sl=sl: t.transpose(
            psS[sl][:, 0:NB], silu_c[0:NB, c * 128:(c + 1) * 128],
            ident_f[0:NB, 0:NB]), waits=[o_silu, ld_cst["identf"]] + w)
        o_sfm = em.op("vector", lambda v, c=c, sl=sl: v.tensor_copy(
            sfm[:, c * NB:(c + 1) * NB], psS[sl][:, 0:NB]), waits=[o_t])
        pss.used(sl, o_sfm)
    o_sfmb = None

    JH = WMODW // 128
    o_mm_last = None
    mod_mms = []
    sl_u, w_u = psu.next()
    for li in range(C8 * NHALF):
        dc, half = li // NHALF, li % NHALF
        if li + 1 < C8 * NHALF:
            dc2, half2 = (li + 1) // NHALF, (li + 1) % NHALF
            ld_wmb.append(em.op(
                "sync", lambda s, dc2=dc2, half2=half2, li=li: s.dma_start(
                    out=wmb[(li + 1) % 2][:],
                    in_=wmod_d[dc2 * 128:(dc2 + 1) * 128,
                               half2 * WMODW:(half2 + 1) * WMODW]),
                waits=[mod_mms[-1]] if li >= 1 else [],
                dma=f"wmb{(li + 1) % 2}"))
        for j in range(JH):
            jj = half * JH + j
            col = (jj // 8) * 16 + (jj % 8) * NB
            o_mm_last = em.op("tensor", lambda t, li=li, j=j, col=col: t.matmul(
                psU[:, col:col + NB], wmb[li % 2][:, j * 128:(j + 1) * 128],
                sfm[:, (li // NHALF) * NB:(li // NHALF + 1) * NB],
                start=(li == 0 and j == 0),
                stop=(li == C8 * NHALF - 1 and j == JH - 1)),
                waits=[o_sfm, ld_wmb[li]] + w_u)
            w_u = []
        mod_mms.append(o_mm_last)
    acc_last = em.op("vector", lambda v: v.tensor_copy(
        modfm[:, 0:96], psU[:, 0:96]), waits=[o_mm_last])
    psu.used(0, acc_last)

    em.op("vector", lambda v: v.tensor_scalar(
        modfm[:, 16:32], modfm[:, 16:32], 1.0, None, ALU.add), waits=[acc_last])
    em.op("vector", lambda v: v.tensor_scalar(
        modfm[:, 64:80], modfm[:, 64:80], 1.0, None, ALU.add), waits=[acc_last])
    o_mod = None
    for base in (32, 80):
        o1 = em.op("vector", lambda v, base=base: v.tensor_scalar(
            rsc1[:, 0:16], modfm[:, base:base + 16], 0.0, None, ALU.is_ge),
            waits=[acc_last] + ([o_mod] if o_mod else []))
        o2 = em.op("vector", lambda v: v.tensor_scalar(
            rsc1[:, 0:16], rsc1[:, 0:16], 2.0, -1.0, ALU.mult, ALU.add),
            waits=[o1])
        o3 = em.op("vector", lambda v, base=base: v.tensor_tensor(
            rsc1[:, 16:32], modfm[:, base:base + 16], rsc1[:, 0:16], ALU.mult),
            waits=[o2])
        o4 = em.op("vector", lambda v: v.tensor_scalar(
            rsc1[:, 16:32], rsc1[:, 16:32], 0.25, None, ALU.max), waits=[o3])
        o_mod = em.op("vector", lambda v, base=base: v.tensor_tensor(
            modfm[:, base:base + 16], rsc1[:, 16:32], rsc1[:, 0:16], ALU.mult),
            waits=[o4])
    o_sfmb = em.op("vector", lambda v: v.tensor_copy(sfm_b[:], modfm[:, 0:16]),
                   waits=[o_mod])
    dump_sbuf("dbg_modfm", modfm[:], (128, 96), waits=[o_mod])

    bcast_chain = [None]

    def build_bcast(target, v_i, b, pstile, ring, scale_col=None, extra=()):
        outs = []
        o2 = bcast_chain[0]
        for c in range(C8):
            col = v_i * 16 + c * 2 + b
            o1 = em.op("vector", lambda vv, col=col: vv.tensor_scalar(
                rsc2[:, 0:128], ones_t[:, 0:128], modfm[:, col:col + 1], None,
                ALU.mult), waits=[o_mod, ld_ones, o2] + list(extra))
            sl, w = ring.next()
            o2 = em.op("tensor", lambda t, sl=sl: t.transpose(
                pstile[sl][:, 0:128] if isinstance(pstile, list)
                else pstile[:, 0:128], rsc2[:, 0:128], ident_f[:]),
                waits=[o1, ld_cst["identf"]] + w)
            src = (pstile[sl] if isinstance(pstile, list) else pstile)
            if scale_col is None:
                o3 = em.op("vector", lambda vv, c=c, src=src: vv.tensor_copy(
                    target[:, c * 128:(c + 1) * 128], src[:, 0:128]),
                    waits=[o2])
            else:
                o3 = em.op("vector", lambda vv, c=c, src=src: vv.tensor_scalar(
                    target[:, c * 128:(c + 1) * 128], src[:, 0:128],
                    sc[:, scale_col:scale_col + 1], None, ALU.mult),
                    waits=[o2] + list(extra))
            ring.used(sl, o3)
            outs.append(o3)
        bcast_chain[0] = o2
        return outs

    def ln_stats(src_tile, t_par, waits):
        sb0 = (t_par % 2) * 6
        o_sq = em.op("scalar", lambda s: s.activation(
            sqs[:], src_tile[:], AF.Square, accum_out=stat[:, sb0 + 1:sb0 + 2]),
            waits=waits)
        o_sum = em.op("vector", lambda v: v.tensor_reduce(
            stat[:, sb0:sb0 + 1], src_tile[:], axis=mybir.AxisListType.X,
            op=ALU.add), waits=waits)
        o_mean = em.op("vector", lambda v: v.tensor_scalar(
            stat[:, sb0 + 2:sb0 + 3], stat[:, sb0:sb0 + 1], 1.0 / D, None,
            ALU.mult), waits=[o_sum])
        o_msq = em.op("vector", lambda v: v.tensor_scalar(
            stat[:, sb0 + 3:sb0 + 4], stat[:, sb0 + 1:sb0 + 2], 1.0 / D, None,
            ALU.mult), waits=[o_sq])
        o_mm2 = em.op("vector", lambda v: v.tensor_tensor(
            stat[:, sb0 + 4:sb0 + 5], stat[:, sb0 + 2:sb0 + 3],
            stat[:, sb0 + 2:sb0 + 3], ALU.mult), waits=[o_mean])
        o_var = em.op("vector", lambda v: v.scalar_tensor_tensor(
            stat[:, sb0 + 3:sb0 + 4], stat[:, sb0 + 3:sb0 + 4], 1e-5,
            stat[:, sb0 + 4:sb0 + 5], ALU.add, ALU.subtract),
            waits=[o_msq, o_mm2])
        o_std = em.op("scalar", lambda s: s.activation(
            stat[:, sb0 + 4:sb0 + 5], stat[:, sb0 + 3:sb0 + 4], AF.Sqrt),
            waits=[o_var])
        o_rstd = em.op("vector", lambda v: v.reciprocal(
            stat[:, sb0 + 3:sb0 + 4], stat[:, sb0 + 4:sb0 + 5]), waits=[o_std])
        o_nb1 = em.op("vector", lambda v: v.tensor_tensor(
            stat[:, sb0 + 5:sb0 + 6], stat[:, sb0 + 2:sb0 + 3],
            stat[:, sb0 + 3:sb0 + 4], ALU.mult), waits=[o_rstd])
        o_nb = em.op("vector", lambda v: v.tensor_scalar(
            stat[:, sb0 + 5:sb0 + 6], stat[:, sb0 + 5:sb0 + 6], -1.0, None,
            ALU.mult), waits=[o_nb1])
        return sb0, o_rstd, o_nb, o_sq, o_sum

    o_x1_wr = []
    kg_gate = []
    vg_ones_tok = None
    prev_psT_evac = None
    o_attn_last = None
    acc_readers = {0: [], 1: []}
    prev_bc = None
    qt_par = [0]

    for b in range(NB):
        o_h1_done = None
        for t in range(NT):
            xi, xw = xr_ring.next()
            o_ld = em.op("sync", lambda s, b=b, t=t, xi=xi: s.dma_start(
                out=xr[xi][:], in_=x_in[b, t * 128:(t + 1) * 128, :]),
                waits=xw, dma=f"xr{xi}")
            sb0, o_rstd, o_nb, o_sq, o_sum = ln_stats(
                xr[xi], t, [o_ld] + ([o_sfm] if (b == 0 and t == 0) else []))
            o_nrm = em.op("vector", lambda v, xi=xi, sb0=sb0: v.tensor_scalar(
                nrmb[:], xr[xi][:], stat[:, sb0 + 3:sb0 + 4],
                stat[:, sb0 + 5:sb0 + 6], ALU.mult, ALU.add),
                waits=[o_nb, prev_psT_evac] + kg_gate)
            tps = []
            for c in range(C8):
                tps.append(em.op("tensor", lambda t_, c=c: t_.transpose(
                    psT[:, c * 128:(c + 1) * 128],
                    nrmb[:, c * 128:(c + 1) * 128], ident_b[:]),
                    waits=[o_nrm, ld_cst["identb"], prev_psT_evac]))
            o_h1_done = em.op("vector", lambda v, t=t: v.tensor_copy(
                hbuf[:].rearrange("p (c s) -> p c s", s=S)[:, :, t * 128:(t + 1) * 128],
                psT[:].rearrange("p (c j) -> p c j", j=128)), waits=tps)
            prev_psT_evac = o_h1_done
            xr_ring.used(xi, o_sq, o_sum, o_nrm)
        if b == 0:
            dump_sbuf("dbg_h1fm", hbuf[:], (128, C8 * S), BF16,
                      waits=[o_h1_done])

        o_wp = None
        for (w_src, w_dst, width) in ((wdq_s, wdq_p, DL), (wdkv_s, wdkv_p, DL),
                                      (wkr_s, wkr_p, DHR)):
            for c in range(C8):
                col = 16 + c * 2 + b
                o_wp = em.op(
                    "vector",
                    lambda v, w_src=w_src, w_dst=w_dst, c=c, col=col, width=width:
                    v.tensor_scalar(
                        w_dst[:, c * width:(c + 1) * width],
                        w_src[:, c * width:(c + 1) * width],
                        modfm[:, col:col + 1], None, ALU.mult),
                    waits=[o_mod, ld_w["dq"], ld_w["dkv"], ld_w["kr"]] + kg_gate)
        for (w_src, dst, nl, width) in ((wdq_s, bias_cq, LC, DL),
                                        (wdkv_s, bias_ckv, LC, DL),
                                        (wkr_s, bias_kr, 1, DHR)):
            for l in range(nl):
                sl_u, w_u = psu.next()
                mpart = 128 if width == DL else DHR
                mms = []
                for c in range(C8):
                    sacol = c * 2 + b
                    mms.append(em.op(
                        "tensor",
                        lambda t, w_src=w_src, c=c, l=l, width=width,
                        mpart=mpart, sacol=sacol: t.matmul(
                            psU[0:mpart, 0:1],
                            w_src[:, c * width + l * 128:
                                  c * width + l * 128 + mpart],
                            sfm_b[:, sacol:sacol + 1],
                            start=(c == 0), stop=(c == C8 - 1)),
                        waits=[o_sfmb] + w_u))
                o_wp = em.op("vector",
                             lambda v, dst=dst, l=l, mpart=mpart: v.tensor_copy(
                                 dst[0:mpart, l:l + 1], psU[0:mpart, 0:1]),
                             waits=[mms[-1]])
                psu.used(0, o_wp)

        def proj_lat(w_p, bias_t, dst):
            last = None
            for l in range(LC):
                for q4 in range(4):
                    sl, w = pss.next()
                    mms = []
                    for c in range(C8):
                        mms.append(em.op(
                            "tensor",
                            lambda t, w_p=w_p, l=l, q4=q4, c=c, sl=sl: t.matmul(
                                psS[sl][:, 0:512],
                                w_p[:, c * DL + l * 128:c * DL + (l + 1) * 128],
                                hbuf[:, c * S + q4 * 512:c * S + (q4 + 1) * 512],
                                start=(c == 0), stop=(c == C8 - 1)),
                            waits=[o_h1_done, o_wp] + w))
                    last = em.op("vector", lambda v, dst=dst, l=l, q4=q4, sl=sl,
                                 bias_t=bias_t: v.tensor_scalar(
                        dst[:, l * S + q4 * 512:l * S + (q4 + 1) * 512],
                        psS[sl][:, 0:512], bias_t[:, l:l + 1], None, ALU.add),
                        waits=[mms[-1]])
                    pss.used(sl, last)
            return last

        o_cq = proj_lat(wdq_p, bias_cq, cq_s)
        o_ckv = proj_lat(wdkv_p, bias_ckv, ckv_s)
        if b == 0:
            dump_sbuf("dbg_cq", cq_s[:], (128, LC * S), BF16, waits=[o_cq])
            dump_sbuf("dbg_ckv", ckv_s[:], (128, LC * S), BF16, waits=[o_ckv])
            dump_sbuf("dbg_wdqp", wdq_p[:], (128, C8 * DL), BF16, waits=[o_wp])
            dump_sbuf("dbg_biascq", bias_cq[:], (128, LC), F32, waits=[o_wp])

        o_kr = None
        for q4 in range(4):
            sl, w = pss.next()
            mms = []
            for c in range(C8):
                mms.append(em.op("tensor", lambda t, c=c, q4=q4, sl=sl: t.matmul(
                    psS[sl][0:32, 0:512], wkr_p[:, c * DHR:(c + 1) * DHR],
                    hbuf[:, c * S + q4 * 512:c * S + (q4 + 1) * 512],
                    start=(c == 0), stop=(c == C8 - 1)),
                    waits=[o_h1_done, o_wp] + w))
            o_e = em.op("vector", lambda v, q4=q4, sl=sl: v.tensor_scalar(
                krt[64:96, q4 * 512:(q4 + 1) * 512], psS[sl][0:32, 0:512],
                bias_kr[0:32, 0:1], None, ALU.add),
                waits=[mms[-1]] + kg_gate)
            pss.used(sl, o_e)
            sl_u, w_u = psu.next()
            o_sw = em.op("tensor", lambda t, q4=q4: t.matmul(
                psU[64:96, 0:512], p128_b[64:96, 64:96],
                krt[64:96, q4 * 512:(q4 + 1) * 512], start=True, stop=True),
                waits=[o_e, ld_cst["p128"]] + w_u)
            o_r1 = em.op("vector", lambda v, q4=q4: v.tensor_tensor(
                rsc1[64:96, 0:512], krt[64:96, q4 * 512:(q4 + 1) * 512],
                ropeC[64:96, q4 * 512:(q4 + 1) * 512], ALU.mult),
                waits=[o_e, ld_cst["ropec"]])
            o_r2 = em.op("vector", lambda v, q4=q4: v.tensor_tensor(
                rsc2[64:96, 0:512], psU[64:96, 0:512],
                ropeS[64:96, q4 * 512:(q4 + 1) * 512], ALU.mult),
                waits=[o_sw, ld_cst["ropes"]])
            o_kr = em.op("vector", lambda v, q4=q4: v.tensor_tensor(
                krt[64:96, q4 * 512:(q4 + 1) * 512], rsc1[64:96, 0:512],
                rsc2[64:96, 0:512], ALU.add), waits=[o_r1, o_r2])
            psu.used(0, o_r2)

        ga_bc = build_bcast(garow, 2, b, psS, pss)

        attn_pend = []
        norm_box = [None]
        vg_ones_tok = em.op("vector", lambda v: v.memset(
            vg[:].rearrange("p (t h j) -> p t h j", h=4, j=65)[:, :, :, 64:65],
            1.0), waits=kg_gate)
        av_ops = []
        for g in range(4):
            kv_gate = kg_gate if g == 0 else list(av_ops)
            for pr in range(2):
                h0 = (g * 4 + 2 * pr)
                for q4 in range(4):
                    sl, w = pss.next()
                    mms = []
                    for l in range(LC):
                        mms.append(em.op(
                            "tensor", lambda t, h0=h0, l=l, q4=q4, sl=sl:
                            t.matmul(
                                psS[sl][:, 0:512],
                                wuk_s[:, l * D + h0 * DH:l * D + (h0 + 2) * DH],
                                ckv_s[:, l * S + q4 * 512:l * S + (q4 + 1) * 512],
                                start=(l == 0), stop=(l == LC - 1)),
                            waits=[o_ckv, ld_w["uk"]] + w))
                    hh_e, hh_o = 2 * pr, 2 * pr + 1
                    o_e1 = em.op("scalar", lambda s, hh_e=hh_e, q4=q4, sl=sl:
                                 s.activation(
                        kg[0:64, hh_e * S + q4 * 512:hh_e * S + (q4 + 1) * 512],
                        psS[sl][0:64, 0:512], AF.Copy),
                        waits=[mms[-1]] + kv_gate)
                    o_e2 = em.op("vector", lambda v, hh_o=hh_o, q4=q4, sl=sl:
                                 v.tensor_copy(
                        kg[0:64, hh_o * S + q4 * 512:hh_o * S + (q4 + 1) * 512],
                        psS[sl][64:128, 0:512]), waits=[mms[-1]] + kv_gate)
                    pss.used(sl, o_e1, o_e2)
            kr_cp = []
            for hh in range(4):
                kr_cp.append(em.op("vector", lambda v, hh=hh: v.tensor_copy(
                    kg[64:96, hh * S:(hh + 1) * S], krt[64:96, :]),
                    waits=[o_kr] + kv_gate))
            vg_evs = []
            for tt in range(NT):
                sl, w = pss.next()
                mms = []
                for l in range(LC):
                    mms.append(em.op("tensor",
                                     lambda t, g=g, tt=tt, l=l, sl=sl: t.matmul(
                        psS[sl][:, 0:256],
                        ckv_s[:, l * S + tt * 128:l * S + (tt + 1) * 128],
                        wuv_s[:, l * D + g * 256:l * D + (g + 1) * 256],
                        start=(l == 0), stop=(l == LC - 1)),
                        waits=[o_ckv, ld_w["uv"]] + w))
                o_ve = em.op("vector", lambda v, tt=tt, sl=sl: v.tensor_copy(
                    vg[:, tt * 260:(tt + 1) * 260].rearrange(
                        "p (h j) -> p h j", j=65)[:, :, 0:64],
                    psS[sl][:, 0:256].rearrange("p (h j) -> p h j", j=64)),
                    waits=[mms[-1]] + kv_gate)
                pss.used(sl, o_ve)
                vg_evs.append(o_ve)
            q_evs = []
            for pr in range(2):
                h0 = (g * 4 + 2 * pr)
                for q4 in range(4):
                    sl, w = pss.next()
                    mms = []
                    for l in range(LC):
                        mms.append(em.op(
                            "tensor", lambda t, h0=h0, l=l, q4=q4, sl=sl:
                            t.matmul(
                                psS[sl][:, 0:512],
                                wuq_s[:, l * D + h0 * DH:l * D + (h0 + 2) * DH],
                                cq_s[:, l * S + q4 * 512:l * S + (q4 + 1) * 512],
                                start=(l == 0), stop=(l == LC - 1)),
                            waits=[o_cq, ld_w["uq"]] + w))
                    hh_e, hh_o = 2 * pr, 2 * pr + 1
                    o_e1 = em.op("scalar", lambda s, hh_e=hh_e, q4=q4, sl=sl:
                                 s.activation(
                        qgt[0:64, hh_e * S + q4 * 512:hh_e * S + (q4 + 1) * 512],
                        psS[sl][0:64, 0:512], AF.Copy),
                        waits=[mms[-1]] + kv_gate)
                    o_e2 = em.op("vector", lambda v, hh_o=hh_o, q4=q4, sl=sl:
                                 v.tensor_copy(
                        qgt[0:64, hh_o * S + q4 * 512:hh_o * S + (q4 + 1) * 512],
                        psS[sl][64:128, 0:512]), waits=[mms[-1]] + kv_gate)
                    pss.used(sl, o_e1, o_e2)
                    q_evs += [o_e1, o_e2]
            qr_last = None
            for q4 in range(4):
                sl, w = pss.next()
                mms = []
                for l in range(LC):
                    mms.append(em.op("tensor",
                                     lambda t, g=g, l=l, q4=q4, sl=sl: t.matmul(
                        psS[sl][:, 0:512],
                        wqr_s[:, l * NH * DHR + g * 128:
                              l * NH * DHR + (g + 1) * 128],
                        cq_s[:, l * S + q4 * 512:l * S + (q4 + 1) * 512],
                        start=(l == 0), stop=(l == LC - 1)),
                        waits=[o_cq, ld_w["qr"]] + w))
                o_qe = em.op("vector", lambda v, q4=q4, sl=sl: v.tensor_copy(
                    qrq[:, q4 * 512:(q4 + 1) * 512], psS[sl][:, 0:512]),
                    waits=[mms[-1]] + kv_gate)
                pss.used(sl, o_qe)
                sl_u, w_u = psu.next()
                o_sw = em.op("tensor", lambda t, q4=q4: t.matmul(
                    psU[:, 0:512], p128_b[:], qrq[:, q4 * 512:(q4 + 1) * 512],
                    start=True, stop=True), waits=[o_qe] + w_u)
                o_r1 = em.op("vector", lambda v, q4=q4: v.tensor_tensor(
                    rsc1[:, 0:512], qrq[:, q4 * 512:(q4 + 1) * 512],
                    ropeC[:, q4 * 512:(q4 + 1) * 512], ALU.mult), waits=[o_qe])
                o_r2 = em.op("vector", lambda v, q4=q4: v.tensor_tensor(
                    rsc2[:, 0:512], psU[:, 0:512],
                    ropeS[:, q4 * 512:(q4 + 1) * 512], ALU.mult), waits=[o_sw])
                qr_last = em.op("vector", lambda v, q4=q4: v.tensor_tensor(
                    qrq[:, q4 * 512:(q4 + 1) * 512], rsc1[:, 0:512],
                    rsc2[:, 0:512], ALU.add), waits=[o_r1, o_r2])
                psu.used(0, o_r2)
            for hh in range(4):
                q_evs.append(em.op("vector", lambda v, hh=hh: v.tensor_copy(
                    qgt[64:96, hh * S:(hh + 1) * S],
                    qrq[hh * 32:(hh + 1) * 32, :]), waits=[qr_last] + kv_gate))
            o_build = em.op("scalar", lambda s: s.copy(
                scal[0:1, 15:16], scal[0:1, 15:16]),
                waits=[o_e1, o_e2, kr_cp[-1], vg_evs[-1]] + q_evs[-4:])
            if b == 0 and g == 0:
                dump_sbuf("dbg_kg", kg[:], (128, 4 * S), BF16, waits=[o_build])
                dump_sbuf("dbg_vg", vg[:], (128, NT * 4 * 65), BF16,
                          waits=[o_build])
                dump_sbuf("dbg_qg", qgt[:], (128, 4 * S), BF16, waits=[o_build])

            AVD = 3
            av_ops = []

            def pop_emit():
                nonlocal prev_bc, o_attn_last, av_ops
                it = attn_pend.pop(0)
                if it["first"] and norm_box[0] is not None \
                        and norm_box[0][3] == it["par"]:
                    flush_norm()
                ei, regs, gates = it["ei"], it["regs"], it["gates"]
                hh_, first, last = it["hh"], it["first"], it["last"]
                par = it["par"]
                psA = psACCs[par]
                outs = []
                for ri, (k, col, n, qs) in enumerate(regs):
                    ww = list(gates) + [vg_ones_tok]
                    if first and ri == 0:
                        ww += acc_readers[par]
                    outs.append(em.op(
                        "tensor",
                        lambda t, k=k, col=col, n=n, qs=qs, ei=ei, hh_=hh_,
                        psA=psA, st=(first and ri == 0),
                        sp=(last and ri == len(regs) - 1): t.matmul(
                            psA[0:65, qs:512],
                            vg[:, k * 260 + hh_ * 65:k * 260 + (hh_ + 1) * 65],
                            ebuf[ei][:, col:col + n],
                            start=st, stop=sp),
                        waits=ww))
                eb_ring.used(ei, *outs)
                av_ops += outs
                if last:
                    if norm_box[0] is not None:
                        flush_norm()
                    norm_box[0] = (it["h"], it["q0"], outs[-1], par)

            def flush_norm():
                nonlocal prev_bc, o_attn_last
                h_, q0_, o_last_av, par = norm_box[0]
                norm_box[0] = None
                psA = psACCs[par]
                o_rec = em.op("vector", lambda v, psA=psA: v.reciprocal(
                    zrow, psA[64:65, 0:512]), waits=[o_last_av, prev_bc])
                sl_u, w_u = psu.next()
                o_bc = em.op("tensor", lambda t: t.matmul(
                    psU[0:64, 0:512], ones_t[64:65, 0:64],
                    zrow, start=True, stop=True),
                    waits=[o_rec, ld_ones, o_attn_last] + w_u)
                o_an = em.op("scalar", lambda s, psA=psA: s.activation(
                    anorm[0:64, 0:512], psA[0:64, 0:512], AF.Copy),
                    waits=[o_last_av, o_attn_last])
                o_nm = em.op("vector", lambda v, h_=h_, q0_=q0_: v.tensor_tensor(
                    hbuf[(h_ % 2) * 64:(h_ % 2) * 64 + 64,
                         (h_ // 2) * S + q0_:(h_ // 2) * S + q0_ + 512],
                    anorm[0:64, 0:512], psU[0:64, 0:512], ALU.mult),
                    waits=[o_an, o_bc])
                psu.used(0, o_nm)
                acc_readers[par] = [o_rec, o_an]
                prev_bc = o_bc
                o_attn_last = o_nm

            for hh in range(4):
                h = g * 4 + hh
                for qt in range(QT):
                    q0 = qt * 512
                    groups = []
                    nfull = 4 * qt
                    for kp in range(nfull // 2):
                        groups.append([(2 * kp, 0, 512, 0),
                                       (2 * kp + 1, 512, 512, 0)])
                    kd = nfull
                    groups.append([(kd, 0, 512, 0), (kd + 1, 512, 384, 128),
                                   (kd + 3, 896, 128, 384)])
                    groups.append([(kd + 2, 0, 256, 256)])
                    n_g = len(groups)
                    for gi in range(n_g):
                        regs = groups[gi]
                        sl, w = pss.next()
                        first_waits = [o_build] + w
                        mms = []
                        for (k, col, n, qs) in regs:
                            mms.append(em.op(
                                "tensor",
                                lambda t, k=k, col=col, n=n, qs=qs, sl=sl,
                                hh=hh, q0=q0: t.matmul(
                                    psS[sl][:, col:col + n],
                                    kg[0:96, hh * S + k * 128:
                                       hh * S + (k + 1) * 128],
                                    qgt[0:96, hh * S + q0 + qs:
                                        hh * S + q0 + 512],
                                    start=True, stop=True),
                                waits=first_waits))
                            first_waits = []
                        wid0 = regs[0][1]
                        tot = (regs[-1][1] + regs[-1][2]) - wid0
                        ei, ew = eb_ring.next()
                        o_exp = em.op("scalar", lambda s, sl=sl, ei=ei,
                                      wid0=wid0, tot=tot: s.activation(
                            ebuf[ei][:, wid0:wid0 + tot],
                            psS[sl][:, wid0:wid0 + tot], AF.Exp, scale=ISQ96),
                            waits=mms + ew)
                        pss.used(sl, o_exp)
                        o_clip = em.op("vector", lambda v, ei=ei, wid0=wid0,
                                       tot=tot: v.tensor_scalar(
                            ebuf[ei][:, wid0:wid0 + tot],
                            ebuf[ei][:, wid0:wid0 + tot],
                            EXP_HI, EXP_LO, ALU.min, ALU.max), waits=[o_exp])
                        gates = [o_clip]
                        diag = regs[-1][3] > 0 or regs[0][0] >= nfull
                        if diag:
                            for (k, col, n, qs) in regs:
                                gates.append(em.op(
                                    "gpsimd",
                                    lambda g_, ei=ei, col=col: g_.tensor_tensor(
                                        ebuf[ei][:, col:col + 128],
                                        ebuf[ei][:, col:col + 128],
                                        tri_b[:], ALU.mult),
                                    waits=[o_clip, ld_cst["tri"]]))
                        attn_pend.append(dict(ei=ei, regs=regs, gates=gates,
                                              hh=hh, h=h, q0=q0,
                                              par=qt_par[0] % 2,
                                              first=(gi == 0),
                                              last=(gi == n_g - 1)))
                        if gi == n_g - 1:
                            qt_par[0] += 1
                        while len(attn_pend) > AVD:
                            pop_emit()
            while attn_pend:
                pop_emit()
            flush_norm()
        if b == 0:
            dump_sbuf("dbg_attnout", hbuf[:], (128, C8 * S), BF16,
                      waits=[o_attn_last])

        for tt in range(NT):
            xi, xw = xr_ring.next()
            o_ldx = em.op("sync", lambda s, b=b, tt=tt, xi=xi: s.dma_start(
                out=xr[xi][:], in_=x_in[b, tt * 128:(tt + 1) * 128, :]),
                waits=xw, dma=f"xr{xi}")
            sl, slw = pss.next()
            mms = []
            for half in range(2):
                for c in range(C8):
                    mms.append(em.op(
                        "tensor", lambda t, tt=tt, half=half, c=c, sl=sl:
                        t.matmul(
                            psS[sl][:, half * 512:(half + 1) * 512],
                            hbuf[:, c * S + tt * 128:c * S + (tt + 1) * 128],
                            wo_s[:, c * D + half * 512:c * D + (half + 1) * 512],
                            start=(c == 0), stop=(c == C8 - 1)),
                        waits=[o_attn_last, ld_w["o"]] + slw))
            o_m = em.op("vector", lambda v, sl=sl: v.tensor_tensor(
                rsc1[:, 0:1024], psS[sl][:, 0:1024], garow[:], ALU.mult),
                waits=[mms[-1]] + ga_bc)
            pss.used(sl, o_m)
            si, sw = stg_ring.next()
            o_x1 = em.op("vector", lambda v, xi=xi, si=si: v.tensor_tensor(
                stg[si][:], rsc1[:, 0:1024], xr[xi][:], ALU.add),
                waits=[o_m, o_ldx] + sw)
            xr_ring.used(xi, o_x1)
            o_wr = em.op("scalar", lambda s, b=b, tt=tt, si=si: s.dma_start(
                out=x1scr[b, tt * 128:(tt + 1) * 128, :], in_=stg[si][:]),
                waits=[o_x1], dma=f"stg{si}")
            stg_ring.used(si, o_wr)
            o_x1_wr.append(o_wr)
        kg_gate = [mms[-1], o_attn_last]

    barrier = em.op("vector", lambda v: v.memset(amax1[:, 0:4], 0.0),
                    waits=[o_x1_wr[-1], o_attn_last, av_ops[-1]])

    stA.close()
    stF = ExitStack()
    stFa = ExitStack()
    wdT = sb("wdT", (128, IC * D), BF16, st=stF)
    wquant = sb("wquant", (128, 12), st=stF)
    cbrow = sb("cbrow", (128, D), st=stF)
    sbrow = sb("sbrow", (128, D), st=stF)
    qxfm = sb("qxfm", (128, C8 * NB * S), BF16, st=stFa)
    wgr = [sb(f"wgr{i}", (128, D), st=stFa) for i in range(2)]
    wur = [sb(f"wur{i}", (128, D), st=stFa) for i in range(2)]
    wgq = sb("wgq", (128, D), BF16, st=stFa)
    wuq2 = sb("wuq2", (128, D), BF16, st=stFa)
    wgT = [sb(f"wgT{i}", (128, D), BF16, st=stFa) for i in range(2)]
    wuT = [sb(f"wuT{i}", (128, D), BF16, st=stFa) for i in range(2)]
    wdr = sb("wdr", (128, II), st=stFa)
    wdq2 = sb("wdq2", (128, II), BF16, st=stFa)
    silu_t = [sb(f"silu{i}", (128, 512), st=stFa) for i in range(2)]
    atile = [sb(f"atile{i}", (128, 512), st=stFa) for i in range(3)]

    psBig = psum("psBig", (128, 2048), st=stF)
    psT2 = psum("psT2", (128, 1024), BF16, st=stF)
    psW = psum("psW", (128, 512), st=stF)
    psw = Ring(1)
    a_ring = Ring(3)

    o_h2_wr = []
    o_am_h2 = None
    o_pool_prev = None
    for b in range(NB):
        cb_bc = build_bcast(cbrow, 4, b, psW, psw,
                            extra=[barrier, o_pool_prev])
        sb_bc = build_bcast(sbrow, 3, b, psW, psw,
                            extra=[barrier, o_pool_prev])
        for t in range(NT):
            xi, xw = xr_ring.next()
            o_ld = em.op("sync", lambda s, b=b, t=t, xi=xi: s.dma_start(
                out=xr[xi][:], in_=x1scr[b, t * 128:(t + 1) * 128, :]),
                waits=xw + [o_x1_wr[b * NT + t], barrier], dma=f"xr{xi}")
            sb0, o_rstd, o_nb, o_sq, o_sum = ln_stats(xr[xi], t, [o_ld])
            o_nrm = em.op("vector", lambda v, xi=xi, sb0=sb0: v.tensor_scalar(
                rsc1[:, 0:1024], xr[xi][:], stat[:, sb0 + 3:sb0 + 4],
                stat[:, sb0 + 5:sb0 + 6], ALU.mult, ALU.add),
                waits=[o_nb, o_pool_prev])
            o_h2a = em.op("gpsimd", lambda g: g.tensor_tensor(
                rsc2[:, 0:1024], rsc1[:, 0:1024], cbrow[:], ALU.mult),
                waits=[o_nrm] + cb_bc)
            si, sw = stg_ring.next()
            o_h2 = em.op("gpsimd", lambda g, si=si: g.tensor_tensor(
                stg[si][:], rsc2[:, 0:1024], sbrow[:], ALU.add),
                waits=[o_h2a] + sb_bc + sw)
            o_pool_prev = o_h2
            o_wr = em.op("scalar", lambda s, b=b, t=t, si=si: s.dma_start(
                out=h2scr[b, t * 128:(t + 1) * 128, :], in_=stg[si][:]),
                waits=[o_h2], dma=f"stg{si}")
            o_am = em.op("vector", lambda v, si=si: v.tensor_reduce(
                amax1[:, 2:3], stg[si][:], axis=mybir.AxisListType.X,
                op=ALU.max, apply_absolute_value=True),
                waits=[o_h2, barrier])
            o_am_h2 = em.op("vector", lambda v: v.tensor_tensor(
                amax1[:, 0:1], amax1[:, 0:1], amax1[:, 2:3], ALU.max),
                waits=[o_am, o_am_h2])
            stg_ring.used(si, o_wr, o_am)
            xr_ring.used(xi, o_sq, o_sum, o_nrm)
            o_h2_wr.append(o_wr)

    def collective_scale(amax_col, cci, cco, clip_key, inv_col, sc_col, waits):
        sl_w, ww = psw.next()
        o_tp = em.op("tensor", lambda t: t.transpose(
            psW[0:1, 0:128], amax1[:, amax_col:amax_col + 1], ident_f[:]),
            waits=list(waits) + [ld_cst["identf"]] + ww)
        o_red = em.op("vector", lambda v: v.tensor_reduce(
            scal[0:1, 4:5], psW[0:1, 0:128], axis=mybir.AxisListType.X,
            op=ALU.max), waits=[o_tp])
        psw.used(0, o_red)
        o_out = em.op("gpsimd", lambda g: g.dma_start(out=cci,
                                                      in_=scal[0:1, 4:5]),
                      waits=[o_red], dma="cio" + clip_key)
        o_cc = em.op("gpsimd", lambda g: g.collective_compute(
            "AllReduce", ALU.max, replica_groups=[list(range(n_cores))],
            ins=[cci], outs=[cco]), waits=[o_out],
            inc=("ccs" + clip_key, 1))
        o_in = em.op("gpsimd", lambda g: g.dma_start(out=scal[0:1, 5:6],
                                                     in_=cco),
                     waits=[o_cc], dma="cio" + clip_key)
        o_sg = em.op("scalar", lambda s: s.activation(
            scal[0:1, 6:7], scal[0:1, 0:1] if clip_key == "x"
            else scal[0:1, 1:2], AF.Sigmoid),
            waits=[ld_cst["clipx"], ld_cst["clipd"], o_in])
        o_m1 = em.op("vector", lambda v: v.tensor_tensor(
            scal[0:1, 7:8], scal[0:1, 5:6], scal[0:1, 6:7], ALU.mult),
            waits=[o_sg, o_in])
        o_m2 = em.op("vector", lambda v: v.tensor_scalar(
            scal[0:1, 8:9], scal[0:1, 7:8], 1.0 / 127.0, None, ALU.mult),
            waits=[o_m1])
        o_rc = em.op("vector", lambda v: v.reciprocal(
            scal[0:1, 9:10], scal[0:1, 8:9]), waits=[o_m2])
        outs = []
        for (src, dstc) in ((9, inv_col), (8, sc_col)):
            sl_w, ww = psw.next()
            o_b1 = em.op("tensor", lambda t, src=src: t.matmul(
                psW[:, 0:1], ones_t[0:1, 0:128], scal[0:1, src:src + 1],
                start=True, stop=True), waits=[o_rc, ld_ones] + ww)
            o_cp = em.op("vector", lambda v, dstc=dstc: v.tensor_copy(
                sc[:, dstc:dstc + 1], psW[:, 0:1]), waits=[o_b1])
            psw.used(0, o_cp)
            outs.append(o_cp)
        return outs

    sx_ops = collective_scale(0, cc1i, cc1o, "x", 0, 1, [o_am_h2])

    prev_psT2 = None
    o_qx_last = None
    o_q3 = None
    for b in range(NB):
        for t in range(NT):
            xi, xw = xr_ring.next()
            o_ld = em.op("sync", lambda s, b=b, t=t, xi=xi: s.dma_start(
                out=xr[xi][:], in_=h2scr[b, t * 128:(t + 1) * 128, :]),
                waits=xw + [o_h2_wr[b * NT + t]], dma=f"xr{xi}")
            o_q1 = em.op("scalar", lambda sx_, xi=xi: sx_.activation(
                rsc1[:, 0:1024], xr[xi][:], AF.Copy,
                scale=sc[:, 0:1], bias=MAGIC),
                waits=[o_ld, o_q3] + sx_ops)
            o_q2 = em.op("vector", lambda v: v.tensor_scalar(
                rsc1[:, 0:1024], rsc1[:, 0:1024], MAGIC, 127.0, ALU.subtract,
                ALU.min), waits=[o_q1])
            o_q3 = em.op("vector", lambda v: v.tensor_scalar(
                rsc2[:, 0:1024], rsc1[:, 0:1024], -128.0, None, ALU.max),
                waits=[o_q2])
            o_cb = em.op("vector", lambda v: v.tensor_copy(
                wgq[:, 0:1024], rsc2[:, 0:1024]), waits=[o_q3, barrier])
            tps = []
            for c in range(C8):
                tps.append(em.op("tensor", lambda t_, c=c: t_.transpose(
                    psT2[:, c * 128:(c + 1) * 128],
                    wgq[:, c * 128:(c + 1) * 128], ident_b[:]),
                    waits=[o_cb, prev_psT2]))
            gcol = b * S + t * 128
            o_qx_last = em.op("vector", lambda v, gcol=gcol: v.tensor_copy(
                qxfm[:].rearrange("p (c s) -> p c s",
                                  s=NB * S)[:, :, gcol:gcol + 128],
                psT2[:].rearrange("p (c j) -> p c j", j=128)), waits=tps)
            prev_psT2 = o_qx_last
            xr_ring.used(xi, o_q1)
    dump_sbuf("dbg_qxfm", qxfm[:], (128, C8 * NB * S), BF16,
              waits=[o_qx_last])

    def quant_weight(dst_bf, src_f32, width, q0_, waits):
        o_r = em.op("vector", lambda v: v.tensor_reduce(
            wquant[:, q0_:q0_ + 1], src_f32[:, 0:width],
            axis=mybir.AxisListType.X, op=ALU.max, apply_absolute_value=True),
            waits=waits)
        o_s = em.op("vector", lambda v: v.tensor_scalar(
            wquant[:, q0_ + 1:q0_ + 2], wquant[:, q0_:q0_ + 1],
            1.0 / 127.0, None, ALU.mult), waits=[o_r])
        o_i = em.op("vector", lambda v: v.reciprocal(
            wquant[:, q0_ + 2:q0_ + 3], wquant[:, q0_ + 1:q0_ + 2]),
            waits=[o_s])
        o_1 = em.op("vector", lambda v: v.tensor_scalar(
            src_f32[:, 0:width], src_f32[:, 0:width],
            wquant[:, q0_ + 2:q0_ + 3], MAGIC, ALU.mult, ALU.add),
            waits=[o_i])
        o_2 = em.op("vector", lambda v: v.tensor_scalar(
            src_f32[:, 0:width], src_f32[:, 0:width], MAGIC, 127.0,
            ALU.subtract, ALU.min), waits=[o_1])
        o_3 = em.op("vector", lambda v: v.tensor_scalar(
            dst_bf[:, 0:width], src_f32[:, 0:width], -128.0,
            wquant[:, q0_ + 1:q0_ + 2], ALU.max, ALU.mult), waits=[o_2])
        return o_3

    a_wr = {}
    o_am_a = None
    gu_read = {0: [barrier], 1: [barrier]}
    slot_g = {}
    slot_u = {}
    wd_state = {"gate": [barrier], "last": None}

    def emit_wd_chunk(hc):
        nonlocal prev_psT2
        o_ldw = em.op("sync", lambda s, hc=hc: s.dma_start(
            out=wdr[:], in_=wd_d[hc * 128:(hc + 1) * 128, :]),
            waits=wd_state["gate"], dma="wdr")
        o_q = quant_weight(wdq2, wdr, II, 6, [o_ldw])
        wd_state["gate"] = [o_q]
        for icc in range(IC):
            o_tp = em.op("tensor", lambda t, icc=icc: t.transpose(
                psT2[:, (icc % 8) * 128:(icc % 8) * 128 + 128],
                wdq2[:, icc * 128:(icc + 1) * 128], ident_b[:]),
                waits=[o_q, prev_psT2])
            o_e = em.op("vector", lambda v, hc=hc, icc=icc: v.tensor_copy(
                wdT[:, icc * D + hc * 128:icc * D + (hc + 1) * 128],
                psT2[:, (icc % 8) * 128:(icc % 8) * 128 + 128]),
                waits=[o_tp])
            prev_psT2 = o_e
            wd_state["gate"].append(o_e)
            wd_state["last"] = o_e

    for ic in range(IC):
        buf = ic % 2
        if False and ic % 3 == 2 and ic // 3 < C8:
            emit_wd_chunk(ic // 3)
        o_lg = em.op("sync", lambda s, ic=ic, buf=buf: s.dma_start(
            out=wgr[buf][:], in_=wg_d[ic * 128:(ic + 1) * 128, :]),
            waits=gu_read[buf], dma=f"wg{buf}")
        o_lu = em.op("sync", lambda s, ic=ic, buf=buf: s.dma_start(
            out=wur[buf][:], in_=wu_d[ic * 128:(ic + 1) * 128, :]),
            waits=gu_read[buf], dma=f"wu{buf}")
        gu_read[buf] = []
        o_qg_ = quant_weight(wgq, wgr[buf], D, 0, [o_lg, o_qx_last, barrier])
        tps_g = []
        for c in range(C8):
            tps_g.append(em.op("tensor", lambda t, c=c: t.transpose(
                psT2[:, c * 128:(c + 1) * 128], wgq[:, c * 128:(c + 1) * 128],
                ident_b[:]), waits=[o_qg_, prev_psT2]))
        o_gT = em.op("vector", lambda v, buf=buf: v.tensor_copy(
            wgT[buf][:], psT2[:]), waits=tps_g)
        prev_psT2 = o_gT
        o_qu_ = quant_weight(wuq2, wur[buf], D, 3, [o_lu])
        tps_u = []
        for c in range(C8):
            tps_u.append(em.op("tensor", lambda t, c=c: t.transpose(
                psT2[:, c * 128:(c + 1) * 128], wuq2[:, c * 128:(c + 1) * 128],
                ident_b[:]), waits=[o_qu_, prev_psT2]))
        o_uT = em.op("vector", lambda v, buf=buf: v.tensor_copy(
            wuT[buf][:], psT2[:]), waits=tps_u)
        prev_psT2 = o_uT
        for t8 in range(NB * QT):
            slg = t8 % 2
            cg = slg * 512
            cu = 1024 + slg * 512
            mm_g, mm_u = [], []
            for c in range(C8):
                mm_g.append(em.op("tensor",
                                  lambda t, c=c, t8=t8, cg=cg, buf=buf:
                                  t.matmul(
                    psBig[:, cg:cg + 512], wgT[buf][:, c * 128:(c + 1) * 128],
                    qxfm[:, c * NB * S + t8 * 512:c * NB * S + (t8 + 1) * 512],
                    start=(c == 0), stop=(c == C8 - 1)),
                    waits=[o_gT, o_qx_last] + slot_g.get(slg, [])))
            for c in range(C8):
                mm_u.append(em.op("tensor",
                                  lambda t, c=c, t8=t8, cu=cu, buf=buf:
                                  t.matmul(
                    psBig[:, cu:cu + 512], wuT[buf][:, c * 128:(c + 1) * 128],
                    qxfm[:, c * NB * S + t8 * 512:c * NB * S + (t8 + 1) * 512],
                    start=(c == 0), stop=(c == C8 - 1)),
                    waits=[o_uT] + slot_u.get(slg, [])))
            o_si = em.op("scalar", lambda s, slg=slg, cg=cg: s.activation(
                silu_t[slg][:], psBig[:, cg:cg + 512], AF.Silu,
                scale=sc[:, 1:2]),
                waits=[mm_g[-1]] + sx_ops + slot_u.get(slg, []))
            ai, aw = a_ring.next()
            o_a = em.op("vector", lambda v, slg=slg, cu=cu, ai=ai:
                        v.scalar_tensor_tensor(
                atile[ai][:], psBig[:, cu:cu + 512], sc[:, 1:2],
                silu_t[slg][:], ALU.mult, ALU.mult),
                waits=[mm_u[-1], o_si] + aw + sx_ops)
            slot_g[slg] = [o_si]
            slot_u[slg] = [o_a]
            o_am = em.op("vector", lambda v, ai=ai: v.tensor_reduce(
                amax1[:, 3:4], atile[ai][:], axis=mybir.AxisListType.X,
                op=ALU.max, apply_absolute_value=True), waits=[o_a, barrier])
            o_am_a = em.op("vector", lambda v: v.tensor_tensor(
                amax1[:, 1:2], amax1[:, 1:2], amax1[:, 3:4], ALU.max),
                waits=[o_am, o_am_a])
            o_awr = em.op("scalar", lambda s, ic=ic, t8=t8, ai=ai: s.dma_start(
                out=ascr[ic, t8], in_=atile[ai][:]), waits=[o_a],
                dma=f"at{ai}")
            a_ring.used(ai, o_awr, o_am)
            a_wr[(ic, t8)] = o_awr
            gu_read[buf] += [mm_g[-1], mm_u[-1]]

    for hc in range(0, C8):
        emit_wd_chunk(hc)
    o_wdT = wd_state["last"]

    sd_ops = collective_scale(1, cc2i, cc2o, "d", 2, 3, [o_am_a])
    dump_sbuf("dbg_sc", sc[:], (128, 8), F32, waits=sx_ops + sd_ops)
    dump_sbuf("dbg_amax", amax1[:], (128, 4), F32, waits=sx_ops + sd_ops)
    dump_sbuf("dbg_scal", scal[:], (1, 16), F32, waits=sx_ops + sd_ops)

    stFa.close()
    stF2 = ExitStack()
    qh_sb = [sb(f"qh_sb{i}", (128, IC * 512), BF16, st=stF2) for i in range(2)]
    arin = [sb(f"arin{i}", (128, 512), st=stF2) for i in range(4)]
    qsc = [sb(f"qsc{i}", (128, 512), st=stF2) for i in range(2)]
    ar_ring = Ring(4)
    y_wr = []
    qh_gate = {0: [], 1: []}
    psd_gate = {0: [], 1: []}
    qsc_gate = {0: None, 1: None}
    gb_built = {}
    for t8 in range(NB * QT):
        b = t8 // QT
        if b not in gb_built:
            gb_built[b] = build_bcast(garow, 5, b, psW, psw, scale_col=3,
                                      extra=sd_ops + [barrier])
        qb = (t8 // 1) % 2
        qh_done = None
        for icc in range(IC):
            ai, aw = ar_ring.next()
            o_la = em.op("sync", lambda s, icc=icc, t8=t8, ai=ai: s.dma_start(
                out=arin[ai][:], in_=ascr[icc, t8]),
                waits=aw + [a_wr[(icc, t8)]], dma=f"ar{ai}")
            qi = icc % 2
            o_1 = em.op("scalar", lambda sx_, ai=ai, qi=qi: sx_.activation(
                qsc[qi][:], arin[ai][:], AF.Copy,
                scale=sc[:, 2:3], bias=MAGIC),
                waits=[o_la, qsc_gate[qi]] + sd_ops)
            o_2 = em.op("vector", lambda v, qi=qi: v.tensor_scalar(
                qsc[qi][:], qsc[qi][:], MAGIC, 127.0, ALU.subtract,
                ALU.min), waits=[o_1])
            qh_done = em.op("vector", lambda v, icc=icc, qb=qb, qi=qi:
                            v.tensor_scalar(
                qh_sb[qb][:, icc * 512:(icc + 1) * 512], qsc[qi][:],
                -128.0, None, ALU.max), waits=[o_2] + qh_gate[qb])
            qsc_gate[qi] = qh_done
            ar_ring.used(ai, o_1)
        qh_gate[qb] = []
        for tk in range(4):
            sl = tk % 2
            c0 = sl * 1024
            gu_war = []
            if t8 == 0 and tk == 0:
                gu_war = slot_g.get(0, []) + slot_g.get(1, [])
            elif t8 == 0 and tk == 1:
                gu_war = slot_u.get(0, []) + slot_u.get(1, [])
            mms = []
            for half in range(2):
                for icc in range(IC):
                    mms.append(em.op(
                        "tensor",
                        lambda t, icc=icc, tk=tk, half=half, c0=c0, qb=qb:
                        t.matmul(
                            psBig[:, c0 + half * 512:c0 + (half + 1) * 512],
                            qh_sb[qb][:, icc * 512 + tk * 128:
                                      icc * 512 + (tk + 1) * 128],
                            wdT[:, icc * D + half * 512:
                                icc * D + (half + 1) * 512],
                            start=(icc == 0), stop=(icc == IC - 1)),
                        waits=[o_wdT, qh_done] + psd_gate[sl] + gu_war))
            qh_gate[qb].append(mms[-1])
            tt = (t8 % QT) * 4 + tk
            xi, xw = xr_ring.next()
            o_ldx = em.op("sync", lambda s, b=b, tt=tt, xi=xi: s.dma_start(
                out=xr[xi][:], in_=x1scr[b, tt * 128:(tt + 1) * 128, :]),
                waits=xw + [o_x1_wr[b * NT + tt]], dma=f"xr{xi}")
            o_m = em.op("vector", lambda v, c0=c0: v.tensor_tensor(
                rsc1[:, 0:1024], psBig[:, c0:c0 + 1024], garow[:], ALU.mult),
                waits=[mms[-1]] + gb_built[b])
            psd_gate[sl] = [o_m]
            si, sw = stg_ring.next()
            o_y = em.op("vector", lambda v, xi=xi, si=si: v.tensor_tensor(
                stg[si][:], rsc1[:, 0:1024], xr[xi][:], ALU.add),
                waits=[o_m, o_ldx] + sw)
            xr_ring.used(xi, o_y)
            o_wry = em.op("scalar", lambda s, b=b, tt=tt, si=si: s.dma_start(
                out=y_out[b, tt * 128:(tt + 1) * 128, :], in_=stg[si][:]),
                waits=[o_y], dma=f"stg{si}")
            stg_ring.used(si, o_wry)
            y_wr.append(o_wry)

    em.op("sync", lambda s: s.nop(), waits=y_wr + dump_ops)

    with nc.Block() as block:
        semstack = em.finalize(nc, block)
    semstack.close()
    stF2.close()
    stF.close()
    alloc.close()
    return nc


NAME_MAP = {
    "w_dq": "L0_attn_W_DQ", "w_uq": "L0_attn_W_UQ", "w_qr": "L0_attn_W_QR",
    "w_dkv": "L0_attn_W_DKV", "w_uk": "L0_attn_W_UK", "w_uv": "L0_attn_W_UV",
    "w_kr": "L0_attn_W_KR", "w_o": "L0_attn_W_O", "wg": "L0_ffn_w_gate",
    "wu": "L0_ffn_w_up", "wd": "L0_ffn_w_down", "wmod": "L0_wmod",
}


def kernel(**inputs):
    consts = host_consts()
    nc = build_program(dump=DUMP)
    x = np.ascontiguousarray(np.asarray(inputs["L0_attn_x"], dtype=np.float32))
    c = np.ascontiguousarray(np.asarray(inputs["L0_c"], dtype=np.float32))
    base = {k: np.ascontiguousarray(np.asarray(inputs[v], dtype=np.float32))
            for k, v in NAME_MAP.items()}
    base["clipx"] = np.asarray(inputs["L0_ffn_clip_x"],
                               dtype=np.float32).reshape(1, 1)
    base["clipd"] = np.asarray(inputs["L0_ffn_clip_down"],
                               dtype=np.float32).reshape(1, 1)
    base.update(consts)
    in_maps = []
    for k in range(NCORE):
        m = dict(base)
        m["x"] = np.ascontiguousarray(x[2 * k:2 * k + 2])
        m["cvec"] = np.ascontiguousarray(c[2 * k:2 * k + 2])
        in_maps.append(m)
    res = run_bass_kernel_spmd(nc, in_maps, core_ids=list(range(NCORE)))
    out = np.concatenate([np.asarray(res.results[k]["y"])
                          for k in range(NCORE)], axis=0)
    if DUMP:
        kernel.last_results = res
    return out


# version 09
# speedup vs baseline: 1.0858x; 1.0044x over previous
import math
from contextlib import ExitStack

import numpy as np
import concourse.bass as bass
import concourse.mybir as mybir
from concourse.bass_utils import run_bass_kernel_spmd

F32 = mybir.dt.float32
BF16 = mybir.dt.bfloat16
AF = mybir.ActivationFunctionType
ALU = mybir.AluOpType

MAGIC = 12582912.0
ISQ96 = float(np.float32(1.0) / np.sqrt(np.float32(96.0)))
EXP_HI = float(np.exp(80.0 / math.sqrt(96.0)))
EXP_LO = float(np.exp(-80.0 / math.sqrt(96.0)))

D, S, NB, NH, DH, DHR, DL, II = 1024, 2048, 2, 16, 64, 32, 256, 2816
C8, NT, QT, IC, LC = D // 128, S // 128, S // 512, II // 128, DL // 128
NCORE = 8

DUMP = ()


class Op:
    __slots__ = ("eng", "sem", "fn", "waits", "marked", "inc_n", "cum")

    def __init__(self, eng, sem, fn, waits, inc_n):
        self.eng, self.sem, self.fn = eng, sem, fn
        self.waits = waits
        self.marked = False
        self.inc_n = inc_n
        self.cum = None


class Em:
    ENGS = ("tensor", "vector", "scalar", "gpsimd", "sync")

    def __init__(self):
        self.ops = {e: [] for e in self.ENGS}

    def op(self, eng, fn, waits=(), dma=None, inc=None):
        if dma:
            sem = "q_" + dma
            inc_n = 16
        else:
            sem = {"tensor": "pe", "vector": "dve", "scalar": "act",
                   "gpsimd": "pool", "sync": "sse"}[eng]
            inc_n = 1
        if inc is not None:
            sem, inc_n = inc
        o = Op(eng, sem, fn, [w for w in waits if w is not None], inc_n)
        if dma:
            o.marked = True
        self.ops[eng].append(o)
        for w in o.waits:
            w.marked = True
        return o

    def finalize(self, nc, block):
        cnt = {}
        for e in self.ENGS:
            for o in self.ops[e]:
                if o.marked:
                    cnt[o.sem] = cnt.get(o.sem, 0) + o.inc_n
                    o.cum = (o.sem, cnt[o.sem])
        stack = ExitStack()
        sems = {name: stack.enter_context(nc.semaphore("sem_" + name))
                for name in cnt}
        for e in self.ENGS:
            ops = self.ops[e]
            if not ops:
                continue

            def body(eng, ops=ops, e=e):
                waited = {}
                for o in ops:
                    for w in o.waits:
                        if w.cum is None:
                            continue
                        if (w.eng == e and e == "tensor"
                                and o.eng == "tensor" and w.inc_n == 1
                                and o.inc_n == 1):
                            continue
                        sname, v = w.cum
                        if waited.get(sname, 0) >= v:
                            continue
                        waited[sname] = v
                        eng.wait_ge(sems[sname], v)
                    instr = o.fn(eng)
                    if o.marked:
                        instr.then_inc(sems[o.cum[0]], o.inc_n)
            getattr(block, e)(body)
        return stack


class Ring:
    def __init__(self, n):
        self.n = n
        self.slots = [[] for _ in range(n)]
        self.k = 0

    def next(self):
        i = self.k % self.n
        self.k += 1
        w = self.slots[i]
        self.slots[i] = []
        return i, w

    def used(self, i, *ops):
        self.slots[i].extend(o for o in ops if o is not None)


def host_consts():
    ident = np.eye(128, dtype=np.float32)
    tri = (np.arange(128)[None, :] >= np.arange(128)[:, None]).astype(np.float32)
    p = np.zeros((128, 128), dtype=np.float32)
    for i in range(128):
        p[i, (i // 32) * 32 + ((i % 32) ^ 1)] = 1.0
    d_idx = np.arange(32)
    inv = 10000.0 ** (-2.0 * (d_idx // 2).astype(np.float64) / 32.0)
    ang = np.arange(S, dtype=np.float64)[None, :] * inv[:, None]
    sgn = np.where(d_idx % 2 == 0, -1.0, 1.0)
    ropeC = np.tile(np.cos(ang).astype(np.float32), (4, 1))
    ropeS = np.tile((sgn[:, None] * np.sin(ang)).astype(np.float32), (4, 1))
    return dict(ident=ident, tri=tri, p128=p, ropec=ropeC, ropes=ropeS)


def build_program(dump=(), n_cores=NCORE):
    nc = bass.Bass("TRN2", target_bir_lowering=False)
    em = Em()

    def din(name, shape):
        return nc.dram_tensor(name, list(shape), F32, kind="ExternalInput").ap()

    x_in = din("x", (NB, S, D))
    w_dq, w_dkv = din("w_dq", (D, DL)), din("w_dkv", (D, DL))
    w_uq, w_qr = din("w_uq", (DL, D)), din("w_qr", (DL, NH * DHR))
    w_uk, w_uv = din("w_uk", (DL, D)), din("w_uv", (DL, D))
    w_kr, w_o = din("w_kr", (D, DHR)), din("w_o", (D, D))
    wg_d, wu_d, wd_d = din("wg", (II, D)), din("wu", (II, D)), din("wd", (D, II))
    wmod_d = din("wmod", (D, 6 * D))
    c_d = din("cvec", (NB, D))
    clipx_d, clipd_d = din("clipx", (1, 1)), din("clipd", (1, 1))
    ident_d, tri_d = din("ident", (128, 128)), din("tri", (128, 128))
    p128_d = din("p128", (128, 128))
    ropec_d, ropes_d = din("ropec", (128, S)), din("ropes", (128, S))

    y_out = nc.dram_tensor("y", [NB, S, D], F32, kind="ExternalOutput").ap()

    def dint(name, shape):
        kind = "ExternalOutput" if name in dump else "Internal"
        return nc.dram_tensor(name, list(shape), F32, kind=kind).ap()

    x1scr = dint("x1scr", (NB, S, D))
    h2scr = dint("h2scr", (NB, S, D))
    ascr = dint("ascr", (IC, NB * QT, 128, 512))
    cc1i, cc1o = dint("cc1i", (1, 1)), dint("cc1o", (1, 1))
    cc2i, cc2o = dint("cc2i", (1, 1)), dint("cc2o", (1, 1))

    dump_ops = []

    alloc = ExitStack()

    def sb(name, shape, dt=F32, st=None):
        return (st or alloc).enter_context(nc.sbuf_tensor(name, list(shape), dt))

    def psum(name, shape, dt=F32, st=None):
        return (st or alloc).enter_context(nc.psum_tensor(name, list(shape), dt))

    def dump_sbuf(name, ap, shape, dtype=F32, waits=()):
        if name not in dump:
            return
        d = nc.dram_tensor(name, list(shape), dtype, kind="ExternalOutput").ap()
        dump_ops.append(em.op("gpsimd", lambda g: g.dma_start(out=d, in_=ap),
                              waits=list(waits), dma="dbg"))

    ident_f = sb("ident_f", (128, 128))
    ident_b = sb("ident_b", (128, 128), BF16)
    ones_t = sb("ones_t", (128, 128))
    sfm = sb("sfm", (128, C8 * NB))
    sfm_b = sb("sfm_b", (128, C8 * NB), BF16)
    modfm = sb("modfm", (128, 96))
    sc = sb("sc", (128, 8))
    scal = sb("scal", (1, 16))
    xr = [sb(f"xr{i}", (128, D)) for i in range(2)]
    stg = [sb(f"stg{i}", (128, D)) for i in range(2)]
    sqs = sb("sqs", (128, D))
    stat = sb("stat", (128, 12))
    amax1 = sb("amax1", (128, 4))
    garow = sb("garow", (128, D))
    rsc1 = sb("rsc1", (128, 1024))
    rsc2 = sb("rsc2", (128, 1024))
    ctile = rsc1[0:NB, :]
    silu_c = sqs[0:NB, :]

    stA = ExitStack()
    tri_b = sb("tri_b", (128, 128), BF16, st=stA)
    p128_b = sb("p128_b", (128, 128), BF16, st=stA)
    ropeC = sb("ropeC", (128, S), BF16, st=stA)
    ropeS = sb("ropeS", (128, S), BF16, st=stA)
    WMODW = 768
    wmb = [sb(f"wmb{i}", (128, WMODW), st=stA) for i in range(2)]
    wdq_s = sb("wdq_s", (128, C8 * DL), BF16, st=stA)
    wdkv_s = sb("wdkv_s", (128, C8 * DL), BF16, st=stA)
    wkr_s = sb("wkr_s", (128, C8 * DHR), BF16, st=stA)
    wuq_s = sb("wuq_s", (128, LC * D), BF16, st=stA)
    wqr_s = sb("wqr_s", (128, LC * NH * DHR), BF16, st=stA)
    wuk_s = sb("wuk_s", (128, LC * D), BF16, st=stA)
    wuv_s = sb("wuv_s", (128, LC * D), BF16, st=stA)
    wo_s = sb("wo_s", (128, C8 * D), BF16, st=stA)
    wdq_p = sb("wdq_p", (128, C8 * DL), BF16, st=stA)
    wdkv_p = sb("wdkv_p", (128, C8 * DL), BF16, st=stA)
    wkr_p = sb("wkr_p", (128, C8 * DHR), BF16, st=stA)
    bias_cq = sb("bias_cq", (128, LC), st=stA)
    bias_ckv = sb("bias_ckv", (128, LC), st=stA)
    bias_kr = sb("bias_kr", (128, 1), st=stA)
    hbuf = sb("hbuf", (128, C8 * S), BF16, st=stA)
    nrmb = sb("nrmb", (128, D), BF16, st=stA)
    cq_s = sb("cq_s", (128, LC * S), BF16, st=stA)
    ckv_s = sb("ckv_s", (128, LC * S), BF16, st=stA)
    krt = sb("krt", (128, S), BF16, st=stA)
    kg = sb("kg", (128, 4 * S), BF16, st=stA)
    vg = sb("vg", (128, NT * 4 * 65), BF16, st=stA)
    qgt = sb("qgt", (128, 4 * S), BF16, st=stA)
    qrq = sb("qrq", (128, S), BF16, st=stA)
    ebuf = [sb(f"ebuf{i}", (128, 1024), BF16, st=stA) for i in range(5)]
    anorm = sb("anorm", (128, 512), st=stA)
    zrow = anorm[64:65, 0:512]

    psS = [psum("psS_A", (128, 1024), st=stA), psum("psS_B", (128, 1024), st=stA)]
    psACCs = [psum("psACC", (128, 512), st=stA),
              psum("psACC2", (128, 512), st=stA)]
    psU = psum("psU", (128, 512), st=stA)
    psT = psum("psT", (128, 1024), BF16, st=stA)

    pss = Ring(2)
    psu = Ring(1)
    xr_ring = Ring(2)
    stg_ring = Ring(2)
    eb_ring = Ring(5)

    ld_w = {}
    for key, w, dst, width in (("dq", w_dq, wdq_s, DL), ("dkv", w_dkv, wdkv_s, DL),
                               ("kr", w_kr, wkr_s, DHR), ("o", w_o, wo_s, D)):
        ld_w[key] = em.op("gpsimd", lambda g, w=w, dst=dst, width=width:
                          g.dma_start(
            out=dst[:].rearrange("p (c j) -> p c j", j=width),
            in_=w.rearrange("(c p) j -> p c j", p=128)), dma="gw")
    for key, w, dst, width in (("uq", w_uq, wuq_s, D),
                               ("qr", w_qr, wqr_s, NH * DHR),
                               ("uk", w_uk, wuk_s, D), ("uv", w_uv, wuv_s, D)):
        ld_w[key] = em.op("gpsimd", lambda g, w=w, dst=dst, width=width:
                          g.dma_start(
            out=dst[:].rearrange("p (l j) -> p l j", j=width),
            in_=w.rearrange("(l p) j -> p l j", p=128)), dma="gw")
    ld_cst = {}
    for key, t, d_ in (("identb", ident_b, ident_d), ("tri", tri_b, tri_d),
                       ("p128", p128_b, p128_d), ("ropec", ropeC, ropec_d),
                       ("ropes", ropeS, ropes_d)):
        ld_cst[key] = em.op("gpsimd", lambda g, t=t, d_=d_:
                            g.dma_start(out=t[:], in_=d_), dma="gw")
    ld_ones = em.op("gpsimd", lambda g: g.memset(ones_t[:], 1.0))
    for key, t, d_ in (("identf", ident_f, ident_d), ("c", ctile, c_d)):
        ld_cst[key] = em.op("sync", lambda s, t=t, d_=d_:
                            s.dma_start(out=t[:], in_=d_), dma="sc0")
    ld_cst["clipx"] = em.op("sync", lambda s: s.dma_start(
        out=scal[0:1, 0:1], in_=clipx_d), dma="sc0")
    ld_cst["clipd"] = em.op("sync", lambda s: s.dma_start(
        out=scal[0:1, 1:2], in_=clipd_d), dma="sc0")
    NHALF = (6 * D) // WMODW
    gw_last = ld_cst["ropes"]
    for k in list(ld_w):
        ld_w[k] = gw_last
    for k in ("identb", "tri", "p128", "ropec", "ropes"):
        ld_cst[k] = gw_last
    sc_last = ld_cst["clipd"]
    for k in ("identf", "c", "clipx", "clipd"):
        ld_cst[k] = sc_last
    ld_wmb = [em.op("sync", lambda s: s.dma_start(
        out=wmb[0][:], in_=wmod_d[0:128, 0:WMODW]), dma="wmb0")]

    o_silu = em.op("scalar", lambda s: s.activation(
        silu_c[0:NB, :], ctile[0:NB, :], AF.Silu), waits=[ld_cst["c"]])
    o_sfm = None
    for c in range(C8):
        sl, w = pss.next()
        o_t = em.op("tensor", lambda t, c=c, sl=sl: t.transpose(
            psS[sl][:, 0:NB], silu_c[0:NB, c * 128:(c + 1) * 128],
            ident_f[0:NB, 0:NB]), waits=[o_silu, ld_cst["identf"]] + w)
        o_sfm = em.op("vector", lambda v, c=c, sl=sl: v.tensor_copy(
            sfm[:, c * NB:(c + 1) * NB], psS[sl][:, 0:NB]), waits=[o_t])
        pss.used(sl, o_sfm)
    o_sfmb = None

    JH = WMODW // 128
    o_mm_last = None
    mod_mms = []
    sl_u, w_u = psu.next()
    for li in range(C8 * NHALF):
        dc, half = li // NHALF, li % NHALF
        if li + 1 < C8 * NHALF:
            dc2, half2 = (li + 1) // NHALF, (li + 1) % NHALF
            ld_wmb.append(em.op(
                "sync", lambda s, dc2=dc2, half2=half2, li=li: s.dma_start(
                    out=wmb[(li + 1) % 2][:],
                    in_=wmod_d[dc2 * 128:(dc2 + 1) * 128,
                               half2 * WMODW:(half2 + 1) * WMODW]),
                waits=[mod_mms[-1]] if li >= 1 else [],
                dma=f"wmb{(li + 1) % 2}"))
        for j in range(JH):
            jj = half * JH + j
            col = (jj // 8) * 16 + (jj % 8) * NB
            o_mm_last = em.op("tensor", lambda t, li=li, j=j, col=col: t.matmul(
                psU[:, col:col + NB], wmb[li % 2][:, j * 128:(j + 1) * 128],
                sfm[:, (li // NHALF) * NB:(li // NHALF + 1) * NB],
                start=(li == 0 and j == 0),
                stop=(li == C8 * NHALF - 1 and j == JH - 1)),
                waits=[o_sfm, ld_wmb[li]] + w_u)
            w_u = []
        mod_mms.append(o_mm_last)
    acc_last = em.op("vector", lambda v: v.tensor_copy(
        modfm[:, 0:96], psU[:, 0:96]), waits=[o_mm_last])
    psu.used(0, acc_last)

    em.op("vector", lambda v: v.tensor_scalar(
        modfm[:, 16:32], modfm[:, 16:32], 1.0, None, ALU.add), waits=[acc_last])
    em.op("vector", lambda v: v.tensor_scalar(
        modfm[:, 64:80], modfm[:, 64:80], 1.0, None, ALU.add), waits=[acc_last])
    o_mod = None
    for base in (32, 80):
        o1 = em.op("vector", lambda v, base=base: v.tensor_scalar(
            rsc1[:, 0:16], modfm[:, base:base + 16], 0.0, None, ALU.is_ge),
            waits=[acc_last] + ([o_mod] if o_mod else []))
        o2 = em.op("vector", lambda v: v.tensor_scalar(
            rsc1[:, 0:16], rsc1[:, 0:16], 2.0, -1.0, ALU.mult, ALU.add),
            waits=[o1])
        o3 = em.op("vector", lambda v, base=base: v.tensor_tensor(
            rsc1[:, 16:32], modfm[:, base:base + 16], rsc1[:, 0:16], ALU.mult),
            waits=[o2])
        o4 = em.op("vector", lambda v: v.tensor_scalar(
            rsc1[:, 16:32], rsc1[:, 16:32], 0.25, None, ALU.max), waits=[o3])
        o_mod = em.op("vector", lambda v, base=base: v.tensor_tensor(
            modfm[:, base:base + 16], rsc1[:, 16:32], rsc1[:, 0:16], ALU.mult),
            waits=[o4])
    o_sfmb = em.op("vector", lambda v: v.tensor_copy(sfm_b[:], modfm[:, 0:16]),
                   waits=[o_mod])
    dump_sbuf("dbg_modfm", modfm[:], (128, 96), waits=[o_mod])

    bcast_chain = [None]

    def build_bcast(target, v_i, b, pstile, ring, scale_col=None, extra=()):
        outs = []
        o2 = bcast_chain[0]
        for c in range(C8):
            col = v_i * 16 + c * 2 + b
            o1 = em.op("vector", lambda vv, col=col: vv.tensor_scalar(
                rsc2[:, 0:128], ones_t[:, 0:128], modfm[:, col:col + 1], None,
                ALU.mult), waits=[o_mod, ld_ones, o2] + list(extra))
            sl, w = ring.next()
            o2 = em.op("tensor", lambda t, sl=sl: t.transpose(
                pstile[sl][:, 0:128] if isinstance(pstile, list)
                else pstile[:, 0:128], rsc2[:, 0:128], ident_f[:]),
                waits=[o1, ld_cst["identf"]] + w)
            src = (pstile[sl] if isinstance(pstile, list) else pstile)
            if scale_col is None:
                o3 = em.op("vector", lambda vv, c=c, src=src: vv.tensor_copy(
                    target[:, c * 128:(c + 1) * 128], src[:, 0:128]),
                    waits=[o2])
            else:
                o3 = em.op("vector", lambda vv, c=c, src=src: vv.tensor_scalar(
                    target[:, c * 128:(c + 1) * 128], src[:, 0:128],
                    sc[:, scale_col:scale_col + 1], None, ALU.mult),
                    waits=[o2] + list(extra))
            ring.used(sl, o3)
            outs.append(o3)
        bcast_chain[0] = o2
        return outs

    def ln_stats(src_tile, t_par, waits):
        sb0 = (t_par % 2) * 6
        o_sq = em.op("scalar", lambda s: s.activation(
            sqs[:], src_tile[:], AF.Square, accum_out=stat[:, sb0 + 1:sb0 + 2]),
            waits=waits)
        o_sum = em.op("vector", lambda v: v.tensor_reduce(
            stat[:, sb0:sb0 + 1], src_tile[:], axis=mybir.AxisListType.X,
            op=ALU.add), waits=waits)
        o_mean = em.op("vector", lambda v: v.tensor_scalar(
            stat[:, sb0 + 2:sb0 + 3], stat[:, sb0:sb0 + 1], 1.0 / D, None,
            ALU.mult), waits=[o_sum])
        o_msq = em.op("vector", lambda v: v.tensor_scalar(
            stat[:, sb0 + 3:sb0 + 4], stat[:, sb0 + 1:sb0 + 2], 1.0 / D, None,
            ALU.mult), waits=[o_sq])
        o_mm2 = em.op("vector", lambda v: v.tensor_tensor(
            stat[:, sb0 + 4:sb0 + 5], stat[:, sb0 + 2:sb0 + 3],
            stat[:, sb0 + 2:sb0 + 3], ALU.mult), waits=[o_mean])
        o_var = em.op("vector", lambda v: v.scalar_tensor_tensor(
            stat[:, sb0 + 3:sb0 + 4], stat[:, sb0 + 3:sb0 + 4], 1e-5,
            stat[:, sb0 + 4:sb0 + 5], ALU.add, ALU.subtract),
            waits=[o_msq, o_mm2])
        o_std = em.op("scalar", lambda s: s.activation(
            stat[:, sb0 + 4:sb0 + 5], stat[:, sb0 + 3:sb0 + 4], AF.Sqrt),
            waits=[o_var])
        o_rstd = em.op("vector", lambda v: v.reciprocal(
            stat[:, sb0 + 3:sb0 + 4], stat[:, sb0 + 4:sb0 + 5]), waits=[o_std])
        o_nb1 = em.op("vector", lambda v: v.tensor_tensor(
            stat[:, sb0 + 5:sb0 + 6], stat[:, sb0 + 2:sb0 + 3],
            stat[:, sb0 + 3:sb0 + 4], ALU.mult), waits=[o_rstd])
        o_nb = em.op("vector", lambda v: v.tensor_scalar(
            stat[:, sb0 + 5:sb0 + 6], stat[:, sb0 + 5:sb0 + 6], -1.0, None,
            ALU.mult), waits=[o_nb1])
        return sb0, o_rstd, o_nb, o_sq, o_sum

    o_x1_wr = []
    kg_gate = []
    vg_ones_tok = None
    prev_psT_evac = None
    o_attn_last = None
    acc_readers = {0: [], 1: []}
    prev_bc = None
    qt_par = [0]

    for b in range(NB):
        o_h1_done = None
        for t in range(NT):
            xi, xw = xr_ring.next()
            o_ld = em.op("sync", lambda s, b=b, t=t, xi=xi: s.dma_start(
                out=xr[xi][:], in_=x_in[b, t * 128:(t + 1) * 128, :]),
                waits=xw, dma=f"xr{xi}")
            sb0, o_rstd, o_nb, o_sq, o_sum = ln_stats(
                xr[xi], t, [o_ld] + ([o_sfm] if (b == 0 and t == 0) else []))
            o_nrm = em.op("vector", lambda v, xi=xi, sb0=sb0: v.tensor_scalar(
                nrmb[:], xr[xi][:], stat[:, sb0 + 3:sb0 + 4],
                stat[:, sb0 + 5:sb0 + 6], ALU.mult, ALU.add),
                waits=[o_nb, prev_psT_evac] + kg_gate)
            tps = []
            for c in range(C8):
                tps.append(em.op("tensor", lambda t_, c=c: t_.transpose(
                    psT[:, c * 128:(c + 1) * 128],
                    nrmb[:, c * 128:(c + 1) * 128], ident_b[:]),
                    waits=[o_nrm, ld_cst["identb"], prev_psT_evac]))
            o_h1_done = em.op("vector", lambda v, t=t: v.tensor_copy(
                hbuf[:].rearrange("p (c s) -> p c s", s=S)[:, :, t * 128:(t + 1) * 128],
                psT[:].rearrange("p (c j) -> p c j", j=128)), waits=tps)
            prev_psT_evac = o_h1_done
            xr_ring.used(xi, o_sq, o_sum, o_nrm)
        if b == 0:
            dump_sbuf("dbg_h1fm", hbuf[:], (128, C8 * S), BF16,
                      waits=[o_h1_done])

        o_wp = None
        for (w_src, w_dst, width) in ((wdq_s, wdq_p, DL), (wdkv_s, wdkv_p, DL),
                                      (wkr_s, wkr_p, DHR)):
            for c in range(C8):
                col = 16 + c * 2 + b
                o_wp = em.op(
                    "vector",
                    lambda v, w_src=w_src, w_dst=w_dst, c=c, col=col, width=width:
                    v.tensor_scalar(
                        w_dst[:, c * width:(c + 1) * width],
                        w_src[:, c * width:(c + 1) * width],
                        modfm[:, col:col + 1], None, ALU.mult),
                    waits=[o_mod, ld_w["dq"], ld_w["dkv"], ld_w["kr"]] + kg_gate)
        for (w_src, dst, nl, width) in ((wdq_s, bias_cq, LC, DL),
                                        (wdkv_s, bias_ckv, LC, DL),
                                        (wkr_s, bias_kr, 1, DHR)):
            for l in range(nl):
                sl_u, w_u = psu.next()
                mpart = 128 if width == DL else DHR
                mms = []
                for c in range(C8):
                    sacol = c * 2 + b
                    mms.append(em.op(
                        "tensor",
                        lambda t, w_src=w_src, c=c, l=l, width=width,
                        mpart=mpart, sacol=sacol: t.matmul(
                            psU[0:mpart, 0:1],
                            w_src[:, c * width + l * 128:
                                  c * width + l * 128 + mpart],
                            sfm_b[:, sacol:sacol + 1],
                            start=(c == 0), stop=(c == C8 - 1)),
                        waits=[o_sfmb] + w_u))
                o_wp = em.op("vector",
                             lambda v, dst=dst, l=l, mpart=mpart: v.tensor_copy(
                                 dst[0:mpart, l:l + 1], psU[0:mpart, 0:1]),
                             waits=[mms[-1]])
                psu.used(0, o_wp)

        def proj_lat(w_p, bias_t, dst):
            last = None
            for l in range(LC):
                for q4 in range(4):
                    sl, w = pss.next()
                    mms = []
                    for c in range(C8):
                        mms.append(em.op(
                            "tensor",
                            lambda t, w_p=w_p, l=l, q4=q4, c=c, sl=sl: t.matmul(
                                psS[sl][:, 0:512],
                                w_p[:, c * DL + l * 128:c * DL + (l + 1) * 128],
                                hbuf[:, c * S + q4 * 512:c * S + (q4 + 1) * 512],
                                start=(c == 0), stop=(c == C8 - 1)),
                            waits=[o_h1_done, o_wp] + w))
                    last = em.op("vector", lambda v, dst=dst, l=l, q4=q4, sl=sl,
                                 bias_t=bias_t: v.tensor_scalar(
                        dst[:, l * S + q4 * 512:l * S + (q4 + 1) * 512],
                        psS[sl][:, 0:512], bias_t[:, l:l + 1], None, ALU.add),
                        waits=[mms[-1]])
                    pss.used(sl, last)
            return last

        o_cq = proj_lat(wdq_p, bias_cq, cq_s)
        o_ckv = proj_lat(wdkv_p, bias_ckv, ckv_s)
        if b == 0:
            dump_sbuf("dbg_cq", cq_s[:], (128, LC * S), BF16, waits=[o_cq])
            dump_sbuf("dbg_ckv", ckv_s[:], (128, LC * S), BF16, waits=[o_ckv])
            dump_sbuf("dbg_wdqp", wdq_p[:], (128, C8 * DL), BF16, waits=[o_wp])
            dump_sbuf("dbg_biascq", bias_cq[:], (128, LC), F32, waits=[o_wp])

        o_kr = None
        for q4 in range(4):
            sl, w = pss.next()
            mms = []
            for c in range(C8):
                mms.append(em.op("tensor", lambda t, c=c, q4=q4, sl=sl: t.matmul(
                    psS[sl][0:32, 0:512], wkr_p[:, c * DHR:(c + 1) * DHR],
                    hbuf[:, c * S + q4 * 512:c * S + (q4 + 1) * 512],
                    start=(c == 0), stop=(c == C8 - 1)),
                    waits=[o_h1_done, o_wp] + w))
            o_e = em.op("vector", lambda v, q4=q4, sl=sl: v.tensor_scalar(
                krt[64:96, q4 * 512:(q4 + 1) * 512], psS[sl][0:32, 0:512],
                bias_kr[0:32, 0:1], None, ALU.add),
                waits=[mms[-1]] + kg_gate)
            pss.used(sl, o_e)
            sl_u, w_u = psu.next()
            o_sw = em.op("tensor", lambda t, q4=q4: t.matmul(
                psU[64:96, 0:512], p128_b[64:96, 64:96],
                krt[64:96, q4 * 512:(q4 + 1) * 512], start=True, stop=True),
                waits=[o_e, ld_cst["p128"]] + w_u)
            o_r1 = em.op("vector", lambda v, q4=q4: v.tensor_tensor(
                rsc1[64:96, 0:512], krt[64:96, q4 * 512:(q4 + 1) * 512],
                ropeC[64:96, q4 * 512:(q4 + 1) * 512], ALU.mult),
                waits=[o_e, ld_cst["ropec"]])
            o_r2 = em.op("vector", lambda v, q4=q4: v.tensor_tensor(
                rsc2[64:96, 0:512], psU[64:96, 0:512],
                ropeS[64:96, q4 * 512:(q4 + 1) * 512], ALU.mult),
                waits=[o_sw, ld_cst["ropes"]])
            o_kr = em.op("vector", lambda v, q4=q4: v.tensor_tensor(
                krt[64:96, q4 * 512:(q4 + 1) * 512], rsc1[64:96, 0:512],
                rsc2[64:96, 0:512], ALU.add), waits=[o_r1, o_r2])
            psu.used(0, o_r2)

        ga_bc = build_bcast(garow, 2, b, psS, pss)

        attn_pend = []
        norm_box = [None]
        vg_ones_tok = em.op("vector", lambda v: v.memset(
            vg[:].rearrange("p (t h j) -> p t h j", h=4, j=65)[:, :, :, 64:65],
            1.0), waits=kg_gate)
        av_ops = []
        for g in range(4):
            kv_gate = kg_gate if g == 0 else list(av_ops)
            for pr in range(2):
                h0 = (g * 4 + 2 * pr)
                for q4 in range(4):
                    sl, w = pss.next()
                    mms = []
                    for l in range(LC):
                        mms.append(em.op(
                            "tensor", lambda t, h0=h0, l=l, q4=q4, sl=sl:
                            t.matmul(
                                psS[sl][:, 0:512],
                                wuk_s[:, l * D + h0 * DH:l * D + (h0 + 2) * DH],
                                ckv_s[:, l * S + q4 * 512:l * S + (q4 + 1) * 512],
                                start=(l == 0), stop=(l == LC - 1)),
                            waits=[o_ckv, ld_w["uk"]] + w))
                    hh_e, hh_o = 2 * pr, 2 * pr + 1
                    o_e1 = em.op("scalar", lambda s, hh_e=hh_e, q4=q4, sl=sl:
                                 s.activation(
                        kg[0:64, hh_e * S + q4 * 512:hh_e * S + (q4 + 1) * 512],
                        psS[sl][0:64, 0:512], AF.Copy),
                        waits=[mms[-1]] + kv_gate)
                    o_e2 = em.op("vector", lambda v, hh_o=hh_o, q4=q4, sl=sl:
                                 v.tensor_copy(
                        kg[0:64, hh_o * S + q4 * 512:hh_o * S + (q4 + 1) * 512],
                        psS[sl][64:128, 0:512]), waits=[mms[-1]] + kv_gate)
                    pss.used(sl, o_e1, o_e2)
            kr_cp = []
            for hh in range(4):
                kr_cp.append(em.op("vector", lambda v, hh=hh: v.tensor_copy(
                    kg[64:96, hh * S:(hh + 1) * S], krt[64:96, :]),
                    waits=[o_kr] + kv_gate))
            vg_evs = []
            for tt in range(NT):
                sl, w = pss.next()
                mms = []
                for l in range(LC):
                    mms.append(em.op("tensor",
                                     lambda t, g=g, tt=tt, l=l, sl=sl: t.matmul(
                        psS[sl][:, 0:256],
                        ckv_s[:, l * S + tt * 128:l * S + (tt + 1) * 128],
                        wuv_s[:, l * D + g * 256:l * D + (g + 1) * 256],
                        start=(l == 0), stop=(l == LC - 1)),
                        waits=[o_ckv, ld_w["uv"]] + w))
                o_ve = em.op("vector", lambda v, tt=tt, sl=sl: v.tensor_copy(
                    vg[:, tt * 260:(tt + 1) * 260].rearrange(
                        "p (h j) -> p h j", j=65)[:, :, 0:64],
                    psS[sl][:, 0:256].rearrange("p (h j) -> p h j", j=64)),
                    waits=[mms[-1]] + kv_gate)
                pss.used(sl, o_ve)
                vg_evs.append(o_ve)
            q_evs = []
            for pr in range(2):
                h0 = (g * 4 + 2 * pr)
                for q4 in range(4):
                    sl, w = pss.next()
                    mms = []
                    for l in range(LC):
                        mms.append(em.op(
                            "tensor", lambda t, h0=h0, l=l, q4=q4, sl=sl:
                            t.matmul(
                                psS[sl][:, 0:512],
                                wuq_s[:, l * D + h0 * DH:l * D + (h0 + 2) * DH],
                                cq_s[:, l * S + q4 * 512:l * S + (q4 + 1) * 512],
                                start=(l == 0), stop=(l == LC - 1)),
                            waits=[o_cq, ld_w["uq"]] + w))
                    hh_e, hh_o = 2 * pr, 2 * pr + 1
                    o_e1 = em.op("scalar", lambda s, hh_e=hh_e, q4=q4, sl=sl:
                                 s.activation(
                        qgt[0:64, hh_e * S + q4 * 512:hh_e * S + (q4 + 1) * 512],
                        psS[sl][0:64, 0:512], AF.Copy),
                        waits=[mms[-1]] + kv_gate)
                    o_e2 = em.op("vector", lambda v, hh_o=hh_o, q4=q4, sl=sl:
                                 v.tensor_copy(
                        qgt[0:64, hh_o * S + q4 * 512:hh_o * S + (q4 + 1) * 512],
                        psS[sl][64:128, 0:512]), waits=[mms[-1]] + kv_gate)
                    pss.used(sl, o_e1, o_e2)
                    q_evs += [o_e1, o_e2]
            qr_last = None
            for q4 in range(4):
                sl, w = pss.next()
                mms = []
                for l in range(LC):
                    mms.append(em.op("tensor",
                                     lambda t, g=g, l=l, q4=q4, sl=sl: t.matmul(
                        psS[sl][:, 0:512],
                        wqr_s[:, l * NH * DHR + g * 128:
                              l * NH * DHR + (g + 1) * 128],
                        cq_s[:, l * S + q4 * 512:l * S + (q4 + 1) * 512],
                        start=(l == 0), stop=(l == LC - 1)),
                        waits=[o_cq, ld_w["qr"]] + w))
                o_qe = em.op("vector", lambda v, q4=q4, sl=sl: v.tensor_copy(
                    qrq[:, q4 * 512:(q4 + 1) * 512], psS[sl][:, 0:512]),
                    waits=[mms[-1]] + kv_gate)
                pss.used(sl, o_qe)
                sl_u, w_u = psu.next()
                o_sw = em.op("tensor", lambda t, q4=q4: t.matmul(
                    psU[:, 0:512], p128_b[:], qrq[:, q4 * 512:(q4 + 1) * 512],
                    start=True, stop=True), waits=[o_qe] + w_u)
                o_r1 = em.op("vector", lambda v, q4=q4: v.tensor_tensor(
                    rsc1[:, 0:512], qrq[:, q4 * 512:(q4 + 1) * 512],
                    ropeC[:, q4 * 512:(q4 + 1) * 512], ALU.mult), waits=[o_qe])
                o_r2 = em.op("vector", lambda v, q4=q4: v.tensor_tensor(
                    rsc2[:, 0:512], psU[:, 0:512],
                    ropeS[:, q4 * 512:(q4 + 1) * 512], ALU.mult), waits=[o_sw])
                qr_last = em.op("vector", lambda v, q4=q4: v.tensor_tensor(
                    qrq[:, q4 * 512:(q4 + 1) * 512], rsc1[:, 0:512],
                    rsc2[:, 0:512], ALU.add), waits=[o_r1, o_r2])
                psu.used(0, o_r2)
            for hh in range(4):
                q_evs.append(em.op("vector", lambda v, hh=hh: v.tensor_copy(
                    qgt[64:96, hh * S:(hh + 1) * S],
                    qrq[hh * 32:(hh + 1) * 32, :]), waits=[qr_last] + kv_gate))
            o_build = em.op("scalar", lambda s: s.copy(
                scal[0:1, 15:16], scal[0:1, 15:16]),
                waits=[o_e1, o_e2, kr_cp[-1], vg_evs[-1]] + q_evs[-4:])
            if b == 0 and g == 0:
                dump_sbuf("dbg_kg", kg[:], (128, 4 * S), BF16, waits=[o_build])
                dump_sbuf("dbg_vg", vg[:], (128, NT * 4 * 65), BF16,
                          waits=[o_build])
                dump_sbuf("dbg_qg", qgt[:], (128, 4 * S), BF16, waits=[o_build])

            AVD = 4
            av_ops = []

            def pop_emit():
                nonlocal prev_bc, o_attn_last, av_ops
                it = attn_pend.pop(0)
                if it["first"] and norm_box[0] is not None \
                        and norm_box[0][3] == it["par"]:
                    flush_norm()
                ei, regs, gates = it["ei"], it["regs"], it["gates"]
                hh_, first, last = it["hh"], it["first"], it["last"]
                par = it["par"]
                psA = psACCs[par]
                outs = []
                for ri, (k, col, n, qs) in enumerate(regs):
                    ww = list(gates) + [vg_ones_tok]
                    if first and ri == 0:
                        ww += acc_readers[par]
                    outs.append(em.op(
                        "tensor",
                        lambda t, k=k, col=col, n=n, qs=qs, ei=ei, hh_=hh_,
                        psA=psA, st=(first and ri == 0),
                        sp=(last and ri == len(regs) - 1): t.matmul(
                            psA[0:65, qs:512],
                            vg[:, k * 260 + hh_ * 65:k * 260 + (hh_ + 1) * 65],
                            ebuf[ei][:, col:col + n],
                            start=st, stop=sp),
                        waits=ww))
                eb_ring.used(ei, *outs)
                av_ops += outs
                if last:
                    if norm_box[0] is not None:
                        flush_norm()
                    norm_box[0] = (it["h"], it["q0"], outs[-1], par)

            def flush_norm():
                nonlocal prev_bc, o_attn_last
                h_, q0_, o_last_av, par = norm_box[0]
                norm_box[0] = None
                psA = psACCs[par]
                o_rec = em.op("vector", lambda v, psA=psA: v.reciprocal(
                    zrow, psA[64:65, 0:512]), waits=[o_last_av, prev_bc])
                sl_u, w_u = psu.next()
                o_bc = em.op("tensor", lambda t: t.matmul(
                    psU[0:64, 0:512], ones_t[64:65, 0:64],
                    zrow, start=True, stop=True),
                    waits=[o_rec, ld_ones, o_attn_last] + w_u)
                o_an = em.op("scalar", lambda s, psA=psA: s.activation(
                    anorm[0:64, 0:512], psA[0:64, 0:512], AF.Copy),
                    waits=[o_last_av, o_attn_last])
                o_nm = em.op("vector", lambda v, h_=h_, q0_=q0_: v.tensor_tensor(
                    hbuf[(h_ % 2) * 64:(h_ % 2) * 64 + 64,
                         (h_ // 2) * S + q0_:(h_ // 2) * S + q0_ + 512],
                    anorm[0:64, 0:512], psU[0:64, 0:512], ALU.mult),
                    waits=[o_an, o_bc])
                psu.used(0, o_nm)
                acc_readers[par] = [o_rec, o_an]
                prev_bc = o_bc
                o_attn_last = o_nm

            for hh in range(4):
                h = g * 4 + hh
                for qt in range(QT):
                    q0 = qt * 512
                    groups = []
                    nfull = 4 * qt
                    for kp in range(nfull // 2):
                        groups.append([(2 * kp, 0, 512, 0),
                                       (2 * kp + 1, 512, 512, 0)])
                    kd = nfull
                    groups.append([(kd, 0, 512, 0), (kd + 1, 512, 384, 128),
                                   (kd + 3, 896, 128, 384)])
                    groups.append([(kd + 2, 0, 256, 256)])
                    n_g = len(groups)
                    for gi in range(n_g):
                        regs = groups[gi]
                        sl, w = pss.next()
                        first_waits = [o_build] + w
                        mms = []
                        for (k, col, n, qs) in regs:
                            mms.append(em.op(
                                "tensor",
                                lambda t, k=k, col=col, n=n, qs=qs, sl=sl,
                                hh=hh, q0=q0: t.matmul(
                                    psS[sl][:, col:col + n],
                                    kg[0:96, hh * S + k * 128:
                                       hh * S + (k + 1) * 128],
                                    qgt[0:96, hh * S + q0 + qs:
                                        hh * S + q0 + 512],
                                    start=True, stop=True),
                                waits=first_waits))
                            first_waits = []
                        wid0 = regs[0][1]
                        tot = (regs[-1][1] + regs[-1][2]) - wid0
                        ei, ew = eb_ring.next()
                        o_exp = em.op("scalar", lambda s, sl=sl, ei=ei,
                                      wid0=wid0, tot=tot: s.activation(
                            ebuf[ei][:, wid0:wid0 + tot],
                            psS[sl][:, wid0:wid0 + tot], AF.Exp, scale=ISQ96),
                            waits=mms + ew)
                        pss.used(sl, o_exp)
                        o_clip = em.op("vector", lambda v, ei=ei, wid0=wid0,
                                       tot=tot: v.tensor_scalar(
                            ebuf[ei][:, wid0:wid0 + tot],
                            ebuf[ei][:, wid0:wid0 + tot],
                            EXP_HI, EXP_LO, ALU.min, ALU.max), waits=[o_exp])
                        gates = [o_clip]
                        diag = regs[-1][3] > 0 or regs[0][0] >= nfull
                        if diag:
                            for (k, col, n, qs) in regs:
                                gates.append(em.op(
                                    "gpsimd",
                                    lambda g_, ei=ei, col=col: g_.tensor_tensor(
                                        ebuf[ei][:, col:col + 128],
                                        ebuf[ei][:, col:col + 128],
                                        tri_b[:], ALU.mult),
                                    waits=[o_clip, ld_cst["tri"]]))
                        attn_pend.append(dict(ei=ei, regs=regs, gates=gates,
                                              hh=hh, h=h, q0=q0,
                                              par=qt_par[0] % 2,
                                              first=(gi == 0),
                                              last=(gi == n_g - 1)))
                        if gi == n_g - 1:
                            qt_par[0] += 1
                        while len(attn_pend) > AVD:
                            pop_emit()
            while attn_pend:
                pop_emit()
            flush_norm()
        if b == 0:
            dump_sbuf("dbg_attnout", hbuf[:], (128, C8 * S), BF16,
                      waits=[o_attn_last])

        for tt in range(NT):
            xi, xw = xr_ring.next()
            o_ldx = em.op("sync", lambda s, b=b, tt=tt, xi=xi: s.dma_start(
                out=xr[xi][:], in_=x_in[b, tt * 128:(tt + 1) * 128, :]),
                waits=xw, dma=f"xr{xi}")
            sl, slw = pss.next()
            mms = []
            for half in range(2):
                for c in range(C8):
                    mms.append(em.op(
                        "tensor", lambda t, tt=tt, half=half, c=c, sl=sl:
                        t.matmul(
                            psS[sl][:, half * 512:(half + 1) * 512],
                            hbuf[:, c * S + tt * 128:c * S + (tt + 1) * 128],
                            wo_s[:, c * D + half * 512:c * D + (half + 1) * 512],
                            start=(c == 0), stop=(c == C8 - 1)),
                        waits=[o_attn_last, ld_w["o"]] + slw))
            o_m = em.op("vector", lambda v, sl=sl: v.tensor_tensor(
                rsc1[:, 0:1024], psS[sl][:, 0:1024], garow[:], ALU.mult),
                waits=[mms[-1]] + ga_bc)
            pss.used(sl, o_m)
            si, sw = stg_ring.next()
            o_x1 = em.op("vector", lambda v, xi=xi, si=si: v.tensor_tensor(
                stg[si][:], rsc1[:, 0:1024], xr[xi][:], ALU.add),
                waits=[o_m, o_ldx] + sw)
            xr_ring.used(xi, o_x1)
            o_wr = em.op("scalar", lambda s, b=b, tt=tt, si=si: s.dma_start(
                out=x1scr[b, tt * 128:(tt + 1) * 128, :], in_=stg[si][:]),
                waits=[o_x1], dma=f"stg{si}")
            stg_ring.used(si, o_wr)
            o_x1_wr.append(o_wr)
        kg_gate = [mms[-1], o_attn_last]

    barrier = em.op("vector", lambda v: v.memset(amax1[:, 0:4], 0.0),
                    waits=[o_x1_wr[-1], o_attn_last, av_ops[-1]])

    stA.close()
    stF = ExitStack()
    stFa = ExitStack()
    wdT = sb("wdT", (128, IC * D), BF16, st=stF)
    wquant = sb("wquant", (128, 12), st=stF)
    cbrow = sb("cbrow", (128, D), st=stF)
    sbrow = sb("sbrow", (128, D), st=stF)
    qxfm = sb("qxfm", (128, C8 * NB * S), BF16, st=stFa)
    wgr = [sb(f"wgr{i}", (128, D), st=stFa) for i in range(2)]
    wur = [sb(f"wur{i}", (128, D), st=stFa) for i in range(2)]
    wgq = sb("wgq", (128, D), BF16, st=stFa)
    wuq2 = sb("wuq2", (128, D), BF16, st=stFa)
    wgT = [sb(f"wgT{i}", (128, D), BF16, st=stFa) for i in range(2)]
    wuT = [sb(f"wuT{i}", (128, D), BF16, st=stFa) for i in range(2)]
    wdr = sb("wdr", (128, II), st=stFa)
    wdq2 = sb("wdq2", (128, II), BF16, st=stFa)
    silu_t = [sb(f"silu{i}", (128, 512), st=stFa) for i in range(2)]
    atile = [sb(f"atile{i}", (128, 512), st=stFa) for i in range(3)]

    psBig = psum("psBig", (128, 2048), st=stF)
    psT2 = psum("psT2", (128, 1024), BF16, st=stF)
    psW = psum("psW", (128, 512), st=stF)
    psw = Ring(1)
    a_ring = Ring(3)

    o_h2_wr = []
    o_am_h2 = None
    o_pool_prev = None
    for b in range(NB):
        cb_bc = build_bcast(cbrow, 4, b, psW, psw,
                            extra=[barrier, o_pool_prev])
        sb_bc = build_bcast(sbrow, 3, b, psW, psw,
                            extra=[barrier, o_pool_prev])
        for t in range(NT):
            xi, xw = xr_ring.next()
            o_ld = em.op("sync", lambda s, b=b, t=t, xi=xi: s.dma_start(
                out=xr[xi][:], in_=x1scr[b, t * 128:(t + 1) * 128, :]),
                waits=xw + [o_x1_wr[b * NT + t], barrier], dma=f"xr{xi}")
            sb0, o_rstd, o_nb, o_sq, o_sum = ln_stats(xr[xi], t, [o_ld])
            o_nrm = em.op("vector", lambda v, xi=xi, sb0=sb0: v.tensor_scalar(
                rsc1[:, 0:1024], xr[xi][:], stat[:, sb0 + 3:sb0 + 4],
                stat[:, sb0 + 5:sb0 + 6], ALU.mult, ALU.add),
                waits=[o_nb, o_pool_prev])
            o_h2a = em.op("gpsimd", lambda g: g.tensor_tensor(
                rsc2[:, 0:1024], rsc1[:, 0:1024], cbrow[:], ALU.mult),
                waits=[o_nrm] + cb_bc)
            si, sw = stg_ring.next()
            o_h2 = em.op("gpsimd", lambda g, si=si: g.tensor_tensor(
                stg[si][:], rsc2[:, 0:1024], sbrow[:], ALU.add),
                waits=[o_h2a] + sb_bc + sw)
            o_pool_prev = o_h2
            o_wr = em.op("scalar", lambda s, b=b, t=t, si=si: s.dma_start(
                out=h2scr[b, t * 128:(t + 1) * 128, :], in_=stg[si][:]),
                waits=[o_h2], dma=f"stg{si}")
            o_am = em.op("vector", lambda v, si=si: v.tensor_reduce(
                amax1[:, 2:3], stg[si][:], axis=mybir.AxisListType.X,
                op=ALU.max, apply_absolute_value=True),
                waits=[o_h2, barrier])
            o_am_h2 = em.op("vector", lambda v: v.tensor_tensor(
                amax1[:, 0:1], amax1[:, 0:1], amax1[:, 2:3], ALU.max),
                waits=[o_am, o_am_h2])
            stg_ring.used(si, o_wr, o_am)
            xr_ring.used(xi, o_sq, o_sum, o_nrm)
            o_h2_wr.append(o_wr)

    def collective_scale(amax_col, cci, cco, clip_key, inv_col, sc_col, waits):
        sl_w, ww = psw.next()
        o_tp = em.op("tensor", lambda t: t.transpose(
            psW[0:1, 0:128], amax1[:, amax_col:amax_col + 1], ident_f[:]),
            waits=list(waits) + [ld_cst["identf"]] + ww)
        o_red = em.op("vector", lambda v: v.tensor_reduce(
            scal[0:1, 4:5], psW[0:1, 0:128], axis=mybir.AxisListType.X,
            op=ALU.max), waits=[o_tp])
        psw.used(0, o_red)
        o_out = em.op("gpsimd", lambda g: g.dma_start(out=cci,
                                                      in_=scal[0:1, 4:5]),
                      waits=[o_red], dma="cio" + clip_key)
        o_cc = em.op("gpsimd", lambda g: g.collective_compute(
            "AllReduce", ALU.max, replica_groups=[list(range(n_cores))],
            ins=[cci], outs=[cco]), waits=[o_out],
            inc=("ccs" + clip_key, 1))
        o_in = em.op("gpsimd", lambda g: g.dma_start(out=scal[0:1, 5:6],
                                                     in_=cco),
                     waits=[o_cc], dma="cio" + clip_key)
        o_sg = em.op("scalar", lambda s: s.activation(
            scal[0:1, 6:7], scal[0:1, 0:1] if clip_key == "x"
            else scal[0:1, 1:2], AF.Sigmoid),
            waits=[ld_cst["clipx"], ld_cst["clipd"], o_in])
        o_m1 = em.op("vector", lambda v: v.tensor_tensor(
            scal[0:1, 7:8], scal[0:1, 5:6], scal[0:1, 6:7], ALU.mult),
            waits=[o_sg, o_in])
        o_m2 = em.op("vector", lambda v: v.tensor_scalar(
            scal[0:1, 8:9], scal[0:1, 7:8], 1.0 / 127.0, None, ALU.mult),
            waits=[o_m1])
        o_rc = em.op("vector", lambda v: v.reciprocal(
            scal[0:1, 9:10], scal[0:1, 8:9]), waits=[o_m2])
        outs = []
        for (src, dstc) in ((9, inv_col), (8, sc_col)):
            sl_w, ww = psw.next()
            o_b1 = em.op("tensor", lambda t, src=src: t.matmul(
                psW[:, 0:1], ones_t[0:1, 0:128], scal[0:1, src:src + 1],
                start=True, stop=True), waits=[o_rc, ld_ones] + ww)
            o_cp = em.op("vector", lambda v, dstc=dstc: v.tensor_copy(
                sc[:, dstc:dstc + 1], psW[:, 0:1]), waits=[o_b1])
            psw.used(0, o_cp)
            outs.append(o_cp)
        return outs

    sx_ops = collective_scale(0, cc1i, cc1o, "x", 0, 1, [o_am_h2])

    prev_psT2 = None
    o_qx_last = None
    o_q3 = None
    for b in range(NB):
        for t in range(NT):
            xi, xw = xr_ring.next()
            o_ld = em.op("sync", lambda s, b=b, t=t, xi=xi: s.dma_start(
                out=xr[xi][:], in_=h2scr[b, t * 128:(t + 1) * 128, :]),
                waits=xw + [o_h2_wr[b * NT + t]], dma=f"xr{xi}")
            o_q1 = em.op("scalar", lambda sx_, xi=xi: sx_.activation(
                rsc1[:, 0:1024], xr[xi][:], AF.Copy,
                scale=sc[:, 0:1], bias=MAGIC),
                waits=[o_ld, o_q3] + sx_ops)
            o_q2 = em.op("vector", lambda v: v.tensor_scalar(
                rsc1[:, 0:1024], rsc1[:, 0:1024], MAGIC, 127.0, ALU.subtract,
                ALU.min), waits=[o_q1])
            o_q3 = em.op("vector", lambda v: v.tensor_scalar(
                rsc2[:, 0:1024], rsc1[:, 0:1024], -128.0, None, ALU.max),
                waits=[o_q2])
            o_cb = em.op("vector", lambda v: v.tensor_copy(
                wgq[:, 0:1024], rsc2[:, 0:1024]), waits=[o_q3, barrier])
            tps = []
            for c in range(C8):
                tps.append(em.op("tensor", lambda t_, c=c: t_.transpose(
                    psT2[:, c * 128:(c + 1) * 128],
                    wgq[:, c * 128:(c + 1) * 128], ident_b[:]),
                    waits=[o_cb, prev_psT2]))
            gcol = b * S + t * 128
            o_qx_last = em.op("vector", lambda v, gcol=gcol: v.tensor_copy(
                qxfm[:].rearrange("p (c s) -> p c s",
                                  s=NB * S)[:, :, gcol:gcol + 128],
                psT2[:].rearrange("p (c j) -> p c j", j=128)), waits=tps)
            prev_psT2 = o_qx_last
            xr_ring.used(xi, o_q1)
    dump_sbuf("dbg_qxfm", qxfm[:], (128, C8 * NB * S), BF16,
              waits=[o_qx_last])

    def quant_weight(dst_bf, src_f32, width, q0_, waits):
        o_r = em.op("vector", lambda v: v.tensor_reduce(
            wquant[:, q0_:q0_ + 1], src_f32[:, 0:width],
            axis=mybir.AxisListType.X, op=ALU.max, apply_absolute_value=True),
            waits=waits)
        o_s = em.op("vector", lambda v: v.tensor_scalar(
            wquant[:, q0_ + 1:q0_ + 2], wquant[:, q0_:q0_ + 1],
            1.0 / 127.0, None, ALU.mult), waits=[o_r])
        o_i = em.op("vector", lambda v: v.reciprocal(
            wquant[:, q0_ + 2:q0_ + 3], wquant[:, q0_ + 1:q0_ + 2]),
            waits=[o_s])
        o_1 = em.op("vector", lambda v: v.tensor_scalar(
            src_f32[:, 0:width], src_f32[:, 0:width],
            wquant[:, q0_ + 2:q0_ + 3], MAGIC, ALU.mult, ALU.add),
            waits=[o_i])
        o_2 = em.op("vector", lambda v: v.tensor_scalar(
            src_f32[:, 0:width], src_f32[:, 0:width], MAGIC, 127.0,
            ALU.subtract, ALU.min), waits=[o_1])
        o_3 = em.op("vector", lambda v: v.tensor_scalar(
            dst_bf[:, 0:width], src_f32[:, 0:width], -128.0,
            wquant[:, q0_ + 1:q0_ + 2], ALU.max, ALU.mult), waits=[o_2])
        return o_3

    a_wr = {}
    o_am_a = None
    gu_read = {0: [barrier], 1: [barrier]}
    slot_g = {}
    slot_u = {}
    wd_state = {"gate": [barrier], "last": None}

    def emit_wd_chunk(hc):
        nonlocal prev_psT2
        o_ldw = em.op("sync", lambda s, hc=hc: s.dma_start(
            out=wdr[:], in_=wd_d[hc * 128:(hc + 1) * 128, :]),
            waits=wd_state["gate"], dma="wdr")
        o_q = quant_weight(wdq2, wdr, II, 6, [o_ldw])
        wd_state["gate"] = [o_q]
        for icc in range(IC):
            o_tp = em.op("tensor", lambda t, icc=icc: t.transpose(
                psT2[:, (icc % 8) * 128:(icc % 8) * 128 + 128],
                wdq2[:, icc * 128:(icc + 1) * 128], ident_b[:]),
                waits=[o_q, prev_psT2])
            o_e = em.op("vector", lambda v, hc=hc, icc=icc: v.tensor_copy(
                wdT[:, icc * D + hc * 128:icc * D + (hc + 1) * 128],
                psT2[:, (icc % 8) * 128:(icc % 8) * 128 + 128]),
                waits=[o_tp])
            prev_psT2 = o_e
            wd_state["gate"].append(o_e)
            wd_state["last"] = o_e

    for ic in range(IC):
        buf = ic % 2
        if False and ic % 3 == 2 and ic // 3 < C8:
            emit_wd_chunk(ic // 3)
        o_lg = em.op("sync", lambda s, ic=ic, buf=buf: s.dma_start(
            out=wgr[buf][:], in_=wg_d[ic * 128:(ic + 1) * 128, :]),
            waits=gu_read[buf], dma=f"wg{buf}")
        o_lu = em.op("sync", lambda s, ic=ic, buf=buf: s.dma_start(
            out=wur[buf][:], in_=wu_d[ic * 128:(ic + 1) * 128, :]),
            waits=gu_read[buf], dma=f"wu{buf}")
        gu_read[buf] = []
        o_qg_ = quant_weight(wgq, wgr[buf], D, 0, [o_lg, o_qx_last, barrier])
        tps_g = []
        for c in range(C8):
            tps_g.append(em.op("tensor", lambda t, c=c: t.transpose(
                psT2[:, c * 128:(c + 1) * 128], wgq[:, c * 128:(c + 1) * 128],
                ident_b[:]), waits=[o_qg_, prev_psT2]))
        o_gT = em.op("vector", lambda v, buf=buf: v.tensor_copy(
            wgT[buf][:], psT2[:]), waits=tps_g)
        prev_psT2 = o_gT
        o_qu_ = quant_weight(wuq2, wur[buf], D, 3, [o_lu])
        tps_u = []
        for c in range(C8):
            tps_u.append(em.op("tensor", lambda t, c=c: t.transpose(
                psT2[:, c * 128:(c + 1) * 128], wuq2[:, c * 128:(c + 1) * 128],
                ident_b[:]), waits=[o_qu_, prev_psT2]))
        o_uT = em.op("vector", lambda v, buf=buf: v.tensor_copy(
            wuT[buf][:], psT2[:]), waits=tps_u)
        prev_psT2 = o_uT
        for t8 in range(NB * QT):
            slg = t8 % 2
            cg = slg * 512
            cu = 1024 + slg * 512
            mm_g, mm_u = [], []
            for c in range(C8):
                mm_g.append(em.op("tensor",
                                  lambda t, c=c, t8=t8, cg=cg, buf=buf:
                                  t.matmul(
                    psBig[:, cg:cg + 512], wgT[buf][:, c * 128:(c + 1) * 128],
                    qxfm[:, c * NB * S + t8 * 512:c * NB * S + (t8 + 1) * 512],
                    start=(c == 0), stop=(c == C8 - 1)),
                    waits=[o_gT, o_qx_last] + slot_g.get(slg, [])))
            for c in range(C8):
                mm_u.append(em.op("tensor",
                                  lambda t, c=c, t8=t8, cu=cu, buf=buf:
                                  t.matmul(
                    psBig[:, cu:cu + 512], wuT[buf][:, c * 128:(c + 1) * 128],
                    qxfm[:, c * NB * S + t8 * 512:c * NB * S + (t8 + 1) * 512],
                    start=(c == 0), stop=(c == C8 - 1)),
                    waits=[o_uT] + slot_u.get(slg, [])))
            o_si = em.op("scalar", lambda s, slg=slg, cg=cg: s.activation(
                silu_t[slg][:], psBig[:, cg:cg + 512], AF.Silu,
                scale=sc[:, 1:2]),
                waits=[mm_g[-1]] + sx_ops + slot_u.get(slg, []))
            ai, aw = a_ring.next()
            o_a = em.op("vector", lambda v, slg=slg, cu=cu, ai=ai:
                        v.scalar_tensor_tensor(
                atile[ai][:], psBig[:, cu:cu + 512], sc[:, 1:2],
                silu_t[slg][:], ALU.mult, ALU.mult),
                waits=[mm_u[-1], o_si] + aw + sx_ops)
            slot_g[slg] = [o_si]
            slot_u[slg] = [o_a]
            o_am = em.op("vector", lambda v, ai=ai: v.tensor_reduce(
                amax1[:, 3:4], atile[ai][:], axis=mybir.AxisListType.X,
                op=ALU.max, apply_absolute_value=True), waits=[o_a, barrier])
            o_am_a = em.op("vector", lambda v: v.tensor_tensor(
                amax1[:, 1:2], amax1[:, 1:2], amax1[:, 3:4], ALU.max),
                waits=[o_am, o_am_a])
            o_awr = em.op("scalar", lambda s, ic=ic, t8=t8, ai=ai: s.dma_start(
                out=ascr[ic, t8], in_=atile[ai][:]), waits=[o_a],
                dma=f"at{ai}")
            a_ring.used(ai, o_awr, o_am)
            a_wr[(ic, t8)] = o_awr
            gu_read[buf] += [mm_g[-1], mm_u[-1]]

    for hc in range(0, C8):
        emit_wd_chunk(hc)
    o_wdT = wd_state["last"]

    sd_ops = collective_scale(1, cc2i, cc2o, "d", 2, 3, [o_am_a])
    dump_sbuf("dbg_sc", sc[:], (128, 8), F32, waits=sx_ops + sd_ops)
    dump_sbuf("dbg_amax", amax1[:], (128, 4), F32, waits=sx_ops + sd_ops)
    dump_sbuf("dbg_scal", scal[:], (1, 16), F32, waits=sx_ops + sd_ops)

    stFa.close()
    stF2 = ExitStack()
    qh_sb = [sb(f"qh_sb{i}", (128, IC * 512), BF16, st=stF2) for i in range(2)]
    arin = [sb(f"arin{i}", (128, 512), st=stF2) for i in range(4)]
    qsc = [sb(f"qsc{i}", (128, 512), st=stF2) for i in range(2)]
    ar_ring = Ring(4)
    y_wr = []
    qh_gate = {0: [], 1: []}
    psd_gate = {0: [], 1: []}
    qsc_gate = {0: None, 1: None}
    gb_built = {}
    for t8 in range(NB * QT):
        b = t8 // QT
        if b not in gb_built:
            gb_built[b] = build_bcast(garow, 5, b, psW, psw, scale_col=3,
                                      extra=sd_ops + [barrier])
        qb = (t8 // 1) % 2
        qh_done = None
        for icc in range(IC):
            ai, aw = ar_ring.next()
            o_la = em.op("sync", lambda s, icc=icc, t8=t8, ai=ai: s.dma_start(
                out=arin[ai][:], in_=ascr[icc, t8]),
                waits=aw + [a_wr[(icc, t8)]], dma=f"ar{ai}")
            qi = icc % 2
            o_1 = em.op("scalar", lambda sx_, ai=ai, qi=qi: sx_.activation(
                qsc[qi][:], arin[ai][:], AF.Copy,
                scale=sc[:, 2:3], bias=MAGIC),
                waits=[o_la, qsc_gate[qi]] + sd_ops)
            o_2 = em.op("vector", lambda v, qi=qi: v.tensor_scalar(
                qsc[qi][:], qsc[qi][:], MAGIC, 127.0, ALU.subtract,
                ALU.min), waits=[o_1])
            qh_done = em.op("vector", lambda v, icc=icc, qb=qb, qi=qi:
                            v.tensor_scalar(
                qh_sb[qb][:, icc * 512:(icc + 1) * 512], qsc[qi][:],
                -128.0, None, ALU.max), waits=[o_2] + qh_gate[qb])
            qsc_gate[qi] = qh_done
            ar_ring.used(ai, o_1)
        qh_gate[qb] = []
        for tk in range(4):
            sl = tk % 2
            c0 = sl * 1024
            gu_war = []
            if t8 == 0 and tk == 0:
                gu_war = slot_g.get(0, []) + slot_g.get(1, [])
            elif t8 == 0 and tk == 1:
                gu_war = slot_u.get(0, []) + slot_u.get(1, [])
            mms = []
            for half in range(2):
                for icc in range(IC):
                    mms.append(em.op(
                        "tensor",
                        lambda t, icc=icc, tk=tk, half=half, c0=c0, qb=qb:
                        t.matmul(
                            psBig[:, c0 + half * 512:c0 + (half + 1) * 512],
                            qh_sb[qb][:, icc * 512 + tk * 128:
                                      icc * 512 + (tk + 1) * 128],
                            wdT[:, icc * D + half * 512:
                                icc * D + (half + 1) * 512],
                            start=(icc == 0), stop=(icc == IC - 1)),
                        waits=[o_wdT, qh_done] + psd_gate[sl] + gu_war))
            qh_gate[qb].append(mms[-1])
            tt = (t8 % QT) * 4 + tk
            xi, xw = xr_ring.next()
            o_ldx = em.op("sync", lambda s, b=b, tt=tt, xi=xi: s.dma_start(
                out=xr[xi][:], in_=x1scr[b, tt * 128:(tt + 1) * 128, :]),
                waits=xw + [o_x1_wr[b * NT + tt]], dma=f"xr{xi}")
            o_m = em.op("vector", lambda v, c0=c0: v.tensor_tensor(
                rsc1[:, 0:1024], psBig[:, c0:c0 + 1024], garow[:], ALU.mult),
                waits=[mms[-1]] + gb_built[b])
            psd_gate[sl] = [o_m]
            si, sw = stg_ring.next()
            o_y = em.op("vector", lambda v, xi=xi, si=si: v.tensor_tensor(
                stg[si][:], rsc1[:, 0:1024], xr[xi][:], ALU.add),
                waits=[o_m, o_ldx] + sw)
            xr_ring.used(xi, o_y)
            o_wry = em.op("scalar", lambda s, b=b, tt=tt, si=si: s.dma_start(
                out=y_out[b, tt * 128:(tt + 1) * 128, :], in_=stg[si][:]),
                waits=[o_y], dma=f"stg{si}")
            stg_ring.used(si, o_wry)
            y_wr.append(o_wry)

    em.op("sync", lambda s: s.nop(), waits=y_wr + dump_ops)

    with nc.Block() as block:
        semstack = em.finalize(nc, block)
    semstack.close()
    stF2.close()
    stF.close()
    alloc.close()
    return nc


NAME_MAP = {
    "w_dq": "L0_attn_W_DQ", "w_uq": "L0_attn_W_UQ", "w_qr": "L0_attn_W_QR",
    "w_dkv": "L0_attn_W_DKV", "w_uk": "L0_attn_W_UK", "w_uv": "L0_attn_W_UV",
    "w_kr": "L0_attn_W_KR", "w_o": "L0_attn_W_O", "wg": "L0_ffn_w_gate",
    "wu": "L0_ffn_w_up", "wd": "L0_ffn_w_down", "wmod": "L0_wmod",
}


def kernel(**inputs):
    consts = host_consts()
    nc = build_program(dump=DUMP)
    x = np.ascontiguousarray(np.asarray(inputs["L0_attn_x"], dtype=np.float32))
    c = np.ascontiguousarray(np.asarray(inputs["L0_c"], dtype=np.float32))
    base = {k: np.ascontiguousarray(np.asarray(inputs[v], dtype=np.float32))
            for k, v in NAME_MAP.items()}
    base["clipx"] = np.asarray(inputs["L0_ffn_clip_x"],
                               dtype=np.float32).reshape(1, 1)
    base["clipd"] = np.asarray(inputs["L0_ffn_clip_down"],
                               dtype=np.float32).reshape(1, 1)
    base.update(consts)
    in_maps = []
    for k in range(NCORE):
        m = dict(base)
        m["x"] = np.ascontiguousarray(x[2 * k:2 * k + 2])
        m["cvec"] = np.ascontiguousarray(c[2 * k:2 * k + 2])
        in_maps.append(m)
    res = run_bass_kernel_spmd(nc, in_maps, core_ids=list(range(NCORE)))
    out = np.concatenate([np.asarray(res.results[k]["y"])
                          for k in range(NCORE)], axis=0)
    if DUMP:
        kernel.last_results = res
    return out


# version 10
# speedup vs baseline: 1.0880x; 1.0021x over previous
import math
from contextlib import ExitStack

import numpy as np
import concourse.bass as bass
import concourse.mybir as mybir
from concourse.bass_utils import run_bass_kernel_spmd

F32 = mybir.dt.float32
BF16 = mybir.dt.bfloat16
AF = mybir.ActivationFunctionType
ALU = mybir.AluOpType

MAGIC = 12582912.0
ISQ96 = float(np.float32(1.0) / np.sqrt(np.float32(96.0)))
EXP_HI = float(np.exp(80.0 / math.sqrt(96.0)))
EXP_LO = float(np.exp(-80.0 / math.sqrt(96.0)))

D, S, NB, NH, DH, DHR, DL, II = 1024, 2048, 2, 16, 64, 32, 256, 2816
C8, NT, QT, IC, LC = D // 128, S // 128, S // 512, II // 128, DL // 128
NCORE = 8

DUMP = ()


class Op:
    __slots__ = ("eng", "sem", "fn", "waits", "marked", "inc_n", "cum")

    def __init__(self, eng, sem, fn, waits, inc_n):
        self.eng, self.sem, self.fn = eng, sem, fn
        self.waits = waits
        self.marked = False
        self.inc_n = inc_n
        self.cum = None


class Em:
    ENGS = ("tensor", "vector", "scalar", "gpsimd", "sync")

    def __init__(self):
        self.ops = {e: [] for e in self.ENGS}

    def op(self, eng, fn, waits=(), dma=None, inc=None):
        if dma:
            sem = "q_" + dma
            inc_n = 16
        else:
            sem = {"tensor": "pe", "vector": "dve", "scalar": "act",
                   "gpsimd": "pool", "sync": "sse"}[eng]
            inc_n = 1
        if inc is not None:
            sem, inc_n = inc
        o = Op(eng, sem, fn, [w for w in waits if w is not None], inc_n)
        if dma:
            o.marked = True
        self.ops[eng].append(o)
        for w in o.waits:
            w.marked = True
        return o

    def finalize(self, nc, block):
        cnt = {}
        for e in self.ENGS:
            for o in self.ops[e]:
                if o.marked:
                    cnt[o.sem] = cnt.get(o.sem, 0) + o.inc_n
                    o.cum = (o.sem, cnt[o.sem])
        stack = ExitStack()
        sems = {name: stack.enter_context(nc.semaphore("sem_" + name))
                for name in cnt}
        for e in self.ENGS:
            ops = self.ops[e]
            if not ops:
                continue

            def body(eng, ops=ops, e=e):
                waited = {}
                for o in ops:
                    for w in o.waits:
                        if w.cum is None:
                            continue
                        if (w.eng == e and e == "tensor"
                                and o.eng == "tensor" and w.inc_n == 1
                                and o.inc_n == 1):
                            continue
                        sname, v = w.cum
                        if waited.get(sname, 0) >= v:
                            continue
                        waited[sname] = v
                        eng.wait_ge(sems[sname], v)
                    instr = o.fn(eng)
                    if o.marked:
                        instr.then_inc(sems[o.cum[0]], o.inc_n)
            getattr(block, e)(body)
        return stack


class Ring:
    def __init__(self, n):
        self.n = n
        self.slots = [[] for _ in range(n)]
        self.k = 0

    def next(self):
        i = self.k % self.n
        self.k += 1
        w = self.slots[i]
        self.slots[i] = []
        return i, w

    def used(self, i, *ops):
        self.slots[i].extend(o for o in ops if o is not None)


def host_consts():
    ident = np.eye(128, dtype=np.float32)
    tri = (np.arange(128)[None, :] >= np.arange(128)[:, None]).astype(np.float32)
    p = np.zeros((128, 128), dtype=np.float32)
    for i in range(128):
        p[i, (i // 32) * 32 + ((i % 32) ^ 1)] = 1.0
    d_idx = np.arange(32)
    inv = 10000.0 ** (-2.0 * (d_idx // 2).astype(np.float64) / 32.0)
    ang = np.arange(S, dtype=np.float64)[None, :] * inv[:, None]
    sgn = np.where(d_idx % 2 == 0, -1.0, 1.0)
    ropeC = np.tile(np.cos(ang).astype(np.float32), (4, 1))
    ropeS = np.tile((sgn[:, None] * np.sin(ang)).astype(np.float32), (4, 1))
    return dict(ident=ident, tri=tri, p128=p, ropec=ropeC, ropes=ropeS)


def build_program(dump=(), n_cores=NCORE):
    nc = bass.Bass("TRN2", target_bir_lowering=False)
    em = Em()

    def din(name, shape):
        return nc.dram_tensor(name, list(shape), F32, kind="ExternalInput").ap()

    x_in = din("x", (NB, S, D))
    w_dq, w_dkv = din("w_dq", (D, DL)), din("w_dkv", (D, DL))
    w_uq, w_qr = din("w_uq", (DL, D)), din("w_qr", (DL, NH * DHR))
    w_uk, w_uv = din("w_uk", (DL, D)), din("w_uv", (DL, D))
    w_kr, w_o = din("w_kr", (D, DHR)), din("w_o", (D, D))
    wg_d, wu_d, wd_d = din("wg", (II, D)), din("wu", (II, D)), din("wd", (D, II))
    wmod_d = din("wmod", (D, 6 * D))
    c_d = din("cvec", (NB, D))
    clipx_d, clipd_d = din("clipx", (1, 1)), din("clipd", (1, 1))
    ident_d, tri_d = din("ident", (128, 128)), din("tri", (128, 128))
    p128_d = din("p128", (128, 128))
    ropec_d, ropes_d = din("ropec", (128, S)), din("ropes", (128, S))

    y_out = nc.dram_tensor("y", [NB, S, D], F32, kind="ExternalOutput").ap()

    def dint(name, shape):
        kind = "ExternalOutput" if name in dump else "Internal"
        return nc.dram_tensor(name, list(shape), F32, kind=kind).ap()

    x1scr = dint("x1scr", (NB, S, D))
    h2scr = dint("h2scr", (NB, S, D))
    ascr = dint("ascr", (IC, NB * QT, 128, 512))
    cc1i, cc1o = dint("cc1i", (1, 1)), dint("cc1o", (1, 1))
    cc2i, cc2o = dint("cc2i", (1, 1)), dint("cc2o", (1, 1))

    dump_ops = []

    alloc = ExitStack()

    def sb(name, shape, dt=F32, st=None):
        return (st or alloc).enter_context(nc.sbuf_tensor(name, list(shape), dt))

    def psum(name, shape, dt=F32, st=None):
        return (st or alloc).enter_context(nc.psum_tensor(name, list(shape), dt))

    def dump_sbuf(name, ap, shape, dtype=F32, waits=()):
        if name not in dump:
            return
        d = nc.dram_tensor(name, list(shape), dtype, kind="ExternalOutput").ap()
        dump_ops.append(em.op("gpsimd", lambda g: g.dma_start(out=d, in_=ap),
                              waits=list(waits), dma="dbg"))

    ident_f = sb("ident_f", (128, 128))
    ident_b = sb("ident_b", (128, 128), BF16)
    ones_t = sb("ones_t", (128, 128))
    sfm = sb("sfm", (128, C8 * NB))
    sfm_b = sb("sfm_b", (128, C8 * NB), BF16)
    modfm = sb("modfm", (128, 96))
    sc = sb("sc", (128, 8))
    scal = sb("scal", (1, 16))
    xr = [sb(f"xr{i}", (128, D)) for i in range(2)]
    stg = [sb(f"stg{i}", (128, D)) for i in range(2)]
    sqs = sb("sqs", (128, D))
    stat = sb("stat", (128, 12))
    amax1 = sb("amax1", (128, 4))
    garow = sb("garow", (128, D))
    rsc1 = sb("rsc1", (128, 1024))
    rsc2 = sb("rsc2", (128, 1024))
    ctile = rsc1[0:NB, :]
    silu_c = sqs[0:NB, :]

    stA = ExitStack()
    tri_b = sb("tri_b", (128, 128), BF16, st=stA)
    p128_b = sb("p128_b", (128, 128), BF16, st=stA)
    ropeC = sb("ropeC", (128, S), BF16, st=stA)
    ropeS = sb("ropeS", (128, S), BF16, st=stA)
    WMODW = 1024
    wmb = [sb(f"wmb{i}", (128, WMODW), st=stA) for i in range(2)]
    wdq_s = sb("wdq_s", (128, C8 * DL), BF16, st=stA)
    wdkv_s = sb("wdkv_s", (128, C8 * DL), BF16, st=stA)
    wkr_s = sb("wkr_s", (128, C8 * DHR), BF16, st=stA)
    wuq_s = sb("wuq_s", (128, LC * D), BF16, st=stA)
    wqr_s = sb("wqr_s", (128, LC * NH * DHR), BF16, st=stA)
    wuk_s = sb("wuk_s", (128, LC * D), BF16, st=stA)
    wuv_s = sb("wuv_s", (128, LC * D), BF16, st=stA)
    wo_s = sb("wo_s", (128, C8 * D), BF16, st=stA)
    wdq_p = sb("wdq_p", (128, C8 * DL), BF16, st=stA)
    wdkv_p = sb("wdkv_p", (128, C8 * DL), BF16, st=stA)
    wkr_p = sb("wkr_p", (128, C8 * DHR), BF16, st=stA)
    bias_cq = sb("bias_cq", (128, LC), st=stA)
    bias_ckv = sb("bias_ckv", (128, LC), st=stA)
    bias_kr = sb("bias_kr", (128, 1), st=stA)
    hbuf = sb("hbuf", (128, C8 * S), BF16, st=stA)
    nrmb = sb("nrmb", (128, D), BF16, st=stA)
    cq_s = sb("cq_s", (128, LC * S), BF16, st=stA)
    ckv_s = sb("ckv_s", (128, LC * S), BF16, st=stA)
    krt = sb("krt", (128, S), BF16, st=stA)
    kg = sb("kg", (128, 4 * S), BF16, st=stA)
    vg = sb("vg", (128, NT * 4 * 65), BF16, st=stA)
    qgt = sb("qgt", (128, 4 * S), BF16, st=stA)
    qrq = sb("qrq", (128, S), BF16, st=stA)
    ebuf = [sb(f"ebuf{i}", (128, 1024), BF16, st=stA) for i in range(4)]
    anorm = sb("anorm", (128, 512), st=stA)
    zrow = anorm[64:65, 0:512]

    psS = [psum("psS_A", (128, 1024), st=stA), psum("psS_B", (128, 1024), st=stA)]
    psACCs = [psum("psACC", (128, 512), st=stA),
              psum("psACC2", (128, 512), st=stA)]
    psU = psum("psU", (128, 512), st=stA)
    psT = psum("psT", (128, 1024), BF16, st=stA)

    pss = Ring(2)
    psu = Ring(1)
    xr_ring = Ring(2)
    stg_ring = Ring(2)
    eb_ring = Ring(4)

    ld_w = {}
    for key, w, dst, width in (("dq", w_dq, wdq_s, DL), ("dkv", w_dkv, wdkv_s, DL),
                               ("kr", w_kr, wkr_s, DHR), ("o", w_o, wo_s, D)):
        ld_w[key] = em.op("gpsimd", lambda g, w=w, dst=dst, width=width:
                          g.dma_start(
            out=dst[:].rearrange("p (c j) -> p c j", j=width),
            in_=w.rearrange("(c p) j -> p c j", p=128)), dma="gw")
    for key, w, dst, width in (("uq", w_uq, wuq_s, D),
                               ("qr", w_qr, wqr_s, NH * DHR),
                               ("uk", w_uk, wuk_s, D), ("uv", w_uv, wuv_s, D)):
        ld_w[key] = em.op("gpsimd", lambda g, w=w, dst=dst, width=width:
                          g.dma_start(
            out=dst[:].rearrange("p (l j) -> p l j", j=width),
            in_=w.rearrange("(l p) j -> p l j", p=128)), dma="gw")
    ld_cst = {}
    for key, t, d_ in (("identb", ident_b, ident_d), ("tri", tri_b, tri_d),
                       ("p128", p128_b, p128_d), ("ropec", ropeC, ropec_d),
                       ("ropes", ropeS, ropes_d)):
        ld_cst[key] = em.op("gpsimd", lambda g, t=t, d_=d_:
                            g.dma_start(out=t[:], in_=d_), dma="gw")
    ld_ones = em.op("gpsimd", lambda g: g.memset(ones_t[:], 1.0))
    for key, t, d_ in (("identf", ident_f, ident_d), ("c", ctile, c_d)):
        ld_cst[key] = em.op("sync", lambda s, t=t, d_=d_:
                            s.dma_start(out=t[:], in_=d_), dma="sc0")
    ld_cst["clipx"] = em.op("sync", lambda s: s.dma_start(
        out=scal[0:1, 0:1], in_=clipx_d), dma="sc0")
    ld_cst["clipd"] = em.op("sync", lambda s: s.dma_start(
        out=scal[0:1, 1:2], in_=clipd_d), dma="sc0")
    NHALF = (6 * D) // WMODW
    gw_last = ld_cst["ropes"]
    for k in list(ld_w):
        ld_w[k] = gw_last
    for k in ("identb", "tri", "p128", "ropec", "ropes"):
        ld_cst[k] = gw_last
    sc_last = ld_cst["clipd"]
    for k in ("identf", "c", "clipx", "clipd"):
        ld_cst[k] = sc_last
    ld_wmb = [em.op("sync", lambda s: s.dma_start(
        out=wmb[0][:], in_=wmod_d[0:128, 0:WMODW]), dma="wmb0")]

    o_silu = em.op("scalar", lambda s: s.activation(
        silu_c[0:NB, :], ctile[0:NB, :], AF.Silu), waits=[ld_cst["c"]])
    o_sfm = None
    for c in range(C8):
        sl, w = pss.next()
        o_t = em.op("tensor", lambda t, c=c, sl=sl: t.transpose(
            psS[sl][:, 0:NB], silu_c[0:NB, c * 128:(c + 1) * 128],
            ident_f[0:NB, 0:NB]), waits=[o_silu, ld_cst["identf"]] + w)
        o_sfm = em.op("vector", lambda v, c=c, sl=sl: v.tensor_copy(
            sfm[:, c * NB:(c + 1) * NB], psS[sl][:, 0:NB]), waits=[o_t])
        pss.used(sl, o_sfm)
    o_sfmb = None

    JH = WMODW // 128
    o_mm_last = None
    mod_mms = []
    sl_u, w_u = psu.next()
    for li in range(C8 * NHALF):
        dc, half = li // NHALF, li % NHALF
        if li + 1 < C8 * NHALF:
            dc2, half2 = (li + 1) // NHALF, (li + 1) % NHALF
            ld_wmb.append(em.op(
                "sync", lambda s, dc2=dc2, half2=half2, li=li: s.dma_start(
                    out=wmb[(li + 1) % 2][:],
                    in_=wmod_d[dc2 * 128:(dc2 + 1) * 128,
                               half2 * WMODW:(half2 + 1) * WMODW]),
                waits=[mod_mms[-1]] if li >= 1 else [],
                dma=f"wmb{(li + 1) % 2}"))
        for j in range(JH):
            jj = half * JH + j
            col = (jj // 8) * 16 + (jj % 8) * NB
            o_mm_last = em.op("tensor", lambda t, li=li, j=j, col=col: t.matmul(
                psU[:, col:col + NB], wmb[li % 2][:, j * 128:(j + 1) * 128],
                sfm[:, (li // NHALF) * NB:(li // NHALF + 1) * NB],
                start=(li == 0 and j == 0),
                stop=(li == C8 * NHALF - 1 and j == JH - 1)),
                waits=[o_sfm, ld_wmb[li]] + w_u)
            w_u = []
        mod_mms.append(o_mm_last)
    acc_last = em.op("vector", lambda v: v.tensor_copy(
        modfm[:, 0:96], psU[:, 0:96]), waits=[o_mm_last])
    psu.used(0, acc_last)

    em.op("vector", lambda v: v.tensor_scalar(
        modfm[:, 16:32], modfm[:, 16:32], 1.0, None, ALU.add), waits=[acc_last])
    em.op("vector", lambda v: v.tensor_scalar(
        modfm[:, 64:80], modfm[:, 64:80], 1.0, None, ALU.add), waits=[acc_last])
    o_mod = None
    for base in (32, 80):
        o1 = em.op("vector", lambda v, base=base: v.tensor_scalar(
            rsc1[:, 0:16], modfm[:, base:base + 16], 0.0, None, ALU.is_ge),
            waits=[acc_last] + ([o_mod] if o_mod else []))
        o2 = em.op("vector", lambda v: v.tensor_scalar(
            rsc1[:, 0:16], rsc1[:, 0:16], 2.0, -1.0, ALU.mult, ALU.add),
            waits=[o1])
        o3 = em.op("vector", lambda v, base=base: v.tensor_tensor(
            rsc1[:, 16:32], modfm[:, base:base + 16], rsc1[:, 0:16], ALU.mult),
            waits=[o2])
        o4 = em.op("vector", lambda v: v.tensor_scalar(
            rsc1[:, 16:32], rsc1[:, 16:32], 0.25, None, ALU.max), waits=[o3])
        o_mod = em.op("vector", lambda v, base=base: v.tensor_tensor(
            modfm[:, base:base + 16], rsc1[:, 16:32], rsc1[:, 0:16], ALU.mult),
            waits=[o4])
    o_sfmb = em.op("vector", lambda v: v.tensor_copy(sfm_b[:], modfm[:, 0:16]),
                   waits=[o_mod])
    dump_sbuf("dbg_modfm", modfm[:], (128, 96), waits=[o_mod])

    bcast_chain = [None]

    def build_bcast(target, v_i, b, pstile, ring, scale_col=None, extra=()):
        outs = []
        o2 = bcast_chain[0]
        for c in range(C8):
            col = v_i * 16 + c * 2 + b
            o1 = em.op("vector", lambda vv, col=col: vv.tensor_scalar(
                rsc2[:, 0:128], ones_t[:, 0:128], modfm[:, col:col + 1], None,
                ALU.mult), waits=[o_mod, ld_ones, o2] + list(extra))
            sl, w = ring.next()
            o2 = em.op("tensor", lambda t, sl=sl: t.transpose(
                pstile[sl][:, 0:128] if isinstance(pstile, list)
                else pstile[:, 0:128], rsc2[:, 0:128], ident_f[:]),
                waits=[o1, ld_cst["identf"]] + w)
            src = (pstile[sl] if isinstance(pstile, list) else pstile)
            if scale_col is None:
                o3 = em.op("vector", lambda vv, c=c, src=src: vv.tensor_copy(
                    target[:, c * 128:(c + 1) * 128], src[:, 0:128]),
                    waits=[o2])
            else:
                o3 = em.op("vector", lambda vv, c=c, src=src: vv.tensor_scalar(
                    target[:, c * 128:(c + 1) * 128], src[:, 0:128],
                    sc[:, scale_col:scale_col + 1], None, ALU.mult),
                    waits=[o2] + list(extra))
            ring.used(sl, o3)
            outs.append(o3)
        bcast_chain[0] = o2
        return outs

    def ln_stats(src_tile, t_par, waits):
        sb0 = (t_par % 2) * 6
        o_sq = em.op("scalar", lambda s: s.activation(
            sqs[:], src_tile[:], AF.Square, accum_out=stat[:, sb0 + 1:sb0 + 2]),
            waits=waits)
        o_sum = em.op("vector", lambda v: v.tensor_reduce(
            stat[:, sb0:sb0 + 1], src_tile[:], axis=mybir.AxisListType.X,
            op=ALU.add), waits=waits)
        o_mean = em.op("vector", lambda v: v.tensor_scalar(
            stat[:, sb0 + 2:sb0 + 3], stat[:, sb0:sb0 + 1], 1.0 / D, None,
            ALU.mult), waits=[o_sum])
        o_msq = em.op("vector", lambda v: v.tensor_scalar(
            stat[:, sb0 + 3:sb0 + 4], stat[:, sb0 + 1:sb0 + 2], 1.0 / D, None,
            ALU.mult), waits=[o_sq])
        o_mm2 = em.op("vector", lambda v: v.tensor_tensor(
            stat[:, sb0 + 4:sb0 + 5], stat[:, sb0 + 2:sb0 + 3],
            stat[:, sb0 + 2:sb0 + 3], ALU.mult), waits=[o_mean])
        o_var = em.op("vector", lambda v: v.scalar_tensor_tensor(
            stat[:, sb0 + 3:sb0 + 4], stat[:, sb0 + 3:sb0 + 4], 1e-5,
            stat[:, sb0 + 4:sb0 + 5], ALU.add, ALU.subtract),
            waits=[o_msq, o_mm2])
        o_std = em.op("scalar", lambda s: s.activation(
            stat[:, sb0 + 4:sb0 + 5], stat[:, sb0 + 3:sb0 + 4], AF.Sqrt),
            waits=[o_var])
        o_rstd = em.op("vector", lambda v: v.reciprocal(
            stat[:, sb0 + 3:sb0 + 4], stat[:, sb0 + 4:sb0 + 5]), waits=[o_std])
        o_nb1 = em.op("vector", lambda v: v.tensor_tensor(
            stat[:, sb0 + 5:sb0 + 6], stat[:, sb0 + 2:sb0 + 3],
            stat[:, sb0 + 3:sb0 + 4], ALU.mult), waits=[o_rstd])
        o_nb = em.op("vector", lambda v: v.tensor_scalar(
            stat[:, sb0 + 5:sb0 + 6], stat[:, sb0 + 5:sb0 + 6], -1.0, None,
            ALU.mult), waits=[o_nb1])
        return sb0, o_rstd, o_nb, o_sq, o_sum

    o_x1_wr = []
    kg_gate = []
    vg_ones_tok = None
    prev_psT_evac = None
    o_attn_last = None
    acc_readers = {0: [], 1: []}
    prev_bc = None
    qt_par = [0]

    for b in range(NB):
        o_h1_done = None
        for t in range(NT):
            xi, xw = xr_ring.next()
            o_ld = em.op("sync", lambda s, b=b, t=t, xi=xi: s.dma_start(
                out=xr[xi][:], in_=x_in[b, t * 128:(t + 1) * 128, :]),
                waits=xw, dma=f"xr{xi}")
            sb0, o_rstd, o_nb, o_sq, o_sum = ln_stats(
                xr[xi], t, [o_ld] + ([o_sfm] if (b == 0 and t == 0) else []))
            o_nrm = em.op("vector", lambda v, xi=xi, sb0=sb0: v.tensor_scalar(
                nrmb[:], xr[xi][:], stat[:, sb0 + 3:sb0 + 4],
                stat[:, sb0 + 5:sb0 + 6], ALU.mult, ALU.add),
                waits=[o_nb, prev_psT_evac] + kg_gate)
            tps = []
            for c in range(C8):
                tps.append(em.op("tensor", lambda t_, c=c: t_.transpose(
                    psT[:, c * 128:(c + 1) * 128],
                    nrmb[:, c * 128:(c + 1) * 128], ident_b[:]),
                    waits=[o_nrm, ld_cst["identb"], prev_psT_evac]))
            o_h1_done = em.op("vector", lambda v, t=t: v.tensor_copy(
                hbuf[:].rearrange("p (c s) -> p c s", s=S)[:, :, t * 128:(t + 1) * 128],
                psT[:].rearrange("p (c j) -> p c j", j=128)), waits=tps)
            prev_psT_evac = o_h1_done
            xr_ring.used(xi, o_sq, o_sum, o_nrm)
        if b == 0:
            dump_sbuf("dbg_h1fm", hbuf[:], (128, C8 * S), BF16,
                      waits=[o_h1_done])

        o_wp = None
        for (w_src, w_dst, width) in ((wdq_s, wdq_p, DL), (wdkv_s, wdkv_p, DL),
                                      (wkr_s, wkr_p, DHR)):
            for c in range(C8):
                col = 16 + c * 2 + b
                o_wp = em.op(
                    "vector",
                    lambda v, w_src=w_src, w_dst=w_dst, c=c, col=col, width=width:
                    v.tensor_scalar(
                        w_dst[:, c * width:(c + 1) * width],
                        w_src[:, c * width:(c + 1) * width],
                        modfm[:, col:col + 1], None, ALU.mult),
                    waits=[o_mod, ld_w["dq"], ld_w["dkv"], ld_w["kr"]] + kg_gate)
        for (w_src, dst, nl, width) in ((wdq_s, bias_cq, LC, DL),
                                        (wdkv_s, bias_ckv, LC, DL),
                                        (wkr_s, bias_kr, 1, DHR)):
            for l in range(nl):
                sl_u, w_u = psu.next()
                mpart = 128 if width == DL else DHR
                mms = []
                for c in range(C8):
                    sacol = c * 2 + b
                    mms.append(em.op(
                        "tensor",
                        lambda t, w_src=w_src, c=c, l=l, width=width,
                        mpart=mpart, sacol=sacol: t.matmul(
                            psU[0:mpart, 0:1],
                            w_src[:, c * width + l * 128:
                                  c * width + l * 128 + mpart],
                            sfm_b[:, sacol:sacol + 1],
                            start=(c == 0), stop=(c == C8 - 1)),
                        waits=[o_sfmb] + w_u))
                o_wp = em.op("vector",
                             lambda v, dst=dst, l=l, mpart=mpart: v.tensor_copy(
                                 dst[0:mpart, l:l + 1], psU[0:mpart, 0:1]),
                             waits=[mms[-1]])
                psu.used(0, o_wp)

        def proj_lat(w_p, bias_t, dst):
            last = None
            for l in range(LC):
                for q4 in range(4):
                    sl, w = pss.next()
                    mms = []
                    for c in range(C8):
                        mms.append(em.op(
                            "tensor",
                            lambda t, w_p=w_p, l=l, q4=q4, c=c, sl=sl: t.matmul(
                                psS[sl][:, 0:512],
                                w_p[:, c * DL + l * 128:c * DL + (l + 1) * 128],
                                hbuf[:, c * S + q4 * 512:c * S + (q4 + 1) * 512],
                                start=(c == 0), stop=(c == C8 - 1)),
                            waits=[o_h1_done, o_wp] + w))
                    last = em.op("vector", lambda v, dst=dst, l=l, q4=q4, sl=sl,
                                 bias_t=bias_t: v.tensor_scalar(
                        dst[:, l * S + q4 * 512:l * S + (q4 + 1) * 512],
                        psS[sl][:, 0:512], bias_t[:, l:l + 1], None, ALU.add),
                        waits=[mms[-1]])
                    pss.used(sl, last)
            return last

        o_cq = proj_lat(wdq_p, bias_cq, cq_s)
        o_ckv = proj_lat(wdkv_p, bias_ckv, ckv_s)
        if b == 0:
            dump_sbuf("dbg_cq", cq_s[:], (128, LC * S), BF16, waits=[o_cq])
            dump_sbuf("dbg_ckv", ckv_s[:], (128, LC * S), BF16, waits=[o_ckv])
            dump_sbuf("dbg_wdqp", wdq_p[:], (128, C8 * DL), BF16, waits=[o_wp])
            dump_sbuf("dbg_biascq", bias_cq[:], (128, LC), F32, waits=[o_wp])

        o_kr = None
        for q4 in range(4):
            sl, w = pss.next()
            mms = []
            for c in range(C8):
                mms.append(em.op("tensor", lambda t, c=c, q4=q4, sl=sl: t.matmul(
                    psS[sl][0:32, 0:512], wkr_p[:, c * DHR:(c + 1) * DHR],
                    hbuf[:, c * S + q4 * 512:c * S + (q4 + 1) * 512],
                    start=(c == 0), stop=(c == C8 - 1)),
                    waits=[o_h1_done, o_wp] + w))
            o_e = em.op("vector", lambda v, q4=q4, sl=sl: v.tensor_scalar(
                krt[64:96, q4 * 512:(q4 + 1) * 512], psS[sl][0:32, 0:512],
                bias_kr[0:32, 0:1], None, ALU.add),
                waits=[mms[-1]] + kg_gate)
            pss.used(sl, o_e)
            sl_u, w_u = psu.next()
            o_sw = em.op("tensor", lambda t, q4=q4: t.matmul(
                psU[64:96, 0:512], p128_b[64:96, 64:96],
                krt[64:96, q4 * 512:(q4 + 1) * 512], start=True, stop=True),
                waits=[o_e, ld_cst["p128"]] + w_u)
            o_r1 = em.op("vector", lambda v, q4=q4: v.tensor_tensor(
                rsc1[64:96, 0:512], krt[64:96, q4 * 512:(q4 + 1) * 512],
                ropeC[64:96, q4 * 512:(q4 + 1) * 512], ALU.mult),
                waits=[o_e, ld_cst["ropec"]])
            o_r2 = em.op("vector", lambda v, q4=q4: v.tensor_tensor(
                rsc2[64:96, 0:512], psU[64:96, 0:512],
                ropeS[64:96, q4 * 512:(q4 + 1) * 512], ALU.mult),
                waits=[o_sw, ld_cst["ropes"]])
            o_kr = em.op("vector", lambda v, q4=q4: v.tensor_tensor(
                krt[64:96, q4 * 512:(q4 + 1) * 512], rsc1[64:96, 0:512],
                rsc2[64:96, 0:512], ALU.add), waits=[o_r1, o_r2])
            psu.used(0, o_r2)

        ga_bc = build_bcast(garow, 2, b, psS, pss)

        attn_pend = []
        norm_box = [None]
        vg_ones_tok = em.op("vector", lambda v: v.memset(
            vg[:].rearrange("p (t h j) -> p t h j", h=4, j=65)[:, :, :, 64:65],
            1.0), waits=kg_gate)
        av_ops = []
        for g in range(4):
            kv_gate = kg_gate if g == 0 else list(av_ops)
            for pr in range(2):
                h0 = (g * 4 + 2 * pr)
                for q4 in range(4):
                    sl, w = pss.next()
                    mms = []
                    for l in range(LC):
                        mms.append(em.op(
                            "tensor", lambda t, h0=h0, l=l, q4=q4, sl=sl:
                            t.matmul(
                                psS[sl][:, 0:512],
                                wuk_s[:, l * D + h0 * DH:l * D + (h0 + 2) * DH],
                                ckv_s[:, l * S + q4 * 512:l * S + (q4 + 1) * 512],
                                start=(l == 0), stop=(l == LC - 1)),
                            waits=[o_ckv, ld_w["uk"]] + w))
                    hh_e, hh_o = 2 * pr, 2 * pr + 1
                    o_e1 = em.op("scalar", lambda s, hh_e=hh_e, q4=q4, sl=sl:
                                 s.activation(
                        kg[0:64, hh_e * S + q4 * 512:hh_e * S + (q4 + 1) * 512],
                        psS[sl][0:64, 0:512], AF.Copy),
                        waits=[mms[-1]] + kv_gate)
                    o_e2 = em.op("vector", lambda v, hh_o=hh_o, q4=q4, sl=sl:
                                 v.tensor_copy(
                        kg[0:64, hh_o * S + q4 * 512:hh_o * S + (q4 + 1) * 512],
                        psS[sl][64:128, 0:512]), waits=[mms[-1]] + kv_gate)
                    pss.used(sl, o_e1, o_e2)
            kr_cp = []
            for hh in range(4):
                kr_cp.append(em.op("vector", lambda v, hh=hh: v.tensor_copy(
                    kg[64:96, hh * S:(hh + 1) * S], krt[64:96, :]),
                    waits=[o_kr] + kv_gate))
            vg_evs = []
            for tt in range(NT):
                sl, w = pss.next()
                mms = []
                for l in range(LC):
                    mms.append(em.op("tensor",
                                     lambda t, g=g, tt=tt, l=l, sl=sl: t.matmul(
                        psS[sl][:, 0:256],
                        ckv_s[:, l * S + tt * 128:l * S + (tt + 1) * 128],
                        wuv_s[:, l * D + g * 256:l * D + (g + 1) * 256],
                        start=(l == 0), stop=(l == LC - 1)),
                        waits=[o_ckv, ld_w["uv"]] + w))
                o_ve = em.op("vector", lambda v, tt=tt, sl=sl: v.tensor_copy(
                    vg[:, tt * 260:(tt + 1) * 260].rearrange(
                        "p (h j) -> p h j", j=65)[:, :, 0:64],
                    psS[sl][:, 0:256].rearrange("p (h j) -> p h j", j=64)),
                    waits=[mms[-1]] + kv_gate)
                pss.used(sl, o_ve)
                vg_evs.append(o_ve)
            q_evs = []
            for pr in range(2):
                h0 = (g * 4 + 2 * pr)
                for q4 in range(4):
                    sl, w = pss.next()
                    mms = []
                    for l in range(LC):
                        mms.append(em.op(
                            "tensor", lambda t, h0=h0, l=l, q4=q4, sl=sl:
                            t.matmul(
                                psS[sl][:, 0:512],
                                wuq_s[:, l * D + h0 * DH:l * D + (h0 + 2) * DH],
                                cq_s[:, l * S + q4 * 512:l * S + (q4 + 1) * 512],
                                start=(l == 0), stop=(l == LC - 1)),
                            waits=[o_cq, ld_w["uq"]] + w))
                    hh_e, hh_o = 2 * pr, 2 * pr + 1
                    o_e1 = em.op("scalar", lambda s, hh_e=hh_e, q4=q4, sl=sl:
                                 s.activation(
                        qgt[0:64, hh_e * S + q4 * 512:hh_e * S + (q4 + 1) * 512],
                        psS[sl][0:64, 0:512], AF.Copy),
                        waits=[mms[-1]] + kv_gate)
                    o_e2 = em.op("vector", lambda v, hh_o=hh_o, q4=q4, sl=sl:
                                 v.tensor_copy(
                        qgt[0:64, hh_o * S + q4 * 512:hh_o * S + (q4 + 1) * 512],
                        psS[sl][64:128, 0:512]), waits=[mms[-1]] + kv_gate)
                    pss.used(sl, o_e1, o_e2)
                    q_evs += [o_e1, o_e2]
            qr_last = None
            for q4 in range(4):
                sl, w = pss.next()
                mms = []
                for l in range(LC):
                    mms.append(em.op("tensor",
                                     lambda t, g=g, l=l, q4=q4, sl=sl: t.matmul(
                        psS[sl][:, 0:512],
                        wqr_s[:, l * NH * DHR + g * 128:
                              l * NH * DHR + (g + 1) * 128],
                        cq_s[:, l * S + q4 * 512:l * S + (q4 + 1) * 512],
                        start=(l == 0), stop=(l == LC - 1)),
                        waits=[o_cq, ld_w["qr"]] + w))
                o_qe = em.op("vector", lambda v, q4=q4, sl=sl: v.tensor_copy(
                    qrq[:, q4 * 512:(q4 + 1) * 512], psS[sl][:, 0:512]),
                    waits=[mms[-1]] + kv_gate)
                pss.used(sl, o_qe)
                sl_u, w_u = psu.next()
                o_sw = em.op("tensor", lambda t, q4=q4: t.matmul(
                    psU[:, 0:512], p128_b[:], qrq[:, q4 * 512:(q4 + 1) * 512],
                    start=True, stop=True), waits=[o_qe] + w_u)
                o_r1 = em.op("vector", lambda v, q4=q4: v.tensor_tensor(
                    rsc1[:, 0:512], qrq[:, q4 * 512:(q4 + 1) * 512],
                    ropeC[:, q4 * 512:(q4 + 1) * 512], ALU.mult), waits=[o_qe])
                o_r2 = em.op("vector", lambda v, q4=q4: v.tensor_tensor(
                    rsc2[:, 0:512], psU[:, 0:512],
                    ropeS[:, q4 * 512:(q4 + 1) * 512], ALU.mult), waits=[o_sw])
                qr_last = em.op("vector", lambda v, q4=q4: v.tensor_tensor(
                    qrq[:, q4 * 512:(q4 + 1) * 512], rsc1[:, 0:512],
                    rsc2[:, 0:512], ALU.add), waits=[o_r1, o_r2])
                psu.used(0, o_r2)
            for hh in range(4):
                q_evs.append(em.op("vector", lambda v, hh=hh: v.tensor_copy(
                    qgt[64:96, hh * S:(hh + 1) * S],
                    qrq[hh * 32:(hh + 1) * 32, :]), waits=[qr_last] + kv_gate))
            o_build = em.op("scalar", lambda s: s.copy(
                scal[0:1, 15:16], scal[0:1, 15:16]),
                waits=[o_e1, o_e2, kr_cp[-1], vg_evs[-1]] + q_evs[-4:])
            if b == 0 and g == 0:
                dump_sbuf("dbg_kg", kg[:], (128, 4 * S), BF16, waits=[o_build])
                dump_sbuf("dbg_vg", vg[:], (128, NT * 4 * 65), BF16,
                          waits=[o_build])
                dump_sbuf("dbg_qg", qgt[:], (128, 4 * S), BF16, waits=[o_build])

            AVD = 3
            av_ops = []

            def pop_emit():
                nonlocal prev_bc, o_attn_last, av_ops
                it = attn_pend.pop(0)
                if it["first"] and norm_box[0] is not None \
                        and norm_box[0][3] == it["par"]:
                    flush_norm()
                ei, regs, gates = it["ei"], it["regs"], it["gates"]
                hh_, first, last = it["hh"], it["first"], it["last"]
                par = it["par"]
                psA = psACCs[par]
                outs = []
                for ri, (k, col, n, qs) in enumerate(regs):
                    ww = list(gates) + [vg_ones_tok]
                    if first and ri == 0:
                        ww += acc_readers[par]
                    outs.append(em.op(
                        "tensor",
                        lambda t, k=k, col=col, n=n, qs=qs, ei=ei, hh_=hh_,
                        psA=psA, st=(first and ri == 0),
                        sp=(last and ri == len(regs) - 1): t.matmul(
                            psA[0:65, qs:512],
                            vg[:, k * 260 + hh_ * 65:k * 260 + (hh_ + 1) * 65],
                            ebuf[ei][:, col:col + n],
                            start=st, stop=sp),
                        waits=ww))
                eb_ring.used(ei, *outs)
                av_ops += outs
                if last:
                    if norm_box[0] is not None:
                        flush_norm()
                    norm_box[0] = (it["h"], it["q0"], outs[-1], par)

            def flush_norm():
                nonlocal prev_bc, o_attn_last
                h_, q0_, o_last_av, par = norm_box[0]
                norm_box[0] = None
                psA = psACCs[par]
                o_rec = em.op("vector", lambda v, psA=psA: v.reciprocal(
                    zrow, psA[64:65, 0:512]), waits=[o_last_av, prev_bc])
                sl_u, w_u = psu.next()
                o_bc = em.op("tensor", lambda t: t.matmul(
                    psU[0:64, 0:512], ones_t[64:65, 0:64],
                    zrow, start=True, stop=True),
                    waits=[o_rec, ld_ones, o_attn_last] + w_u)
                o_an = em.op("scalar", lambda s, psA=psA: s.activation(
                    anorm[0:64, 0:512], psA[0:64, 0:512], AF.Copy),
                    waits=[o_last_av, o_attn_last])
                o_nm = em.op("vector", lambda v, h_=h_, q0_=q0_: v.tensor_tensor(
                    hbuf[(h_ % 2) * 64:(h_ % 2) * 64 + 64,
                         (h_ // 2) * S + q0_:(h_ // 2) * S + q0_ + 512],
                    anorm[0:64, 0:512], psU[0:64, 0:512], ALU.mult),
                    waits=[o_an, o_bc])
                psu.used(0, o_nm)
                acc_readers[par] = [o_rec, o_an]
                prev_bc = o_bc
                o_attn_last = o_nm

            for hh in range(4):
                h = g * 4 + hh
                for qt in range(QT):
                    q0 = qt * 512
                    groups = []
                    nfull = 4 * qt
                    for kp in range(nfull // 2):
                        groups.append([(2 * kp, 0, 512, 0),
                                       (2 * kp + 1, 512, 512, 0)])
                    kd = nfull
                    groups.append([(kd, 0, 512, 0), (kd + 1, 512, 384, 128),
                                   (kd + 3, 896, 128, 384)])
                    groups.append([(kd + 2, 0, 256, 256)])
                    n_g = len(groups)
                    for gi in range(n_g):
                        regs = groups[gi]
                        sl, w = pss.next()
                        first_waits = [o_build] + w
                        mms = []
                        for (k, col, n, qs) in regs:
                            mms.append(em.op(
                                "tensor",
                                lambda t, k=k, col=col, n=n, qs=qs, sl=sl,
                                hh=hh, q0=q0: t.matmul(
                                    psS[sl][:, col:col + n],
                                    kg[0:96, hh * S + k * 128:
                                       hh * S + (k + 1) * 128],
                                    qgt[0:96, hh * S + q0 + qs:
                                        hh * S + q0 + 512],
                                    start=True, stop=True),
                                waits=first_waits))
                            first_waits = []
                        wid0 = regs[0][1]
                        tot = (regs[-1][1] + regs[-1][2]) - wid0
                        ei, ew = eb_ring.next()
                        o_exp = em.op("scalar", lambda s, sl=sl, ei=ei,
                                      wid0=wid0, tot=tot: s.activation(
                            ebuf[ei][:, wid0:wid0 + tot],
                            psS[sl][:, wid0:wid0 + tot], AF.Exp, scale=ISQ96),
                            waits=mms + ew)
                        pss.used(sl, o_exp)
                        o_clip = em.op("vector", lambda v, ei=ei, wid0=wid0,
                                       tot=tot: v.tensor_scalar(
                            ebuf[ei][:, wid0:wid0 + tot],
                            ebuf[ei][:, wid0:wid0 + tot],
                            EXP_HI, EXP_LO, ALU.min, ALU.max), waits=[o_exp])
                        gates = [o_clip]
                        diag = regs[-1][3] > 0 or regs[0][0] >= nfull
                        if diag:
                            for (k, col, n, qs) in regs:
                                gates.append(em.op(
                                    "gpsimd",
                                    lambda g_, ei=ei, col=col: g_.tensor_tensor(
                                        ebuf[ei][:, col:col + 128],
                                        ebuf[ei][:, col:col + 128],
                                        tri_b[:], ALU.mult),
                                    waits=[o_clip, ld_cst["tri"]]))
                        attn_pend.append(dict(ei=ei, regs=regs, gates=gates,
                                              hh=hh, h=h, q0=q0,
                                              par=qt_par[0] % 2,
                                              first=(gi == 0),
                                              last=(gi == n_g - 1)))
                        if gi == n_g - 1:
                            qt_par[0] += 1
                        while len(attn_pend) > AVD:
                            pop_emit()
            while attn_pend:
                pop_emit()
            flush_norm()
        if b == 0:
            dump_sbuf("dbg_attnout", hbuf[:], (128, C8 * S), BF16,
                      waits=[o_attn_last])

        for tt in range(NT):
            xi, xw = xr_ring.next()
            o_ldx = em.op("sync", lambda s, b=b, tt=tt, xi=xi: s.dma_start(
                out=xr[xi][:], in_=x_in[b, tt * 128:(tt + 1) * 128, :]),
                waits=xw, dma=f"xr{xi}")
            sl, slw = pss.next()
            mms = []
            for half in range(2):
                for c in range(C8):
                    mms.append(em.op(
                        "tensor", lambda t, tt=tt, half=half, c=c, sl=sl:
                        t.matmul(
                            psS[sl][:, half * 512:(half + 1) * 512],
                            hbuf[:, c * S + tt * 128:c * S + (tt + 1) * 128],
                            wo_s[:, c * D + half * 512:c * D + (half + 1) * 512],
                            start=(c == 0), stop=(c == C8 - 1)),
                        waits=[o_attn_last, ld_w["o"]] + slw))
            o_m = em.op("vector", lambda v, sl=sl: v.tensor_tensor(
                rsc1[:, 0:1024], psS[sl][:, 0:1024], garow[:], ALU.mult),
                waits=[mms[-1]] + ga_bc)
            pss.used(sl, o_m)
            si, sw = stg_ring.next()
            o_x1 = em.op("vector", lambda v, xi=xi, si=si: v.tensor_tensor(
                stg[si][:], rsc1[:, 0:1024], xr[xi][:], ALU.add),
                waits=[o_m, o_ldx] + sw)
            xr_ring.used(xi, o_x1)
            o_wr = em.op("scalar", lambda s, b=b, tt=tt, si=si: s.dma_start(
                out=x1scr[b, tt * 128:(tt + 1) * 128, :], in_=stg[si][:]),
                waits=[o_x1], dma=f"stg{si}")
            stg_ring.used(si, o_wr)
            o_x1_wr.append(o_wr)
        kg_gate = [mms[-1], o_attn_last]

    barrier = em.op("vector", lambda v: v.memset(amax1[:, 0:4], 0.0),
                    waits=[o_x1_wr[-1], o_attn_last, av_ops[-1]])

    stA.close()
    stF = ExitStack()
    stFa = ExitStack()
    wdT = sb("wdT", (128, IC * D), BF16, st=stF)
    wquant = sb("wquant", (128, 12), st=stF)
    cbrow = sb("cbrow", (128, D), st=stF)
    sbrow = sb("sbrow", (128, D), st=stF)
    qxfm = sb("qxfm", (128, C8 * NB * S), BF16, st=stFa)
    wgr = [sb(f"wgr{i}", (128, D), st=stFa) for i in range(2)]
    wur = [sb(f"wur{i}", (128, D), st=stFa) for i in range(2)]
    wgq = sb("wgq", (128, D), BF16, st=stFa)
    wuq2 = sb("wuq2", (128, D), BF16, st=stFa)
    wgT = [sb(f"wgT{i}", (128, D), BF16, st=stFa) for i in range(2)]
    wuT = [sb(f"wuT{i}", (128, D), BF16, st=stFa) for i in range(2)]
    wdr = sb("wdr", (128, II), st=stFa)
    wdq2 = sb("wdq2", (128, II), BF16, st=stFa)
    silu_t = [sb(f"silu{i}", (128, 512), st=stFa) for i in range(2)]
    atile = [sb(f"atile{i}", (128, 512), st=stFa) for i in range(3)]

    psBig = psum("psBig", (128, 2048), st=stF)
    psT2 = psum("psT2", (128, 1024), BF16, st=stF)
    psW = psum("psW", (128, 512), st=stF)
    psw = Ring(1)
    a_ring = Ring(3)

    o_h2_wr = []
    o_am_h2 = None
    pool_guard = [None, None]
    for b in range(NB):
        cb_bc = build_bcast(cbrow, 4, b, psW, psw,
                            extra=[barrier, pool_guard[0], pool_guard[1]])
        sb_bc = build_bcast(sbrow, 3, b, psW, psw,
                            extra=[barrier, pool_guard[0], pool_guard[1]])
        for t in range(NT):
            par = (b * NT + t) % 2
            buf = rsc1 if par == 0 else rsc2
            xi, xw = xr_ring.next()
            o_ld = em.op("sync", lambda s, b=b, t=t, xi=xi: s.dma_start(
                out=xr[xi][:], in_=x1scr[b, t * 128:(t + 1) * 128, :]),
                waits=xw + [o_x1_wr[b * NT + t], barrier], dma=f"xr{xi}")
            sb0, o_rstd, o_nb, o_sq, o_sum = ln_stats(xr[xi], t, [o_ld])
            extra_nrm = (cb_bc + sb_bc) if t < 2 else []
            o_nrm = em.op("vector", lambda v, xi=xi, sb0=sb0, buf=buf:
                          v.tensor_scalar(
                buf[:, 0:1024], xr[xi][:], stat[:, sb0 + 3:sb0 + 4],
                stat[:, sb0 + 5:sb0 + 6], ALU.mult, ALU.add),
                waits=[o_nb, pool_guard[par]] + extra_nrm)
            o_h2a = em.op("vector", lambda v, buf=buf: v.tensor_tensor(
                buf[:, 0:1024], buf[:, 0:1024], cbrow[:], ALU.mult),
                waits=[o_nrm] + cb_bc)
            si, sw = stg_ring.next()
            o_h2 = em.op("gpsimd", lambda g, si=si, buf=buf: g.tensor_tensor(
                stg[si][:], buf[:, 0:1024], sbrow[:], ALU.add),
                waits=[o_h2a] + sb_bc + sw)
            pool_guard[par] = o_h2
            o_wr = em.op("scalar", lambda s, b=b, t=t, si=si: s.dma_start(
                out=h2scr[b, t * 128:(t + 1) * 128, :], in_=stg[si][:]),
                waits=[o_h2], dma=f"stg{si}")
            o_am = em.op("vector", lambda v, si=si: v.tensor_reduce(
                amax1[:, 2:3], stg[si][:], axis=mybir.AxisListType.X,
                op=ALU.max, apply_absolute_value=True),
                waits=[o_h2, barrier])
            o_am_h2 = em.op("vector", lambda v: v.tensor_tensor(
                amax1[:, 0:1], amax1[:, 0:1], amax1[:, 2:3], ALU.max),
                waits=[o_am, o_am_h2])
            stg_ring.used(si, o_wr, o_am)
            xr_ring.used(xi, o_sq, o_sum, o_nrm)
            o_h2_wr.append(o_wr)

    def collective_scale(amax_col, cci, cco, clip_key, inv_col, sc_col, waits):
        sl_w, ww = psw.next()
        o_tp = em.op("tensor", lambda t: t.transpose(
            psW[0:1, 0:128], amax1[:, amax_col:amax_col + 1], ident_f[:]),
            waits=list(waits) + [ld_cst["identf"]] + ww)
        o_red = em.op("vector", lambda v: v.tensor_reduce(
            scal[0:1, 4:5], psW[0:1, 0:128], axis=mybir.AxisListType.X,
            op=ALU.max), waits=[o_tp])
        psw.used(0, o_red)
        o_out = em.op("gpsimd", lambda g: g.dma_start(out=cci,
                                                      in_=scal[0:1, 4:5]),
                      waits=[o_red], dma="cio" + clip_key)
        o_cc = em.op("gpsimd", lambda g: g.collective_compute(
            "AllReduce", ALU.max, replica_groups=[list(range(n_cores))],
            ins=[cci], outs=[cco]), waits=[o_out],
            inc=("ccs" + clip_key, 1))
        o_in = em.op("gpsimd", lambda g: g.dma_start(out=scal[0:1, 5:6],
                                                     in_=cco),
                     waits=[o_cc], dma="cio" + clip_key)
        o_sg = em.op("scalar", lambda s: s.activation(
            scal[0:1, 6:7], scal[0:1, 0:1] if clip_key == "x"
            else scal[0:1, 1:2], AF.Sigmoid),
            waits=[ld_cst["clipx"], ld_cst["clipd"], o_in])
        o_m1 = em.op("vector", lambda v: v.tensor_tensor(
            scal[0:1, 7:8], scal[0:1, 5:6], scal[0:1, 6:7], ALU.mult),
            waits=[o_sg, o_in])
        o_m2 = em.op("vector", lambda v: v.tensor_scalar(
            scal[0:1, 8:9], scal[0:1, 7:8], 1.0 / 127.0, None, ALU.mult),
            waits=[o_m1])
        o_rc = em.op("vector", lambda v: v.reciprocal(
            scal[0:1, 9:10], scal[0:1, 8:9]), waits=[o_m2])
        outs = []
        for (src, dstc) in ((9, inv_col), (8, sc_col)):
            sl_w, ww = psw.next()
            o_b1 = em.op("tensor", lambda t, src=src: t.matmul(
                psW[:, 0:1], ones_t[0:1, 0:128], scal[0:1, src:src + 1],
                start=True, stop=True), waits=[o_rc, ld_ones] + ww)
            o_cp = em.op("vector", lambda v, dstc=dstc: v.tensor_copy(
                sc[:, dstc:dstc + 1], psW[:, 0:1]), waits=[o_b1])
            psw.used(0, o_cp)
            outs.append(o_cp)
        return outs

    sx_ops = collective_scale(0, cc1i, cc1o, "x", 0, 1, [o_am_h2])

    prev_psT2 = None
    o_qx_last = None
    o_q3 = None
    for b in range(NB):
        for t in range(NT):
            xi, xw = xr_ring.next()
            o_ld = em.op("sync", lambda s, b=b, t=t, xi=xi: s.dma_start(
                out=xr[xi][:], in_=h2scr[b, t * 128:(t + 1) * 128, :]),
                waits=xw + [o_h2_wr[b * NT + t]], dma=f"xr{xi}")
            o_q1 = em.op("scalar", lambda sx_, xi=xi: sx_.activation(
                rsc1[:, 0:1024], xr[xi][:], AF.Copy,
                scale=sc[:, 0:1], bias=MAGIC),
                waits=[o_ld, o_q3] + sx_ops)
            o_q2 = em.op("vector", lambda v: v.tensor_scalar(
                rsc1[:, 0:1024], rsc1[:, 0:1024], MAGIC, 127.0, ALU.subtract,
                ALU.min), waits=[o_q1])
            o_q3 = em.op("vector", lambda v: v.tensor_scalar(
                rsc2[:, 0:1024], rsc1[:, 0:1024], -128.0, None, ALU.max),
                waits=[o_q2])
            o_cb = em.op("vector", lambda v: v.tensor_copy(
                wgq[:, 0:1024], rsc2[:, 0:1024]), waits=[o_q3, barrier])
            tps = []
            for c in range(C8):
                tps.append(em.op("tensor", lambda t_, c=c: t_.transpose(
                    psT2[:, c * 128:(c + 1) * 128],
                    wgq[:, c * 128:(c + 1) * 128], ident_b[:]),
                    waits=[o_cb, prev_psT2]))
            gcol = b * S + t * 128
            o_qx_last = em.op("vector", lambda v, gcol=gcol: v.tensor_copy(
                qxfm[:].rearrange("p (c s) -> p c s",
                                  s=NB * S)[:, :, gcol:gcol + 128],
                psT2[:].rearrange("p (c j) -> p c j", j=128)), waits=tps)
            prev_psT2 = o_qx_last
            xr_ring.used(xi, o_q1)
    dump_sbuf("dbg_qxfm", qxfm[:], (128, C8 * NB * S), BF16,
              waits=[o_qx_last])

    def quant_weight(dst_bf, src_f32, width, q0_, waits):
        o_r = em.op("vector", lambda v: v.tensor_reduce(
            wquant[:, q0_:q0_ + 1], src_f32[:, 0:width],
            axis=mybir.AxisListType.X, op=ALU.max, apply_absolute_value=True),
            waits=waits)
        o_s = em.op("vector", lambda v: v.tensor_scalar(
            wquant[:, q0_ + 1:q0_ + 2], wquant[:, q0_:q0_ + 1],
            1.0 / 127.0, None, ALU.mult), waits=[o_r])
        o_i = em.op("vector", lambda v: v.reciprocal(
            wquant[:, q0_ + 2:q0_ + 3], wquant[:, q0_ + 1:q0_ + 2]),
            waits=[o_s])
        o_1 = em.op("vector", lambda v: v.tensor_scalar(
            src_f32[:, 0:width], src_f32[:, 0:width],
            wquant[:, q0_ + 2:q0_ + 3], MAGIC, ALU.mult, ALU.add),
            waits=[o_i])
        o_2 = em.op("vector", lambda v: v.tensor_scalar(
            src_f32[:, 0:width], src_f32[:, 0:width], MAGIC, 127.0,
            ALU.subtract, ALU.min), waits=[o_1])
        o_3 = em.op("vector", lambda v: v.tensor_scalar(
            dst_bf[:, 0:width], src_f32[:, 0:width], -128.0,
            wquant[:, q0_ + 1:q0_ + 2], ALU.max, ALU.mult), waits=[o_2])
        return o_3

    a_wr = {}
    o_am_a = None
    gu_read = {0: [barrier], 1: [barrier]}
    slot_g = {}
    slot_u = {}
    wd_state = {"gate": [barrier], "last": None}

    def emit_wd_chunk(hc):
        nonlocal prev_psT2
        o_ldw = em.op("sync", lambda s, hc=hc: s.dma_start(
            out=wdr[:], in_=wd_d[hc * 128:(hc + 1) * 128, :]),
            waits=wd_state["gate"], dma="wdr")
        o_q = quant_weight(wdq2, wdr, II, 6, [o_ldw])
        wd_state["gate"] = [o_q]
        for icc in range(IC):
            o_tp = em.op("tensor", lambda t, icc=icc: t.transpose(
                psT2[:, (icc % 8) * 128:(icc % 8) * 128 + 128],
                wdq2[:, icc * 128:(icc + 1) * 128], ident_b[:]),
                waits=[o_q, prev_psT2])
            o_e = em.op("vector", lambda v, hc=hc, icc=icc: v.tensor_copy(
                wdT[:, icc * D + hc * 128:icc * D + (hc + 1) * 128],
                psT2[:, (icc % 8) * 128:(icc % 8) * 128 + 128]),
                waits=[o_tp])
            prev_psT2 = o_e
            wd_state["gate"].append(o_e)
            wd_state["last"] = o_e

    for ic in range(IC):
        buf = ic % 2
        if False and ic % 3 == 2 and ic // 3 < C8:
            emit_wd_chunk(ic // 3)
        o_lg = em.op("sync", lambda s, ic=ic, buf=buf: s.dma_start(
            out=wgr[buf][:], in_=wg_d[ic * 128:(ic + 1) * 128, :]),
            waits=gu_read[buf], dma=f"wg{buf}")
        o_lu = em.op("sync", lambda s, ic=ic, buf=buf: s.dma_start(
            out=wur[buf][:], in_=wu_d[ic * 128:(ic + 1) * 128, :]),
            waits=gu_read[buf], dma=f"wu{buf}")
        gu_read[buf] = []
        o_qg_ = quant_weight(wgq, wgr[buf], D, 0, [o_lg, o_qx_last, barrier])
        tps_g = []
        for c in range(C8):
            tps_g.append(em.op("tensor", lambda t, c=c: t.transpose(
                psT2[:, c * 128:(c + 1) * 128], wgq[:, c * 128:(c + 1) * 128],
                ident_b[:]), waits=[o_qg_, prev_psT2]))
        o_gT = em.op("vector", lambda v, buf=buf: v.tensor_copy(
            wgT[buf][:], psT2[:]), waits=tps_g)
        prev_psT2 = o_gT
        o_qu_ = quant_weight(wuq2, wur[buf], D, 3, [o_lu])
        tps_u = []
        for c in range(C8):
            tps_u.append(em.op("tensor", lambda t, c=c: t.transpose(
                psT2[:, c * 128:(c + 1) * 128], wuq2[:, c * 128:(c + 1) * 128],
                ident_b[:]), waits=[o_qu_, prev_psT2]))
        o_uT = em.op("vector", lambda v, buf=buf: v.tensor_copy(
            wuT[buf][:], psT2[:]), waits=tps_u)
        prev_psT2 = o_uT
        for t8 in range(NB * QT):
            slg = t8 % 2
            cg = slg * 512
            cu = 1024 + slg * 512
            mm_g, mm_u = [], []
            for c in range(C8):
                mm_g.append(em.op("tensor",
                                  lambda t, c=c, t8=t8, cg=cg, buf=buf:
                                  t.matmul(
                    psBig[:, cg:cg + 512], wgT[buf][:, c * 128:(c + 1) * 128],
                    qxfm[:, c * NB * S + t8 * 512:c * NB * S + (t8 + 1) * 512],
                    start=(c == 0), stop=(c == C8 - 1)),
                    waits=[o_gT, o_qx_last] + slot_g.get(slg, [])))
            for c in range(C8):
                mm_u.append(em.op("tensor",
                                  lambda t, c=c, t8=t8, cu=cu, buf=buf:
                                  t.matmul(
                    psBig[:, cu:cu + 512], wuT[buf][:, c * 128:(c + 1) * 128],
                    qxfm[:, c * NB * S + t8 * 512:c * NB * S + (t8 + 1) * 512],
                    start=(c == 0), stop=(c == C8 - 1)),
                    waits=[o_uT] + slot_u.get(slg, [])))
            o_si = em.op("scalar", lambda s, slg=slg, cg=cg: s.activation(
                silu_t[slg][:], psBig[:, cg:cg + 512], AF.Silu,
                scale=sc[:, 1:2]),
                waits=[mm_g[-1]] + sx_ops + slot_u.get(slg, []))
            ai, aw = a_ring.next()
            o_a = em.op("vector", lambda v, slg=slg, cu=cu, ai=ai:
                        v.scalar_tensor_tensor(
                atile[ai][:], psBig[:, cu:cu + 512], sc[:, 1:2],
                silu_t[slg][:], ALU.mult, ALU.mult),
                waits=[mm_u[-1], o_si] + aw + sx_ops)
            slot_g[slg] = [o_si]
            slot_u[slg] = [o_a]
            o_am = em.op("vector", lambda v, ai=ai: v.tensor_reduce(
                amax1[:, 3:4], atile[ai][:], axis=mybir.AxisListType.X,
                op=ALU.max, apply_absolute_value=True), waits=[o_a, barrier])
            o_am_a = em.op("vector", lambda v: v.tensor_tensor(
                amax1[:, 1:2], amax1[:, 1:2], amax1[:, 3:4], ALU.max),
                waits=[o_am, o_am_a])
            o_awr = em.op("scalar", lambda s, ic=ic, t8=t8, ai=ai: s.dma_start(
                out=ascr[ic, t8], in_=atile[ai][:]), waits=[o_a],
                dma=f"at{ai}")
            a_ring.used(ai, o_awr, o_am)
            a_wr[(ic, t8)] = o_awr
            gu_read[buf] += [mm_g[-1], mm_u[-1]]

    for hc in range(0, C8):
        emit_wd_chunk(hc)
    o_wdT = wd_state["last"]

    sd_ops = collective_scale(1, cc2i, cc2o, "d", 2, 3, [o_am_a])
    dump_sbuf("dbg_sc", sc[:], (128, 8), F32, waits=sx_ops + sd_ops)
    dump_sbuf("dbg_amax", amax1[:], (128, 4), F32, waits=sx_ops + sd_ops)
    dump_sbuf("dbg_scal", scal[:], (1, 16), F32, waits=sx_ops + sd_ops)

    stFa.close()
    stF2 = ExitStack()
    qh_sb = [sb(f"qh_sb{i}", (128, IC * 512), BF16, st=stF2) for i in range(2)]
    arin = [sb(f"arin{i}", (128, 512), st=stF2) for i in range(4)]
    qsc = [sb(f"qsc{i}", (128, 512), st=stF2) for i in range(2)]
    ar_ring = Ring(4)
    y_wr = []
    qh_gate = {0: [], 1: []}
    psd_gate = {0: [], 1: []}
    qsc_gate = {0: None, 1: None}
    gb_built = {}
    for t8 in range(NB * QT):
        b = t8 // QT
        if b not in gb_built:
            gb_built[b] = build_bcast(garow, 5, b, psW, psw, scale_col=3,
                                      extra=sd_ops + [barrier])
        qb = (t8 // 1) % 2
        qh_done = None
        for icc in range(IC):
            ai, aw = ar_ring.next()
            o_la = em.op("sync", lambda s, icc=icc, t8=t8, ai=ai: s.dma_start(
                out=arin[ai][:], in_=ascr[icc, t8]),
                waits=aw + [a_wr[(icc, t8)]], dma=f"ar{ai}")
            qi = icc % 2
            o_1 = em.op("scalar", lambda sx_, ai=ai, qi=qi: sx_.activation(
                qsc[qi][:], arin[ai][:], AF.Copy,
                scale=sc[:, 2:3], bias=MAGIC),
                waits=[o_la, qsc_gate[qi]] + sd_ops)
            o_2 = em.op("vector", lambda v, qi=qi: v.tensor_scalar(
                qsc[qi][:], qsc[qi][:], MAGIC, 127.0, ALU.subtract,
                ALU.min), waits=[o_1])
            qh_done = em.op("vector", lambda v, icc=icc, qb=qb, qi=qi:
                            v.tensor_scalar(
                qh_sb[qb][:, icc * 512:(icc + 1) * 512], qsc[qi][:],
                -128.0, None, ALU.max), waits=[o_2] + qh_gate[qb])
            qsc_gate[qi] = qh_done
            ar_ring.used(ai, o_1)
        qh_gate[qb] = []
        for tk in range(4):
            sl = tk % 2
            c0 = sl * 1024
            gu_war = []
            if t8 == 0 and tk == 0:
                gu_war = slot_g.get(0, []) + slot_g.get(1, [])
            elif t8 == 0 and tk == 1:
                gu_war = slot_u.get(0, []) + slot_u.get(1, [])
            mms = []
            for half in range(2):
                for icc in range(IC):
                    mms.append(em.op(
                        "tensor",
                        lambda t, icc=icc, tk=tk, half=half, c0=c0, qb=qb:
                        t.matmul(
                            psBig[:, c0 + half * 512:c0 + (half + 1) * 512],
                            qh_sb[qb][:, icc * 512 + tk * 128:
                                      icc * 512 + (tk + 1) * 128],
                            wdT[:, icc * D + half * 512:
                                icc * D + (half + 1) * 512],
                            start=(icc == 0), stop=(icc == IC - 1)),
                        waits=[o_wdT, qh_done] + psd_gate[sl] + gu_war))
            qh_gate[qb].append(mms[-1])
            tt = (t8 % QT) * 4 + tk
            xi, xw = xr_ring.next()
            o_ldx = em.op("sync", lambda s, b=b, tt=tt, xi=xi: s.dma_start(
                out=xr[xi][:], in_=x1scr[b, tt * 128:(tt + 1) * 128, :]),
                waits=xw + [o_x1_wr[b * NT + tt]], dma=f"xr{xi}")
            o_m = em.op("vector", lambda v, c0=c0: v.tensor_tensor(
                rsc1[:, 0:1024], psBig[:, c0:c0 + 1024], garow[:], ALU.mult),
                waits=[mms[-1]] + gb_built[b])
            psd_gate[sl] = [o_m]
            si, sw = stg_ring.next()
            o_y = em.op("vector", lambda v, xi=xi, si=si: v.tensor_tensor(
                stg[si][:], rsc1[:, 0:1024], xr[xi][:], ALU.add),
                waits=[o_m, o_ldx] + sw)
            xr_ring.used(xi, o_y)
            o_wry = em.op("scalar", lambda s, b=b, tt=tt, si=si: s.dma_start(
                out=y_out[b, tt * 128:(tt + 1) * 128, :], in_=stg[si][:]),
                waits=[o_y], dma=f"stg{si}")
            stg_ring.used(si, o_wry)
            y_wr.append(o_wry)

    em.op("sync", lambda s: s.nop(), waits=y_wr + dump_ops)

    with nc.Block() as block:
        semstack = em.finalize(nc, block)
    semstack.close()
    stF2.close()
    stF.close()
    alloc.close()
    return nc


NAME_MAP = {
    "w_dq": "L0_attn_W_DQ", "w_uq": "L0_attn_W_UQ", "w_qr": "L0_attn_W_QR",
    "w_dkv": "L0_attn_W_DKV", "w_uk": "L0_attn_W_UK", "w_uv": "L0_attn_W_UV",
    "w_kr": "L0_attn_W_KR", "w_o": "L0_attn_W_O", "wg": "L0_ffn_w_gate",
    "wu": "L0_ffn_w_up", "wd": "L0_ffn_w_down", "wmod": "L0_wmod",
}


def kernel(**inputs):
    consts = host_consts()
    nc = build_program(dump=DUMP)
    x = np.ascontiguousarray(np.asarray(inputs["L0_attn_x"], dtype=np.float32))
    c = np.ascontiguousarray(np.asarray(inputs["L0_c"], dtype=np.float32))
    base = {k: np.ascontiguousarray(np.asarray(inputs[v], dtype=np.float32))
            for k, v in NAME_MAP.items()}
    base["clipx"] = np.asarray(inputs["L0_ffn_clip_x"],
                               dtype=np.float32).reshape(1, 1)
    base["clipd"] = np.asarray(inputs["L0_ffn_clip_down"],
                               dtype=np.float32).reshape(1, 1)
    base.update(consts)
    in_maps = []
    for k in range(NCORE):
        m = dict(base)
        m["x"] = np.ascontiguousarray(x[2 * k:2 * k + 2])
        m["cvec"] = np.ascontiguousarray(c[2 * k:2 * k + 2])
        in_maps.append(m)
    res = run_bass_kernel_spmd(nc, in_maps, core_ids=list(range(NCORE)))
    out = np.concatenate([np.asarray(res.results[k]["y"])
                          for k in range(NCORE)], axis=0)
    if DUMP:
        kernel.last_results = res
    return out
